# version 11
# baseline (speedup 1.0000x reference)
import numpy as np
import ml_dtypes
import concourse.bass as bass
import concourse.mybir as mybir
from concourse.bass_utils import run_bass_kernel_spmd

F32 = mybir.dt.float32
BF16 = mybir.dt.bfloat16
AFT = mybir.ActivationFunctionType
ALU = mybir.AluOpType
AX = mybir.AxisListType

S = 2048
E = 1024
NHEAD = 16
HD = 64
NCHUNK = 8
NQT = 4
BF = ml_dtypes.bfloat16

GROUPS = [["l0a", "l0f", "l1a", "l1f"]]
DBG = False



def _prep(inp):
    d = {}
    x = np.asarray(inp["L0_attn_x"])
    d["xT"] = [np.ascontiguousarray(x[c].T) for c in range(8)]

    d["w0qT"] = np.ascontiguousarray(np.asarray(inp["L0_attn_Wq"]).T)
    d["w0kT"] = np.ascontiguousarray(np.asarray(inp["L0_attn_Wk"]).T)
    d["w0vT"] = np.ascontiguousarray(np.asarray(inp["L0_attn_Wv"]).T)
    d["g_rep"] = np.full((128, 1), np.float32(np.asarray(inp["L0_attn_g_scale"])), np.float32)

    d["w0f"] = np.ascontiguousarray(np.asarray(inp["L0_ffn_weight"]))
    d["bias0"] = np.asarray(inp["L0_ffn_bias"]).reshape(8, 128).T.copy()
    for br, tag in (("so", "so"), ("si", "si")):
        cw = np.asarray(inp[f"L0_ffn_{br}_conv_w"])
        bd = np.zeros((8, 128, 128), np.float32)
        for c in range(8):
            for gl in range(32):
                g = 32 * c + gl
                col = 32 * (c % 4) + gl
                bd[c, 4 * gl:4 * gl + 4, col] = cw[g]
        d[f"bd_{tag}"] = bd
        d[f"cb_{tag}"] = np.asarray(inp[f"L0_ffn_{br}_conv_b"]).reshape(2, 128).T.copy()
        d[f"gam_{tag}"] = np.asarray(inp[f"L0_ffn_{br}_gamma"]).reshape(2, 128).T.copy()
        d[f"bet_{tag}"] = np.asarray(inp[f"L0_ffn_{br}_beta"]).reshape(2, 128).T.copy()
        d[f"linT_{tag}"] = np.ascontiguousarray(np.asarray(inp[f"L0_ffn_{br}_lin_w"]).T)
        d[f"linb_{tag}"] = np.asarray(inp[f"L0_ffn_{br}_lin_b"]).reshape(1, 1024).copy()

    d["wqkvT"] = np.ascontiguousarray(np.asarray(inp["L1_attn_w_qkv"]).T)
    d["wprjT"] = np.ascontiguousarray(np.asarray(inp["L1_attn_w_proj"]).T)
    d["bprj"] = np.asarray(inp["L1_attn_b_proj"]).reshape(8, 128).T.copy()

    d["wrtT"] = np.ascontiguousarray(np.asarray(inp["L1_ffn_router_w"]).T)
    d["wg"] = np.ascontiguousarray(np.asarray(inp["L1_ffn_gate_w"]).reshape(8192, 2048))
    d["wu"] = np.ascontiguousarray(np.asarray(inp["L1_ffn_up_w"]).reshape(8192, 2048))
    d["wd"] = np.ascontiguousarray(np.asarray(inp["L1_ffn_down_w"]).reshape(16384, 1024))
    d["gb"] = np.asarray(inp["L1_ffn_gate_b"]).reshape(8, 16, 128).transpose(2, 0, 1).reshape(128, 128).copy()
    d["ub"] = np.asarray(inp["L1_ffn_up_b"]).reshape(8, 16, 128).transpose(2, 0, 1).reshape(128, 128).copy()
    d["db"] = np.ascontiguousarray(np.asarray(inp["L1_ffn_down_b"]))

    d["c_id"] = np.eye(128, dtype=np.float32)
    d["c_ones_row"] = np.ones((1, 128), np.float32)
    d["c_ones_col"] = np.ones((128, 1), np.float32)
    blk16 = np.zeros((8, 128, 16), np.float32)
    sel16 = np.zeros((8, 16, 128), np.float32)
    for m in range(8):
        blk16[m, 0:64, 2 * m] = 1.0
        blk16[m, 64:128, 2 * m + 1] = 1.0
        sel16[m, 2 * m, 0:64] = 1.0
        sel16[m, 2 * m + 1, 64:128] = 1.0
    d["c_blk16"] = blk16
    d["c_sel16"] = sel16
    sel8 = np.zeros((8, 8, 128), np.float32)
    for e in range(8):
        sel8[e, e, :] = 1.0
    d["c_sel8"] = sel8
    return d



class Sems:
    SHARED = ("ld", "cvals", "bar")

    def __init__(self, nc, pool_n=48):
        self.nc = nc
        self.m = {}
        self.pool = [nc.alloc_semaphore(f"pool{i}") for i in range(pool_n)]
        self.phase_maps = {}
        self.phase_map = {}
        self.pi_box = [0]

    def set_phase(self, k):
        if k not in self.phase_maps:
            self.phase_maps[k] = ({}, [0])
        self.phase_map, box = self.phase_maps[k]
        self.pi_box = box

    def next_phase(self):
        self.set_phase(len(self.phase_maps))

    def __getattr__(self, name):
        if name in ("nc", "m", "pool", "phase_map", "phase_maps", "pi", "pi_box"):
            raise AttributeError(name)
        if name in self.SHARED:
            if name not in self.m:
                self.m[name] = self.nc.alloc_semaphore(name)
            return self.m[name]
        if name not in self.phase_map:
            assert self.pi_box[0] < len(self.pool), "sem pool exhausted"
            self.phase_map[name] = self.pool[self.pi_box[0]]
            self.pi_box[0] += 1
        return self.phase_map[name]


class Arena:
    def __init__(self, t):
        self.t = t
        self.off = 0
        self.marks = []

    def take(self, cols):
        o = self.off
        self.off += cols
        assert self.off <= self.t.shape[1], f"arena overflow {self.off} > {self.t.shape[1]}"
        return o

    def push(self):
        self.marks.append(self.off)

    def pop(self):
        self.off = self.marks.pop()


class _MockInst:
    def __init__(self, tot):
        self.tot = tot

    def then_inc(self, sem, n=1, **kw):
        self.tot[sem.num] = self.tot.get(sem.num, 0) + n
        return self

    def annotate(self, *_):
        return self


class _MockEng:
    def __init__(self, tot):
        self.tot = tot

    def wait_ge(self, sem, val):
        pass

    def sem_inc(self, sem, n=1):
        self.tot[sem.num] = self.tot.get(sem.num, 0) + n
        return _MockInst(self.tot)

    def __getattr__(self, op):
        if op.startswith("_"):
            raise AttributeError(op)
        def f(*a, **kw):
            return _MockInst(self.tot)
        return f


class _EngProxy:
    def __init__(self, eng, base):
        object.__setattr__(self, "_e", eng)
        object.__setattr__(self, "_b", base)

    def wait_ge(self, sem, val):
        return self._e.wait_ge(sem, val + self._b.get(sem.num, 0))

    def __getattr__(self, n):
        return getattr(self._e, n)


def bank(ps, i, p0=0, p1=128, c0=0, c1=512):
    return ps[p0:p1, 512 * i + c0:512 * i + c1]




class LNBlock:

    def __init__(self, pr, pfx, sq4, mbf, rbf, tt2, final=False, out_scr=None):
        self.pr, self.pfx = pr, pfx
        self.sq4, self.mbf, self.rbf, self.tt2 = sq4, mbf, rbf, tt2
        self.final = final
        self.out_scr = out_scr

    def sm(self, name):
        return getattr(self.pr.sem, self.pfx + name)

    def et(self, t, gate=None):
        pr = self.pr
        ps, AB = pr.ps, pr.AB
        if gate is not None:
            t.wait_ge(*gate)
        for c in range(8):
            for qt in range(4):
                ins = t.matmul(bank(ps, qt, 0, 1), pr.c_oc_bf[:, :],
                               pr.xTb[0:128, 2048 * c + 512 * qt:2048 * c + 512 * qt + 512],
                               start=(c == 0), stop=(c == 7))
                if c == 7:
                    ins.then_inc(self.sm("stm"), 1)
                idx = c * 4 + qt
                t.wait_ge(self.sm("lsq"), idx + 1)
                t.matmul(bank(ps, 4 + qt, 0, 1), pr.c_oc_bf[:, :],
                         AB[0:128, self.sq4 + 512 * (idx % 4):self.sq4 + 512 * (idx % 4) + 512],
                         start=(c == 0), stop=(c == 7)).then_inc(self.sm("lnmc"), 1)
        if not self.final:
            for qt in range(4):
                t.wait_ge(self.sm("lnv"), 1)
                if qt >= 2:
                    t.wait_ge(self.sm("lap"), 8 * (qt - 1))
                t.matmul(bank(ps, qt % 2), pr.c_or_bf[:, :],
                         AB[0:1, self.mbf + 512 * qt:self.mbf + 512 * qt + 512],
                         start=True, stop=True)
                t.matmul(bank(ps, 2 + qt % 2), pr.c_or_bf[:, :],
                         AB[0:1, self.rbf + 512 * qt:self.rbf + 512 * qt + 512],
                         start=True, stop=True).then_inc(self.sm("lbc"), 1)
        else:
            t.wait_ge(self.sm("lnv"), 1)
            for qt in range(4):
                t.matmul(bank(ps, qt), pr.c_or_bf[:, :],
                         AB[0:1, self.mbf + 512 * qt:self.mbf + 512 * qt + 512],
                         start=True, stop=True)
                t.matmul(bank(ps, 4 + qt), pr.c_or_bf[:, :],
                         AB[0:1, self.rbf + 512 * qt:self.rbf + 512 * qt + 512],
                         start=True, stop=True).then_inc(self.sm("lbc"), 1)

    def ev(self, v, gate=None):
        pr = self.pr
        ps, AB, AF = pr.ps, pr.AB, pr.AF
        for idx in range(32):
            c, qt = divmod(idx, 4)
            if idx == 0 and gate is not None:
                v.wait_ge(*gate)
            if idx >= 4:
                v.wait_ge(self.sm("lnmc"), idx - 3)
            v.tensor_mul(AB[0:128, self.sq4 + 512 * (idx % 4):self.sq4 + 512 * (idx % 4) + 512],
                         pr.xTb[0:128, 2048 * c + 512 * qt:2048 * c + 512 * qt + 512],
                         pr.xTb[0:128, 2048 * c + 512 * qt:2048 * c + 512 * qt + 512]).then_inc(self.sm("lsq"), 1)
        v.wait_ge(self.sm("lnmc"), 32)
        v.wait_ge(self.sm("stm"), 4)
        A_ = AF[0:1, pr.lnva:pr.lnva + 2048]
        B_ = AF[0:1, pr.lnvb:pr.lnvb + 2048]
        v.tensor_scalar_mul(A_, pr.ps[0:1, 0:2048], 1.0 / 1024.0)
        v.tensor_copy(AB[0:1, self.mbf:self.mbf + 2048], A_)
        v.tensor_mul(A_, A_, A_)
        v.tensor_scalar_mul(B_, pr.ps[0:1, 2048:4096], 1.0 / 1024.0)
        v.tensor_sub(B_, B_, A_).then_inc(self.sm("var"), 1)
        v.wait_ge(self.sm("sd"), 1)
        v.reciprocal(B_, A_)
        v.tensor_copy(AB[0:1, self.rbf:self.rbf + 2048], B_).then_inc(self.sm("lnv"), 1)
        if not self.final:
            for qt in range(4):
                v.wait_ge(self.sm("lbc"), qt + 1)
                for c in range(8):
                    tt = self.tt2 + 512 * (c % 2)
                    xv = pr.xTb[0:128, 2048 * c + 512 * qt:2048 * c + 512 * qt + 512]
                    v.tensor_tensor(AB[0:128, tt:tt + 512], xv, bank(ps, qt % 2), ALU.subtract)
                    v.tensor_tensor(xv, AB[0:128, tt:tt + 512], bank(ps, 2 + qt % 2),
                                    ALU.mult).then_inc(self.sm("lap"), 1)
        else:
            v.wait_ge(self.sm("lbc"), 4)
            for c in range(8):
                if c >= 2:
                    v.wait_ge(self.sm("ost"), 16 * (c - 1))
                for qt in range(4):
                    tt = self.tt2 + 512 * (qt % 2)
                    xv = pr.xTb[0:128, 2048 * c + 512 * qt:2048 * c + 512 * qt + 512]
                    v.tensor_tensor(AB[0:128, tt:tt + 512], xv, bank(ps, qt), ALU.subtract)
                    v.tensor_tensor(AF[0:128, self.out_scr + 2048 * (c % 2) + 512 * qt:
                                       self.out_scr + 2048 * (c % 2) + 512 * qt + 512],
                                    AB[0:128, tt:tt + 512], bank(ps, 4 + qt),
                                    ALU.mult).then_inc(self.sm("lap"), 1)

    def es(self, sc):
        pr = self.pr
        AF = pr.AF
        sc.wait_ge(self.sm("var"), 1)
        sc.activation(AF[0:1, pr.lnva:pr.lnva + 2048],
                      AF[0:1, pr.lnvb:pr.lnvb + 2048],
                      AFT.Sqrt, bias=pr.cval(1e-5, 1)).then_inc(self.sm("sd"), 1)



class Prog:
    def __init__(self, phases):
        self.phases = phases
        nc = bass.Bass("TRN2", target_bir_lowering=False, debug=False)
        self.nc = nc
        self.sem = Sems(nc)
        self.first = phases[0] == "l0a"
        self.last = phases[-1] == "l1f"
        self.inputs = []

        def inp(name, shape, dt=F32):
            self.inputs.append(name)
            return nc.dram_tensor(name, list(shape), dt, kind="ExternalInput").ap()

        self.inp = inp
        if self.first:
            self.h_x = inp("xT", (1024, 2048))
        else:
            self.h_x = inp("xh", (128, 16384), BF16)
        if self.last:
            self.h_out = nc.dram_tensor("out", [1024, 2048], F32, kind="ExternalOutput").ap()
        else:
            self.h_out = nc.dram_tensor("xho", [128, 16384], BF16, kind="ExternalOutput").ap()

        self.h_id = inp("c_id", (128, 128))
        self.h_ones_row = inp("c_ones_row", (1, 128))
        self.h_ones_col = inp("c_ones_col", (128, 1))
        self.h_blk16 = inp("c_blk16", (8, 128, 16))
        self.h_sel16 = inp("c_sel16", (8, 16, 128))
        ph = set(phases)
        if "l0a" in ph:
            self.h_w0q = inp("w0qT", (1024, 1024))
            self.h_w0k = inp("w0kT", (1024, 1024))
            self.h_w0v = inp("w0vT", (1024, 1024))
            self.h_grep = inp("g_rep", (128, 1))
        if "l0f" in ph:
            self.h_w0f = inp("w0f", (1024, 1024))
            self.h_bias0 = inp("bias0", (128, 8))
            for tag in ("so", "si"):
                setattr(self, f"h_bd_{tag}", inp(f"bd_{tag}", (8, 128, 128)))
                setattr(self, f"h_cb_{tag}", inp(f"cb_{tag}", (128, 2)))
                setattr(self, f"h_gam_{tag}", inp(f"gam_{tag}", (128, 2)))
                setattr(self, f"h_bet_{tag}", inp(f"bet_{tag}", (128, 2)))
                setattr(self, f"h_linT_{tag}", inp(f"linT_{tag}", (256, 1024)))
                setattr(self, f"h_linb_{tag}", inp(f"linb_{tag}", (1, 1024)))
        if "l1a" in ph:
            self.h_wqkvT = inp("wqkvT", (1024, 3072))
            self.h_wprjT = inp("wprjT", (1024, 1024))
            self.h_bprj = inp("bprj", (128, 8))
        if "l1f" in ph:
            self.h_wrtT = inp("wrtT", (1024, 8))
            self.h_wg = inp("wg", (8192, 2048))
            self.h_wu = inp("wu", (8192, 2048))
            self.h_wd = inp("wd", (16384, 1024))
            self.h_gb = inp("gb", (128, 128))
            self.h_ub = inp("ub", (128, 128))
            self.h_db = inp("db", (8, 1024))
            self.h_sel8 = inp("c_sel8", (8, 8, 128))

        self.xTb = nc.alloc_sbuf_tensor("xTb", [128, 16384], BF16)
        self.c_id = nc.alloc_sbuf_tensor("cid", [128, 128], F32)
        self.c_or_bf = nc.alloc_sbuf_tensor("corb", [1, 128], BF16)
        self.c_oc_bf = nc.alloc_sbuf_tensor("cocb", [128, 1], BF16)
        self.c_blk16 = nc.alloc_sbuf_tensor("cblk", [128, 8 * 16], BF16)
        self.c_sel16 = nc.alloc_sbuf_tensor("csel", [16, 8 * 128], BF16)
        self.AB = nc.alloc_sbuf_tensor("AB", [128, 68672], BF16)
        self.AF = nc.alloc_sbuf_tensor("AF", [128, 8296], F32)
        self.ps = nc.alloc_psum_tensor("PSB", [128, 4096], F32)

        self.lnva = 8296 - 4096
        self.lnvb = 8296 - 2048
        self.CONSTS = [1e-30, 1e-5, 1e-9, 0.125, -0.125]
        self.cval_base = None
        self.cvt = nc.alloc_sbuf_tensor("cvt", [128, len(self.CONSTS)], F32)

        self.n_init_dma = 6

    def build(self):
        nc = self.nc
        emitters = {"l0a": L0A, "l0f": L0F, "l1a": L1A, "l1f": L1F}
        phase_objs = []
        for i, name in enumerate(self.phases):
            P = emitters[name](self)
            P.is_first = (i == 0)
            P.is_last = (i == len(self.phases) - 1)
            phase_objs.append(P)

        bar = self.sem.bar
        nph = len(self.phases)

        totals = []
        for k, P in enumerate(phase_objs):
            self.sem.set_phase(k)
            tot = {}
            me = _MockEng(tot)
            for fn in (P.et, P.ev, P.es, P.eg, P.esy):
                fn(me)
            totals.append(tot)
        pool_nums = {h.num for h in self.sem.pool}
        bases = []
        cum = {}
        for k in range(nph):
            bases.append(dict(cum))
            for num, n in totals[k].items():
                if num in pool_nums:
                    cum[num] = cum.get(num, 0) + n

        for k, (name, P) in enumerate(zip(self.phases, phase_objs)):
            self.sem.set_phase(k)
            with nc.allow_low_precision(reason="bf16 stream/operands by design"), \
                 nc.Block(f"ph_{name}") as blk:
                def mk(fn, k=k):
                    def run(eng):
                        pe = _EngProxy(eng, bases[k])
                        if k > 0:
                            eng.wait_ge(bar, 5 * k)
                        fn(pe)
                        eng.sem_inc(bar, 1)
                    return run
                blk.tensor(mk(P.et))
                blk.vector(mk(P.ev))
                blk.scalar(mk(P.es))
                blk.gpsimd(mk(P.eg))
                blk.sync(mk(P.esy))
        return nc

    def emit_init_loads(self, g):
        s = self.sem
        n0 = 6
        g.dma_start(out=self.c_id[:, :], in_=self.h_id).then_inc(s.ld, 16)
        g.dma_start(out=self.c_or_bf[:, :], in_=self.h_ones_row).then_inc(s.ld, 16)
        g.dma_start(out=self.c_oc_bf[:, :], in_=self.h_ones_col).then_inc(s.ld, 16)
        g.dma_start(out=self.c_blk16[:, :],
                    in_=self.h_blk16.rearrange("m p k -> p m k")).then_inc(s.ld, 16)
        g.dma_start(out=self.c_sel16[:, :],
                    in_=self.h_sel16.rearrange("m q k -> q m k")).then_inc(s.ld, 16)
        if self.first:
            g.dma_start(out=self.xTb[:, :],
                        in_=self.h_x.rearrange("(c p) t -> p c t", p=128)).then_inc(s.ld, 16)
        else:
            g.dma_start(out=self.xTb[:, :], in_=self.h_x).then_inc(s.ld, 16)
        assert self.n_init_dma == n0

    def ld_val(self):
        return 16 * self.n_init_dma

    def emit_cval_sets(self, v):
        for j, val in enumerate(self.CONSTS):
            ins = v.memset(self.cvt[:, j:j + 1], val)
        ins.then_inc(self.sem.cvals, 1)

    def cval(self, val, p=128):
        j = self.CONSTS.index(val)
        return self.cvt[0:p, j:j + 1]



class L0A:
    def __init__(self, pr: Prog):
        self.pr = pr
        nc = pr.nc
        A = Arena(pr.AB)
        self.WT = A.take(8192)
        self.qhT = A.take(16384)
        self.khT = A.take(16384)
        self.vaug = A.take(16640)
        self.PT = self.WT
        self.sq4 = A.take(2048)
        self.rstd2 = A.take(1024)
        self.rec2 = A.take(1024)
        self.rbsb2 = A.take(1024)
        self.t1b2 = A.take(1024)
        self.mbf = self.rstd2
        self.rbf = self.rbsb2
        self.tt2 = self.vaug
        self.ln = LNBlock(pr, "a_", self.sq4, self.mbf, self.rbf, self.tt2)
        assert A.off <= pr.AB.shape[1]

        Fa = Arena(pr.AF)
        self.vtmp = Fa.take(1024)
        self.ns2 = Fa.take(1024)
        self.grep = Fa.take(1)
        self.nabsg = Fa.take(1)
        self.lnva = Fa.take(2048)
        self.lnvb = Fa.take(2048)
        self.Fa = Fa

    def et(self, t):
        pr, s = self.pr, self.pr.sem
        nc = pr.nc
        ps = pr.ps
        AB, AF = pr.AB, pr.AF
        t.wait_ge(s.ld, pr.ld_val())

        def main_mm(a, w, qt, m):
            prior = a - 4
            if prior >= 0:
                if prior < 64:
                    t.wait_ge(s.a_ev, prior + 1)
                else:
                    t.wait_ge(s.a_vtmp, prior - 64 + 1)
            t.wait_ge(s.a_wl, 16 * (w + 1))
            for e in range(8):
                ins = t.matmul(bank(ps, a % 4),
                               AB[0:128, self.WT + 1024 * e + 128 * m:self.WT + 1024 * e + 128 * m + 128],
                               pr.xTb[0:128, 2048 * e + 512 * qt:2048 * e + 512 * qt + 512],
                               start=(e == 0), stop=(e == 7))
            ins.then_inc(s.a_mm, 1)

        def nmm(b, G, m):
            if m == 0 and G >= 2:
                t.wait_ge(s.a_sqrt, G - 1)
            t.wait_ge(s.a_sq, b + 1)
            t.matmul(bank(ps, 4 + G % 2, 0, 16),
                     pr.c_blk16[0:128, 16 * m:16 * m + 16],
                     AB[0:128, self.sq4 + 512 * (b % 4):self.sq4 + 512 * (b % 4) + 512],
                     start=(m == 0), stop=(m == 7)).then_inc(s.a_nmm, 1)

        def bmm(G, m):
            bm = G * 8 + m
            t.wait_ge(s.a_rcp, G + 1)
            if bm >= 2:
                t.wait_ge(s.a_nml, bm - 1)
            t.matmul(bank(ps, 6 + m % 2),
                     pr.c_sel16[0:16, 128 * m:128 * m + 128],
                     AB[0:16, self.rstd2 + 512 * (G % 2):self.rstd2 + 512 * (G % 2) + 512],
                     start=True, stop=True).then_inc(s.a_bmm, 1)

        nqk = 64
        for b in range(nqk + 9):
            if b < nqk:
                w, r = divmod(b, 32)
                qt, m = divmod(r, 8)
                main_mm(b, w, qt, m)
            if 1 <= b + 0 and b - 1 < nqk and b - 1 >= 0:
                G, m = divmod(b - 1, 8)
                nmm(b - 1, G, m)
            if b - 9 >= 0 and b - 9 < nqk:
                G, m = divmod(b - 9, 8)
                bmm(G, m)

        def v_tr(vi, qt, m):
            if vi < 2:
                t.wait_ge(s.a_sqrt, 7 + vi)
            t.wait_ge(s.a_vtmp, vi + 1)
            if vi >= 2:
                t.wait_ge(s.a_vaug, vi - 1)
            for j in range(4):
                ins = t.transpose(bank(ps, 4 + vi % 2, 0, 128, 128 * j, 128 * j + 128),
                                  AF[0:128, self.vtmp + 512 * (vi % 2) + 128 * j:
                                     self.vtmp + 512 * (vi % 2) + 128 * j + 128],
                                  pr.c_id[:, :])
            ins.then_inc(s.a_vtr, 1)

        for vi in range(32 + 2):
            if vi < 32:
                qt, m = divmod(vi, 8)
                main_mm(64 + vi, 2, qt, m)
            if vi - 2 >= 0:
                qt, m = divmod(vi - 2, 8)
                v_tr(vi - 2, qt, m)

        t.wait_ge(s.a_nml, 64)
        t.wait_ge(s.a_vaug, 32)

        def sc(j, ktg):
            h, qt = divmod(j, 4)
            p, hb = h // 2, 64 * (h % 2)
            g = 4 * j + ktg
            if g >= 1:
                t.wait_ge(s.a_exp, g)
            for kseq in range(4):
                kt = 4 * ktg + kseq
                ins = t.matmul(bank(ps, kseq),
                               AB[hb:hb + 64, self.khT + 2048 * p + 128 * kt:
                                  self.khT + 2048 * p + 128 * kt + 128],
                               AB[hb:hb + 64, self.qhT + 2048 * p + 512 * qt:
                                  self.qhT + 2048 * p + 512 * qt + 512],
                               start=True, stop=True, tile_position=(hb, 0))
            ins.then_inc(s.a_smm, 1)

        def av(j, ktg):
            h, qt = divmod(j, 4)
            p, half = h // 2, h % 2
            t.wait_ge(s.a_exp, 4 * j + ktg + 1)
            if ktg == 0 and j >= 2:
                t.wait_ge(s.a_t1, j - 1)
            for kseq in range(4):
                kt = 4 * ktg + kseq
                ins = t.matmul(bank(ps, 4 + 2 * (j % 2), 0, 65),
                               AB[0:128, self.vaug + 2080 * p + 130 * kt + 65 * half:
                                  self.vaug + 2080 * p + 130 * kt + 65 * half + 65],
                               AB[0:128, self.PT + 512 * kt:self.PT + 512 * kt + 512],
                               start=(ktg == 0 and kseq == 0), stop=(ktg == 3 and kseq == 3))
            ins.then_inc(s.a_avm, 1)

        def rb(j):
            t.wait_ge(s.a_rec, j + 1)
            if j >= 2:
                t.wait_ge(s.a_t1, j - 1)
            t.matmul(bank(ps, 5 + 2 * (j % 2), 0, 64),
                     pr.c_or_bf[0:1, 0:64],
                     AB[0:1, self.rec2 + 512 * (j % 2):self.rec2 + 512 * (j % 2) + 512],
                     start=True, stop=True).then_inc(s.a_rbm, 1)

        NT = 64
        for j in range(NT + 1):
            for ktg in range(4):
                if j - 1 >= 0:
                    av(j - 1, ktg)
                if j < NT:
                    sc(j, ktg)
            if j - 1 >= 0:
                rb(j - 1)

        self.ln.et(t, gate=(s.a_add, NT))

    def ev(self, v):
        pr, s = self.pr, self.pr.sem
        ps = pr.ps
        AB, AF = pr.AB, pr.AF
        v.wait_ge(s.ld, pr.ld_val())
        if self.is_first:
            pr.emit_cval_sets(v)
        v.wait_ge(s.a_gabs, 1)
        v.tensor_scalar_mul(AF[0:128, self.nabsg:self.nabsg + 1],
                            AF[0:128, self.nabsg:self.nabsg + 1], -1.0).then_inc(s.a_gneg, 1)

        vv = AB[0:128, self.vaug:self.vaug + 16640].rearrange("P (p k c) -> P p k c", p=8, c=130)
        v.memset(vv[:, :, :, 64:65], 1.0)
        v.memset(vv[:, :, :, 129:130], 1.0).then_inc(s.a_vone, 1)

        def copy_sq(b):
            w, r = divmod(b, 32)
            qt, m = divmod(r, 8)
            dst = (self.qhT if w == 0 else self.khT) + 2048 * m + 512 * qt
            v.wait_ge(s.a_mm, b + 1)
            v.tensor_copy(AB[0:128, dst:dst + 512], bank(ps, b % 4)).then_inc(s.a_ev, 1)
            if b >= 4:
                v.wait_ge(s.a_nmm, b - 3)
            v.tensor_mul(AB[0:128, self.sq4 + 512 * (b % 4):self.sq4 + 512 * (b % 4) + 512],
                         AB[0:128, dst:dst + 512],
                         AB[0:128, dst:dst + 512]).then_inc(s.a_sq, 1)

        def recip(G):
            v.wait_ge(s.a_sqrt, G + 1)
            if G >= 2:
                v.wait_ge(s.a_bmm, 8 * G - 8)
            v.reciprocal(AB[0:16, self.rstd2 + 512 * (G % 2):self.rstd2 + 512 * (G % 2) + 512],
                         AF[0:16, self.ns2 + 512 * (G % 2):self.ns2 + 512 * (G % 2) + 512]).then_inc(s.a_rcp, 1)

        def nmult(bm):
            G, m = divmod(bm, 8)
            w, r = divmod(bm, 32)
            qt = r // 8
            dst = (self.qhT if w == 0 else self.khT) + 2048 * m + 512 * qt
            v.wait_ge(s.a_bmm, bm + 1)
            v.tensor_tensor(AB[0:128, dst:dst + 512],
                            AB[0:128, dst:dst + 512],
                            bank(ps, 6 + m % 2), ALU.mult).then_inc(s.a_nml, 1)

        for b in range(64 + 9):
            if b < 64:
                copy_sq(b)
            if b >= 8 and (b % 8) == 0 and (b // 8) - 1 < 8:
                recip((b // 8) - 1)
            if b - 9 >= 0:
                nmult(b - 9)

        v.wait_ge(s.a_vone, 1)
        for vi in range(32):
            qt, m = divmod(vi, 8)
            v.wait_ge(s.a_vtr, vi + 1)
            for j in range(4):
                kt = 4 * qt + j
                base = self.vaug + 2080 * m + 130 * kt
                v.tensor_copy(AB[0:128, base:base + 64],
                              bank(ps, 4 + vi % 2, 0, 128, 128 * j, 128 * j + 64))
                ins = v.tensor_copy(AB[0:128, base + 65:base + 65 + 64],
                                    bank(ps, 4 + vi % 2, 0, 128, 128 * j + 64, 128 * j + 128))
            ins.then_inc(s.a_vaug, 1)

        def a2v(j):
            h, qt = divmod(j, 4)
            if j >= 2:
                v.wait_ge(s.a_rbm, j - 1)
            v.wait_ge(s.a_avm, 4 * j + 4)
            v.reciprocal(AB[0:1, self.rec2 + 512 * (j % 2):self.rec2 + 512 * (j % 2) + 512],
                         bank(ps, 4 + 2 * (j % 2), 64, 65)).then_inc(s.a_rec, 1)

        def a2t1(j):
            h, qt = divmod(j, 4)
            c, hb = h // 2, 64 * (h % 2)
            v.wait_ge(s.a_rbc, j + 1)
            t1 = AB[hb:hb + 64, self.t1b2 + 512 * (j % 2):self.t1b2 + 512 * (j % 2) + 512]
            v.tensor_tensor(t1,
                            bank(ps, 4 + 2 * (j % 2), 0, 64),
                            AB[0:64, self.rbsb2 + 512 * (j % 2):self.rbsb2 + 512 * (j % 2) + 512],
                            ALU.mult).then_inc(s.a_t1, 1)
            dst = 2048 * c + 512 * qt
            v.tensor_add(pr.xTb[hb:hb + 64, dst:dst + 512],
                         pr.xTb[hb:hb + 64, dst:dst + 512],
                         t1).then_inc(s.a_add, 1)

        for j in range(64 + 1):
            if j < 64:
                a2v(j)
            if j - 1 >= 0:
                a2t1(j - 1)

        self.ln.ev(v, gate=(s.a_add, 64))

    def es(self, sc):
        pr, s = self.pr, self.pr.sem
        ps = pr.ps
        AB, AF = pr.AB, pr.AF
        sc.wait_ge(s.ld, pr.ld_val())
        sc.wait_ge(s.a_pl, 16)
        sc.activation(AF[0:128, self.nabsg:self.nabsg + 1],
                      AF[0:128, self.grep:self.grep + 1], AFT.Abs).then_inc(s.a_gabs, 1)

        sc.wait_ge(s.cvals, 1)

        def sqrt_G(G):
            sc.wait_ge(s.a_nmm, 8 * G + 8)
            if G >= 2:
                sc.wait_ge(s.a_rcp, G - 1)
            sc.activation(AF[0:16, self.ns2 + 512 * (G % 2):self.ns2 + 512 * (G % 2) + 512],
                          bank(ps, 4 + G % 2, 0, 16), AFT.Sqrt, bias=pr.cval(1e-30, 16)).then_inc(s.a_sqrt, 1)

        for G in range(8):
            sqrt_G(G)

        def v_copy(vi):
            qt, m = divmod(vi, 8)
            a = 64 + vi
            sc.wait_ge(s.a_mm, a + 1)
            if vi >= 2:
                sc.wait_ge(s.a_vtr, vi - 1)
            sc.copy(AF[0:128, self.vtmp + 512 * (vi % 2):self.vtmp + 512 * (vi % 2) + 512],
                    bank(ps, a % 4)).then_inc(s.a_vtmp, 1)

        for vi in range(32):
            v_copy(vi)

        sc.wait_ge(s.a_gneg, 1)

        def expg(j, ktg):
            g = 4 * j + ktg
            sc.wait_ge(s.a_smm, g + 1)
            if j >= 1:
                sc.wait_ge(s.a_avm, 4 * (j - 1) + ktg + 1)
            sc.activation(AB[0:128, self.PT + 2048 * ktg:self.PT + 2048 * ktg + 2048],
                          pr.ps[0:128, 0:2048], AFT.Exp,
                          bias=AF[0:128, self.nabsg:self.nabsg + 1],
                          scale=AF[0:128, self.grep:self.grep + 1]).then_inc(s.a_exp, 1)

        def rbcopy(j):
            sc.wait_ge(s.a_rbm, j + 1)
            sc.copy(AB[0:64, self.rbsb2 + 512 * (j % 2):self.rbsb2 + 512 * (j % 2) + 512],
                    bank(ps, 5 + 2 * (j % 2), 0, 64)).then_inc(s.a_rbc, 1)

        for j in range(64 + 1):
            if j < 64:
                for ktg in range(4):
                    expg(j, ktg)
            if j - 1 >= 0:
                rbcopy(j - 1)

        self.ln.es(sc)

    def eg(self, g):
        pr, s = self.pr, self.pr.sem
        if self.is_first:
            pr.emit_init_loads(g)
        g.dma_start(out=pr.AF[0:128, self.grep:self.grep + 1], in_=pr.h_grep).then_inc(s.a_pl, 16)
        g.dma_start(out=pr.AB[0:128, self.WT:self.WT + 8192],
                    in_=pr.h_w0q.rearrange("(c p) f -> p c f", p=128)).then_inc(s.a_wl, 16)
        g.wait_ge(s.a_mm, 32)
        g.dma_start(out=pr.AB[0:128, self.WT:self.WT + 8192],
                    in_=pr.h_w0k.rearrange("(c p) f -> p c f", p=128)).then_inc(s.a_wl, 16)
        g.wait_ge(s.a_mm, 64)
        g.dma_start(out=pr.AB[0:128, self.WT:self.WT + 8192],
                    in_=pr.h_w0v.rearrange("(c p) f -> p c f", p=128)).then_inc(s.a_wl, 16)

    def esy(self, sy):
        pr, s = self.pr, self.pr.sem
        if self.is_last:
            sy.wait_ge(s.a_lap, 32)
            sy.dma_start(out=pr.h_out, in_=pr.xTb[:, :]).then_inc(s.a_out, 16)
            sy.wait_ge(s.a_out, 16)




class L0F:

    def __init__(self, pr: Prog):
        self.pr = pr
        nc = pr.nc
        A = Arena(pr.AB)
        self.WT = A.take(8192)
        self.linT = A.take(4096)
        self.bd = A.take(2048)
        self.xc = A.take(8192)
        self.xa = A.take(8192)
        self.full = A.take(16384)
        self.xsp = A.take(16384)
        self.sqb = A.take(2048)
        self.linb = self.sqb
        self.prm = A.take(256)
        self.cmb2 = A.take(1024)
        self.mbf = self.xa + 0
        self.rbf = self.xa + 2048
        self.tt2 = self.xa + 4096
        self.sq4 = self.xa + 5120
        self.ln = LNBlock(pr, "f_", self.sq4, self.mbf, self.rbf, self.tt2)
        assert A.off <= pr.AB.shape[1], A.off

        Fa = Arena(pr.AF)
        self.lg = Fa.take(2048)
        self.zap = Fa.take(2048)
        self.sml = Fa.take(64)
        self.parts = Fa.take(8)
        self.allr = Fa.take(8)
        self.bnv = Fa.take(16)
        assert Fa.off <= pr.lnva, Fa.off

        self.ccin = nc.dram_tensor("ccin", [128, 8], F32, kind="Internal")
        self.ccout = nc.dram_tensor("ccout", [128, 8], F32, kind="Internal")
        if DBG:
            self.d_xsp = nc.dram_tensor("d_xsp", [128, 16384], BF16, kind="ExternalOutput").ap()
            self.d_full = nc.dram_tensor("d_full", [128, 16384], BF16, kind="ExternalOutput").ap()
            self.d_xc = nc.dram_tensor("d_xc", [128, 8192], BF16, kind="ExternalOutput").ap()
            self.d_xa = nc.dram_tensor("d_xa", [128, 8192], BF16, kind="ExternalOutput").ap()
            self.d_lg = nc.dram_tensor("d_lg", [128, 2048], F32, kind="ExternalOutput").ap()
            self.d_zap = nc.dram_tensor("d_zap", [128, 2048], F32, kind="ExternalOutput").ap()
            self.d_bnv = nc.dram_tensor("d_bnv", [128, 16], F32, kind="ExternalOutput").ap()
            self.d_sml = nc.dram_tensor("d_sml", [128, 64], F32, kind="ExternalOutput").ap()

        self.vsc = 0

    def smlv(self, off, p=128, w=1):
        return self.pr.AF[0:p, self.sml + off:self.sml + off + w]

    def sem(self, name, br):
        return getattr(self.pr.sem, f"f_{name}{br}")

    def et(self, t):
        pr, s = self.pr, self.pr.sem
        ps, AB = pr.ps, pr.AB
        t.wait_ge(s.ld, pr.ld_val())

        for idx in range(16):
            br, r = divmod(idx, 8)
            ti, qt = divmod(r, 4)
            t.wait_ge(s.f_wsm, 176)
            if idx >= 2:
                t.wait_ge(s.f_cev, idx - 1)
            for cc in range(4):
                c = 4 * ti + cc
                t.matmul(bank(ps, idx % 2),
                         AB[0:128, self.bd + 1024 * br + 128 * c:self.bd + 1024 * br + 128 * c + 128],
                         pr.xTb[0:128, 2048 * c + 512 * qt:2048 * c + 512 * qt + 512],
                         start=(cc == 0), stop=(cc == 3)).then_inc(s.f_cmm, (1 if cc == 3 else 0))

        for idx in range(32):
            m, qt = divmod(idx, 4)
            t.wait_ge(s.f_wl, 16)
            if idx >= 2:
                t.wait_ge(s.f_fme, idx - 1)
            for e in range(8):
                t.matmul(bank(ps, 2 + idx % 2),
                         AB[0:128, self.WT + 1024 * e + 128 * m:self.WT + 1024 * e + 128 * m + 128],
                         pr.xTb[0:128, 2048 * e + 512 * qt:2048 * e + 512 * qt + 512],
                         start=(e == 0), stop=(e == 7)).then_inc(s.f_fmm, (1 if e == 7 else 0))

        def lmm(j):
            nt, br = divmod(j, 2)
            if nt == 0:
                t.wait_ge(s.f_gel, 2 * br + 2)
                t.wait_ge(s.f_lbl, 32)
                if br == 0:
                    t.wait_ge(s.f_cev, 16)
                else:
                    t.wait_ge(s.f_fme, 32)
            else:
                t.wait_ge(self.sem("exp", br), nt)
            for o in range(2):
                bk = bank(ps, 2 * br + o)
                for ti in range(2):
                    t.matmul(bk,
                             AB[0:128, self.xa + 4096 * br + 2048 * ti + 128 * nt:
                                self.xa + 4096 * br + 2048 * ti + 128 * nt + 128],
                             AB[0:128, self.linT + 2048 * br + 1024 * ti + 512 * o:
                                self.linT + 2048 * br + 1024 * ti + 512 * o + 512],
                             start=(ti == 0), stop=False)
                ins = t.matmul(bk, pr.c_or_bf[0:1, 0:128],
                               AB[0:1, self.linb + 1024 * br + 512 * o:
                                  self.linb + 1024 * br + 512 * o + 512],
                               start=False, stop=True)
            ins.then_inc(self.sem("lmm", br), 1)

        def mtr(j):
            nt, br = divmod(j, 2)
            t.wait_ge(self.sem("zs", br), nt + 1)
            if nt >= 1:
                t.wait_ge(self.sem("mev", br), nt)
            for q in range(8):
                bk = 4 + 2 * br + q // 4
                ins = t.transpose(bank(ps, bk, 0, 128, 128 * (q % 4), 128 * (q % 4) + 128),
                                  pr.AF[0:128, self.zap + 1024 * br + 128 * q:
                                        self.zap + 1024 * br + 128 * q + 128],
                                  pr.c_id[:, :])
            ins.then_inc(self.sem("mtr", br), 1)

        for k in range(34):
            if k < 32:
                lmm(k)
            if 0 <= k - 2 < 32:
                mtr(k - 2)

        for idx in range(32):
            m, qt = divmod(idx, 4)
            if idx == 0:
                t.wait_ge(s.f_mev0, 16)
                t.wait_ge(s.f_mev1, 16)
            if idx >= 4:
                t.wait_ge(s.f_cmb, idx - 3)
            for e in range(8):
                t.matmul(bank(ps, 4 + idx % 4),
                         AB[0:128, self.WT + 1024 * e + 128 * m:self.WT + 1024 * e + 128 * m + 128],
                         AB[0:128, self.xsp + 2048 * e + 512 * qt:self.xsp + 2048 * e + 512 * qt + 512],
                         start=(e == 0), stop=(e == 7)).then_inc(s.f_imm, (1 if e == 7 else 0))

        self.ln.et(t, gate=(s.f_cmb, 32))

    def ev(self, v):
        pr, s = self.pr, self.pr.sem
        ps, AB, AF = pr.ps, pr.AB, pr.AF
        v.wait_ge(s.ld, pr.ld_val())
        if self.is_first:
            pr.emit_cval_sets(v)
        self.vsc = 0

        for br in range(2):
            for ti in range(2):
                v.wait_ge(s.f_cev, 8 * br + 4 * ti + 4)
                xcv = AB[0:128, self.xc + 4096 * br + 2048 * ti:self.xc + 4096 * br + 2048 * ti + 2048]
                v.reduce_sum(AF[0:128, self.parts + 4 * br + ti:self.parts + 4 * br + ti + 1],
                             xcv, axis=AX.X)
                v.tensor_mul(AB[0:128, self.sqb:self.sqb + 2048], xcv, xcv)
                v.reduce_sum(AF[0:128, self.parts + 4 * br + 2 + ti:self.parts + 4 * br + 2 + ti + 1],
                             AB[0:128, self.sqb:self.sqb + 2048], axis=AX.X).then_inc(s.f_bnp, 1)

        v.wait_ge(s.f_ccd, 32)
        cN = 1.0 / 16384.0
        def vsn(ins):
            self.vsc += 1
            ins.then_inc(s.f_vsy, 1)
            return self.vsc
        def vs(ins):
            n = vsn(ins)
            v.wait_ge(s.f_vsy, n)
        for br in range(2):
            b0 = self.bnv + 8 * br
            mean = AF[0:128, b0:b0 + 2]
            var = AF[0:128, b0 + 2:b0 + 4]
            scl = AF[0:128, b0 + 4:b0 + 6]
            shf = AF[0:128, b0 + 6:b0 + 8]
            ar_s = AF[0:128, self.allr + 4 * br:self.allr + 4 * br + 2]
            ar_q = AF[0:128, self.allr + 4 * br + 2:self.allr + 4 * br + 4]
            vs(v.tensor_scalar_mul(mean, ar_s, cN))
            vs(v.tensor_scalar_mul(var, ar_q, cN))
            vs(v.tensor_mul(scl, mean, mean))
            v.tensor_sub(var, var, scl).then_inc(s.f_bvar, 1)
            v.wait_ge(s.f_bsd, br + 1)
            vs(v.reciprocal(scl, var))
            vs(v.tensor_mul(scl, scl, AB[0:128, self.prm + 4 + 8 * br:self.prm + 4 + 8 * br + 2]))
            vs(v.tensor_mul(shf, mean, scl))
            v.tensor_sub(shf, AB[0:128, self.prm + 8 + 8 * br:self.prm + 8 + 8 * br + 2],
                         shf).then_inc(s.f_bn, 1)

        self.r4_cnt = {}

        def Va(nt, br):
            v.wait_ge(self.sem("lmm", br), nt + 1)
            v.reduce_max(self.smlv(0 + br), ps[0:128, 1024 * br:1024 * br + 1024], axis=AX.X)
            v.tensor_scalar_mul(self.smlv(2 + br), self.smlv(0 + br),
                                -1.0).then_inc(self.sem("mx", br), 1)

        def rzR(nt, br):
            v.wait_ge(self.sem("exp", br), nt + 1)
            if nt >= 1:
                v.wait_ge(self.sem("zs", br), nt)
            v.reciprocal(self.smlv(6 + br), self.smlv(4 + br)).then_inc(self.sem("rz", br), 1)

        def Vc(nt, br):
            v.wait_ge(self.sem("mtr", br), nt + 1)
            for q in range(8):
                bk = 4 + 2 * br + q // 4
                bkv = bank(ps, bk, 0, 128, 128 * (q % 4), 128 * (q % 4) + 128)
                if br == 0:
                    ins = v.tensor_tensor(
                        AB[0:128, self.xsp + 2048 * q + 128 * nt:self.xsp + 2048 * q + 128 * nt + 128],
                        bkv, pr.xTb[0:128, 2048 * q + 128 * nt:2048 * q + 128 * nt + 128], ALU.mult)
                else:
                    fv = AB[0:128, self.full + 2048 * q + 128 * nt:self.full + 2048 * q + 128 * nt + 128]
                    ins = v.tensor_tensor(fv, bkv, fv, ALU.mult)
            ins.then_inc(self.sem("mev", br), 1)

        for nt in range(16):
            for br in range(2):
                Va(nt, br)
            if nt >= 1:
                for br in range(2):
                    Vc(nt - 1, br)
            lgv = [AF[0:128, self.lg + 1024 * b:self.lg + 1024 * b + 1024] for b in range(2)]
            zp = [AF[0:128, self.zap + 1024 * b:self.zap + 1024 * b + 1024] for b in range(2)]
            m8 = [self.smlv(8 + 8 * b, 128, 8) for b in range(2)]
            cnt = {}
            for half in range(4):
                for br in range(2):
                    if half == 0:
                        v.wait_ge(self.sem("exp", br), nt + 1)
                        if nt >= 1:
                            v.wait_ge(self.sem("mtr", br), nt)
                        src = lgv[br]
                    else:
                        v.wait_ge(s.f_vsy, cnt[(br, half - 1, 1)])
                        src = zp[br]
                    cnt[(br, half, 0)] = vsn(v.max(out=m8[br], in_=src))
                for br in range(2):
                    v.wait_ge(s.f_vsy, cnt[(br, half, 0)])
                    src = lgv[br] if half == 0 else zp[br]
                    cnt[(br, half, 1)] = vsn(
                        v.match_replace(out=zp[br], in_to_replace=m8[br],
                                        in_values=src, imm_value=0.0))
                    if half == 3:
                        self.r4_cnt[(nt, br)] = cnt[(br, 3, 1)]
            for br in range(2):
                rzR(nt, br)
            for br in range(2):
                lgv = AF[0:128, self.lg + 1024 * br:self.lg + 1024 * br + 1024]
                zp = AF[0:128, self.zap + 1024 * br:self.zap + 1024 * br + 1024]
                v.wait_ge(s.f_vsy, self.r4_cnt[(nt, br)])
                vs(v.tensor_sub(zp, lgv, zp))
                v.tensor_scalar_mul(zp, zp, self.smlv(6 + br)).then_inc(self.sem("zs", br), 1)
        for br in range(2):
            Vc(15, br)

        for idx in range(32):
            m, qt = divmod(idx, 4)
            v.wait_ge(s.f_imm, idx + 1)
            cm = AB[0:128, self.cmb2 + 512 * (idx % 2):self.cmb2 + 512 * (idx % 2) + 512]
            fv = AB[0:128, self.full + 2048 * m + 512 * qt:self.full + 2048 * m + 512 * qt + 512]
            v.scalar_tensor_tensor(cm, bank(ps, 4 + idx % 4),
                                   AB[0:128, self.prm + 16 + m:self.prm + 16 + m + 1],
                                   fv, ALU.add, ALU.add)
            xv = pr.xTb[0:128, 2048 * m + 512 * qt:2048 * m + 512 * qt + 512]
            v.tensor_add(xv, xv, cm).then_inc(s.f_cmb, 1)

        self.ln.ev(v, gate=(s.f_cmb, 32))

    def es(self, sc):
        pr, s = self.pr, self.pr.sem
        ps, AB, AF = pr.ps, pr.AB, pr.AF
        sc.wait_ge(s.ld, pr.ld_val())
        sc.wait_ge(s.cvals, 1)

        for idx in range(16):
            br, r = divmod(idx, 8)
            ti, qt = divmod(r, 4)
            sc.wait_ge(s.f_cmm, idx + 1)
            sc.add(AB[0:128, self.xc + 4096 * br + 2048 * ti + 512 * qt:
                      self.xc + 4096 * br + 2048 * ti + 512 * qt + 512],
                   bank(ps, idx % 2),
                   AB[0:128, self.prm + 0 + 8 * br + ti:self.prm + 0 + 8 * br + ti + 1]).then_inc(s.f_cev, 1)

        for idx in range(32):
            m, qt = divmod(idx, 4)
            sc.wait_ge(s.f_fmm, idx + 1)
            sc.copy(AB[0:128, self.full + 2048 * m + 512 * qt:self.full + 2048 * m + 512 * qt + 512],
                    bank(ps, 2 + idx % 2)).then_inc(s.f_fme, 1)

        for br in range(2):
            b0 = self.bnv + 8 * br
            sc.wait_ge(s.f_bvar, br + 1)
            sc.activation(AF[0:128, b0 + 2:b0 + 4], AF[0:128, b0 + 2:b0 + 4],
                          AFT.Sqrt, bias=pr.cval(1e-5, 128)).then_inc(s.f_bsd, 1)
        for br in range(2):
            sc.wait_ge(s.f_bn, br + 1)
            b0 = self.bnv + 8 * br
            for ti in range(2):
                sc.activation(AB[0:128, self.xa + 4096 * br + 2048 * ti:
                                 self.xa + 4096 * br + 2048 * ti + 2048],
                              AB[0:128, self.xc + 4096 * br + 2048 * ti:
                                 self.xc + 4096 * br + 2048 * ti + 2048],
                              AFT.Gelu,
                              bias=AF[0:128, b0 + 6 + ti:b0 + 6 + ti + 1],
                              scale=AF[0:128, b0 + 4 + ti:b0 + 4 + ti + 1]).then_inc(s.f_gel, 1)

        for nt in range(16):
            for br in range(2):
                sc.wait_ge(self.sem("mx", br), nt + 1)
                if nt >= 1:
                    sc.wait_ge(self.sem("zs", br), nt)
                    sc.wait_ge(self.sem("rz", br), nt)
                lgv = AF[0:128, self.lg + 1024 * br:self.lg + 1024 * br + 1024]
                sc.activation(lgv, ps[0:128, 1024 * br:1024 * br + 1024], AFT.Exp,
                              bias=self.smlv(2 + br),
                              accum_out=self.smlv(4 + br)).then_inc(self.sem("exp", br), 1)

        self.ln.es(sc)

    def eg(self, g):
        pr, s = self.pr, self.pr.sem
        AB, AF = pr.AB, pr.AF
        if self.is_first:
            pr.emit_init_loads(g)
        g.dma_start(out=AB[0:128, self.WT:self.WT + 8192],
                    in_=pr.h_w0f.rearrange("(c p) f -> p c f", p=128)).then_inc(s.f_wl, 16)
        g.dma_start(out=AB[0:128, self.prm + 0:self.prm + 2], in_=pr.h_cb_si).then_inc(s.f_wsm, 16)
        g.dma_start(out=AB[0:128, self.prm + 8:self.prm + 10], in_=pr.h_cb_so).then_inc(s.f_wsm, 16)
        g.dma_start(out=AB[0:128, self.prm + 4:self.prm + 6], in_=pr.h_gam_si).then_inc(s.f_wsm, 16)
        g.dma_start(out=AB[0:128, self.prm + 12:self.prm + 14], in_=pr.h_gam_so).then_inc(s.f_wsm, 16)
        g.dma_start(out=AB[0:128, self.prm + 6:self.prm + 8], in_=pr.h_bet_si).then_inc(s.f_wsm, 16)
        g.dma_start(out=AB[0:128, self.prm + 14:self.prm + 16], in_=pr.h_bet_so).then_inc(s.f_wsm, 16)
        g.dma_start(out=AB[0:128, self.prm + 16:self.prm + 24], in_=pr.h_bias0).then_inc(s.f_wsm, 16)
        g.dma_start(out=AB[0:128, self.bd:self.bd + 1024],
                    in_=pr.h_bd_si.rearrange("c p k -> p c k")).then_inc(s.f_wsm, 16)
        g.dma_start(out=AB[0:128, self.bd + 1024:self.bd + 2048],
                    in_=pr.h_bd_so.rearrange("c p k -> p c k")).then_inc(s.f_wsm, 16)
        g.dma_start(out=AB[0:128, self.linT:self.linT + 2048],
                    in_=pr.h_linT_si.rearrange("(c p) f -> p c f", p=128)).then_inc(s.f_wsm, 16)
        g.dma_start(out=AB[0:128, self.linT + 2048:self.linT + 4096],
                    in_=pr.h_linT_so.rearrange("(c p) f -> p c f", p=128)).then_inc(s.f_wsm, 16)

        g.wait_ge(s.f_bnp, 4)
        g.dma_start(out=AB[0:1, self.linb:self.linb + 1024],
                    in_=pr.h_linb_si).then_inc(s.f_lbl, 16)
        g.dma_start(out=AB[0:1, self.linb + 1024:self.linb + 2048],
                    in_=pr.h_linb_so).then_inc(s.f_lbl, 16)
        g.dma_start(out=self.ccin.ap(), in_=AF[0:128, self.parts:self.parts + 8]).then_inc(s.f_ccd, 16)
        g.wait_ge(s.f_ccd, 16)
        g.collective_compute("AllReduce", ALU.add,
                             replica_groups=[list(range(8))],
                             ins=[self.ccin.ap().opt()],
                             outs=[self.ccout.ap().opt()]).then_inc(s.f_cc, 1)
        g.wait_ge(s.f_cc, 1)
        g.dma_start(out=AF[0:128, self.allr:self.allr + 8], in_=self.ccout.ap()).then_inc(s.f_ccd, 16)


    def esy(self, sy):
        pr, s = self.pr, self.pr.sem
        if DBG:
            sy.wait_ge(s.f_cmb, 32)
            AB, AF = pr.AB, pr.AF
            sy.dma_start(out=self.d_xsp, in_=AB[0:128, self.xsp:self.xsp + 16384]).then_inc(s.f_dbg, 16)
            sy.dma_start(out=self.d_full, in_=AB[0:128, self.full:self.full + 16384]).then_inc(s.f_dbg, 16)
            sy.dma_start(out=self.d_xc, in_=AB[0:128, self.xc:self.xc + 8192]).then_inc(s.f_dbg, 16)
            sy.dma_start(out=self.d_xa, in_=AB[0:128, self.xa:self.xa + 8192]).then_inc(s.f_dbg, 16)
            sy.dma_start(out=self.d_lg, in_=AF[0:128, self.lg:self.lg + 2048]).then_inc(s.f_dbg, 16)
            sy.dma_start(out=self.d_zap, in_=AF[0:128, self.zap:self.zap + 2048]).then_inc(s.f_dbg, 16)
            sy.dma_start(out=self.d_bnv, in_=AF[0:128, self.bnv:self.bnv + 16]).then_inc(s.f_dbg, 16)
            sy.dma_start(out=self.d_sml, in_=AF[0:128, self.sml:self.sml + 64]).then_inc(s.f_dbg, 16)
            sy.wait_ge(s.f_dbg, 128)
        if self.is_last:
            sy.wait_ge(s.f_lap, 32)
            sy.dma_start(out=pr.h_out, in_=pr.xTb[:, :]).then_inc(s.f_out, 16)
            sy.wait_ge(s.f_out, 16)



class _PH:
    def __init__(self, pr):
        self.pr = pr
    def et(self, t): pass
    def ev(self, v): pass
    def es(self, sc): pass
    def eg(self, g):
        if self.is_first:
            self.pr.emit_init_loads(g)
    def esy(self, sy): pass



class L1A:

    def __init__(self, pr: Prog):
        self.pr = pr
        A = Arena(pr.AB)
        self.kT = A.take(16384)
        self.vtok = A.take(16384)
        self.Ost = A.take(16384)
        self.qTb = A.take(4096)
        self.PT = A.take(8192)
        self.wsl = A.take(4096)
        self.prm = A.take(16)
        self.mbf = self.PT + 0
        self.rbf = self.PT + 2048
        self.tt2 = self.PT + 4096
        self.sq4 = self.PT + 5120
        self.ln = LNBlock(pr, "b_", self.sq4, self.mbf, self.rbf, self.tt2)
        assert A.off <= pr.AB.shape[1], A.off

        Fa = Arena(pr.AF)
        self.P2 = Fa.take(4096)
        self.sml = Fa.take(64)
        assert Fa.off <= pr.lnva, Fa.off

    def slab_buf(self, i):
        if i < 4 and i % 2 == 1:
            return self.qTb
        return self.wsl

    def et(self, t):
        pr, s = self.pr, self.pr.sem
        ps, AB, AF = pr.ps, pr.AB, pr.AF
        t.wait_ge(s.ld, pr.ld_val())

        kv_tile = 0
        for sl in range(4):
            kind, half = divmod(sl, 2)
            buf = self.slab_buf(sl)
            t.wait_ge(s.b_wl, 16 * (sl + 1))
            if kind == 0:
                for mm in range(4):
                    m = 4 * half + mm
                    for qt in range(4):
                        if kv_tile >= 4:
                            t.wait_ge(s.b_kev, kv_tile - 3)
                        for e in range(8):
                            t.matmul(bank(ps, kv_tile % 4),
                                     AB[0:128, buf + 512 * e + 128 * mm:buf + 512 * e + 128 * mm + 128],
                                     pr.xTb[0:128, 2048 * e + 512 * qt:2048 * e + 512 * qt + 512],
                                     start=(e == 0), stop=(e == 7)).then_inc(
                                         s.b_kvm, (1 if e == 7 else 0))
                        kv_tile += 1
            else:
                for tt in range(16):
                    if kv_tile >= 4:
                        t.wait_ge(s.b_kev, kv_tile - 3)
                    for e in range(8):
                        t.matmul(bank(ps, kv_tile % 4),
                                 pr.xTb[0:128, 2048 * e + 128 * tt:2048 * e + 128 * tt + 128],
                                 AB[0:128, buf + 512 * e:buf + 512 * e + 512],
                                 start=(e == 0), stop=(e == 7)).then_inc(
                                     s.b_kvm, (1 if e == 7 else 0))
                    kv_tile += 1
            t.sem_inc(s.b_sdone, 1)

        for qb in range(4):
            for half in range(2):
                sl = 4 + 2 * qb + half
                t.wait_ge(s.b_wl, 16 * (sl + 1))
                for mm in range(4):
                    m = 4 * half + mm
                    if 8 * qb + m >= 2:
                        t.wait_ge(s.b_qev, 8 * qb + m - 1)
                    for e in range(8):
                        t.matmul(bank(ps, 4 + m % 2),
                                 AB[0:128, self.slab_buf(sl) + 512 * e + 128 * mm:
                                    self.slab_buf(sl) + 512 * e + 128 * mm + 128],
                                 pr.xTb[0:128, 2048 * e + 512 * qb:2048 * e + 512 * qb + 512],
                                 start=(e == 0), stop=(e == 7)).then_inc(
                                     s.b_qmm, (1 if e == 7 else 0))
                t.sem_inc(s.b_sdone, 1)
            for il in range(4):
                qg = 4 * qb + il
                t.wait_ge(s.b_qev, 8 * qb + 8)
                if qg == 0:
                    t.wait_ge(s.b_kev, 64)
                if qg >= 1:
                    t.wait_ge(s.b_exp, qg)
                for ks in range(4):
                    for m in range(8):
                        t.matmul(bank(ps, ks),
                                 AB[0:128, self.qTb + 512 * m + 128 * il:
                                    self.qTb + 512 * m + 128 * il + 128],
                                 AB[0:128, self.kT + 2048 * m + 512 * ks:
                                    self.kT + 2048 * m + 512 * ks + 512],
                                 start=(m == 0), stop=(m == 7)).then_inc(
                                     s.b_smm, (1 if (ks == 3 and m == 7) else 0))
                t.wait_ge(s.b_exp, qg + 1)
                if qg >= 1:
                    t.wait_ge(s.b_ptv, qg)
                for jt in range(16):
                    bk = 4 + jt // 4
                    ins = t.transpose(bank(ps, bk, 0, 128, 128 * (jt % 4), 128 * (jt % 4) + 128),
                                      AF[0:128, self.P2 + 2048 * (qg % 2) + 128 * jt:
                                         self.P2 + 2048 * (qg % 2) + 128 * jt + 128],
                                      pr.c_id[:, :])
                ins.then_inc(s.b_ptr, 1)
            t.wait_ge(s.b_ptv, 4 * qb + 4)
            for il in range(4):
                for hs in range(2):
                    gidx = 8 * qb + 2 * il + hs
                    if gidx >= 2:
                        t.wait_ge(s.b_oev, gidx - 1)
                    for jt in range(16):
                        t.matmul(bank(ps, gidx % 2),
                                 AB[0:128, self.PT + 512 * jt + 128 * il:
                                    self.PT + 512 * jt + 128 * il + 128],
                                 AB[0:128, self.vtok + 1024 * jt + 512 * hs:
                                    self.vtok + 1024 * jt + 512 * hs + 512],
                                 start=(jt == 0), stop=(jt == 15)).then_inc(
                                     s.b_avm, (1 if jt == 15 else 0))

        re = AB[0:128, self.Ost:self.Ost + 16384].rearrange("p (b rest) -> p rest b", b=2)
        t.wait_ge(s.b_oev, 32)
        for half in range(2):
            sl = 12 + half
            t.wait_ge(s.b_wl, 16 * (sl + 1))
            for mm in range(4):
                m = 4 * half + mm
                for st in range(4):
                    idx = 16 * half + 4 * mm + st
                    if idx >= 4:
                        t.wait_ge(s.b_pja, idx - 3)
                    for c in range(8):
                        t.matmul(bank(ps, idx % 4),
                                 AB[0:128, self.wsl + 512 * c + 128 * mm:
                                    self.wsl + 512 * c + 128 * mm + 128],
                                 re[:, 1024 * c + 256 * st:1024 * c + 256 * st + 256, :],
                                 start=(c == 0), stop=(c == 7)).then_inc(
                                     s.b_pjm, (1 if c == 7 else 0))
            t.sem_inc(s.b_sdone, 1)

        self.ln.et(t, gate=(s.b_pja, 32))

    def ev(self, v):
        pr, s = self.pr, self.pr.sem
        ps, AB, AF = pr.ps, pr.AB, pr.AF
        v.wait_ge(s.ld, pr.ld_val())
        if self.is_first:
            pr.emit_cval_sets(v)

        kv_tile = 0
        for sl in range(4):
            kind, half = divmod(sl, 2)
            if kind == 0:
                for mm in range(4):
                    m = 4 * half + mm
                    for qt in range(4):
                        v.wait_ge(s.b_kvm, kv_tile + 1)
                        v.tensor_copy(AB[0:128, self.kT + 2048 * m + 512 * qt:
                                         self.kT + 2048 * m + 512 * qt + 512],
                                      bank(ps, kv_tile % 4)).then_inc(s.b_kev, 1)
                        kv_tile += 1
            else:
                for tt in range(16):
                    v.wait_ge(s.b_kvm, kv_tile + 1)
                    v.tensor_copy(AB[0:128, self.vtok + 1024 * tt + 512 * half:
                                     self.vtok + 1024 * tt + 512 * half + 512],
                                  bank(ps, kv_tile % 4)).then_inc(s.b_kev, 1)
                    kv_tile += 1

        for qb in range(4):
            for m in range(8):
                if m == 0 and qb >= 1:
                    v.wait_ge(s.b_smm, 4 * qb)
                v.wait_ge(s.b_qmm, 8 * qb + m + 1)
                v.tensor_copy(AB[0:128, self.qTb + 512 * m:self.qTb + 512 * m + 512],
                              bank(ps, 4 + m % 2)).then_inc(s.b_qev, 1)
            for il in range(4):
                qg = 4 * qb + il
                v.wait_ge(s.b_smm, qg + 1)
                v.reduce_max(AF[0:128, self.sml + qg:self.sml + qg + 1],
                             ps[0:128, 0:2048], axis=AX.X)
                v.tensor_scalar_mul(AF[0:128, self.sml + 16 + qg:self.sml + 16 + qg + 1],
                                    AF[0:128, self.sml + qg:self.sml + qg + 1],
                                    -0.125).then_inc(s.b_mx, 1)
                v.wait_ge(s.b_ptr, qg + 1)
                if qb >= 1 and il == 0:
                    v.wait_ge(s.b_avm, 8 * qb)
                for jt in range(16):
                    bk = 4 + jt // 4
                    ins = v.tensor_copy(AB[0:128, self.PT + 512 * jt + 128 * il:
                                           self.PT + 512 * jt + 128 * il + 128],
                                        bank(ps, bk, 0, 128, 128 * (jt % 4), 128 * (jt % 4) + 128))
                ins.then_inc(s.b_ptv, 1)
                v.wait_ge(s.b_exp, qg + 1)
                v.reciprocal(AF[0:128, self.sml + 48 + qg:self.sml + 48 + qg + 1],
                             AF[0:128, self.sml + 32 + qg:self.sml + 32 + qg + 1]).then_inc(s.b_rz, 1)
            for il in range(4):
                for hs in range(2):
                    gidx = 8 * qb + 2 * il + hs
                    qg = 4 * qb + il
                    v.wait_ge(s.b_rz, qg + 1)
                    v.wait_ge(s.b_avm, gidx + 1)
                    tt = 4 * qb + il
                    v.tensor_scalar_mul(AB[0:128, self.Ost + 1024 * tt + 512 * hs:
                                           self.Ost + 1024 * tt + 512 * hs + 512],
                                        bank(ps, gidx % 2),
                                        AF[0:128, self.sml + 48 + qg:self.sml + 48 + qg + 1]).then_inc(s.b_oev, 1)

        v.wait_ge(s.b_pl, 16)
        for half in range(2):
            for mm in range(4):
                m = 4 * half + mm
                for st in range(4):
                    idx = 16 * half + 4 * mm + st
                    v.wait_ge(s.b_pjm, idx + 1)
                    xv = pr.xTb[0:128, 2048 * m + 512 * st:2048 * m + 512 * st + 512]
                    v.scalar_tensor_tensor(xv, bank(ps, idx % 4),
                                           AB[0:128, self.prm + m:self.prm + m + 1],
                                           xv, ALU.add, ALU.add).then_inc(s.b_pja, 1)

        self.ln.ev(v, gate=(s.b_pja, 32))

    def es(self, sc):
        pr, s = self.pr, self.pr.sem
        ps, AF = pr.ps, pr.AF
        sc.wait_ge(s.ld, pr.ld_val())
        sc.wait_ge(s.cvals, 1)
        for qg in range(16):
            sc.wait_ge(s.b_mx, qg + 1)
            if qg >= 2:
                sc.wait_ge(s.b_ptr, qg - 1)
            sc.activation(AF[0:128, self.P2 + 2048 * (qg % 2):self.P2 + 2048 * (qg % 2) + 2048],
                          ps[0:128, 0:2048], AFT.Exp,
                          bias=AF[0:128, self.sml + 16 + qg:self.sml + 16 + qg + 1],
                          scale=pr.cval(0.125, 128),
                          accum_out=AF[0:128, self.sml + 32 + qg:self.sml + 32 + qg + 1]).then_inc(s.b_exp, 1)
        self.ln.es(sc)

    def eg(self, g):
        pr, s = self.pr, self.pr.sem
        AB = pr.AB
        if self.is_first:
            pr.emit_init_loads(g)
        g.dma_start(out=AB[0:128, self.prm:self.prm + 8], in_=pr.h_bprj).then_inc(s.b_pl, 16)
        def load_slab(i, src_ap):
            buf = self.slab_buf(i)
            g.dma_start(out=AB[0:128, buf:buf + 4096], in_=src_ap).then_inc(s.b_wl, 16)
        order = []
        order.append(pr.h_wqkvT[:, 1024:1536])
        order.append(pr.h_wqkvT[:, 1536:2048])
        order.append(pr.h_wqkvT[:, 2048:2560])
        order.append(pr.h_wqkvT[:, 2560:3072])
        for qb in range(4):
            order.append(pr.h_wqkvT[:, 0:512])
            order.append(pr.h_wqkvT[:, 512:1024])
        order.append(pr.h_wprjT[:, 0:512])
        order.append(pr.h_wprjT[:, 512:1024])
        for i, src in enumerate(order):
            if i in (2, 3):
                g.wait_ge(s.b_sdone, i - 1)
            elif i >= 4:
                g.wait_ge(s.b_sdone, i)
            load_slab(i, src.rearrange("(c p) f -> p c f", p=128))

    def esy(self, sy):
        pr, s = self.pr, self.pr.sem
        if self.is_last:
            sy.wait_ge(s.b_lap, 32)
            sy.dma_start(out=pr.h_out, in_=pr.xTb[:, :]).then_inc(s.b_out, 16)
            sy.wait_ge(s.b_out, 16)



class L1F:
    def __init__(self, pr: Prog):
        self.pr = pr
        A = Arena(pr.AB)
        self.RING = A.take(40960)
        self.yac = A.take(16384)
        self.gtr = A.take(2048)
        self.gbs = A.take(2048)
        self.ht = A.take(4096)
        self.tu = A.take(1024)

        self.db = A.take(1024)
        self.sel8 = A.take(1024)
        self.wrt = A.take(64)
        self.mbf = self.RING + 0
        self.rbf = self.RING + 2048
        self.tt2 = self.RING + 4096
        self.sq4 = self.RING + 5120
        self.ln = LNBlock(pr, "c_", self.sq4, self.mbf, self.rbf, self.tt2,
                          final=pr.last, out_scr=None)
        assert A.off <= pr.AB.shape[1], A.off

        Fa = Arena(pr.AF)
        if DBG:
            self.d_gtr = pr.nc.dram_tensor("d_gtr", [8, 2048], BF16, kind="ExternalOutput").ap()
            self.d_yac = pr.nc.dram_tensor("d_yac", [128, 16384], BF16, kind="ExternalOutput").ap()
            self.d_gtt = pr.nc.dram_tensor("d_gtt", [128, 128], F32, kind="ExternalOutput").ap()
            self.d_rtl = pr.nc.dram_tensor("d_rtl", [128, 128], F32, kind="ExternalOutput").ap()
            self.d_rte = pr.nc.dram_tensor("d_rte", [128, 128], F32, kind="ExternalOutput").ap()
        self.rtl = Fa.take(128)
        self.rte = Fa.take(128)
        self.sml = Fa.take(96)
        self.m8 = Fa.take(8)
        self.gtt = Fa.take(128)
        self.gbp = Fa.take(128)
        self.ubp = Fa.take(128)
        self.osc = pr.lnva
        self.ln.out_scr = self.osc
        assert Fa.off <= pr.lnva, Fa.off

    def slab(self, k):
        return self.RING + 8192 * (k % 5)

    def et(self, t):
        pr, s = self.pr, self.pr.sem
        ps, AB, AF = pr.ps, pr.AB, pr.AF
        t.wait_ge(s.ld, pr.ld_val())

        for tt in range(16):
            t.wait_ge(s.c_pl, 16 * 5)
            if tt >= 1:
                t.wait_ge(s.c_rcp, tt)
            for e8 in range(8):
                t.matmul(bank(ps, 7, 0, 128, 0, 8),
                         pr.xTb[0:128, 2048 * e8 + 128 * tt:2048 * e8 + 128 * tt + 128],
                         AB[0:128, self.wrt + 8 * e8:self.wrt + 8 * e8 + 8],
                         start=(e8 == 0), stop=(e8 == 7)).then_inc(
                             s.c_rmm, (1 if e8 == 7 else 0))
            j = tt - 1
            if j >= 0:
                t.wait_ge(s.c_rg, j + 1)
                if j >= 2:
                    t.wait_ge(s.c_gte, j - 1)
                t.transpose(bank(ps, 4 + j % 2, 0, 8, 0, 128),
                            AF[0:128, self.gtt + 8 * j:self.gtt + 8 * j + 8],
                            pr.c_id[:, :]).then_inc(s.c_gtm, 1)
        t.wait_ge(s.c_rg, 16)
        t.wait_ge(s.c_gte, 15)
        t.transpose(bank(ps, 5, 0, 8, 0, 128),
                    AF[0:128, self.gtt + 8 * 15:self.gtt + 8 * 15 + 8],
                    pr.c_id[:, :]).then_inc(s.c_gtm, 1)

        def down_group(T, p, fd, sip):
            dslab = self.slab(3 * sip + 2) + 1024 * fd
            for dc in range(4 * p, 4 * p + 4):
                t.matmul(bank(ps, dc % 4),
                         AB[0:128, dslab + 128 * dc:dslab + 128 * dc + 128],
                         AB[0:128, self.ht + 512 * fd:self.ht + 512 * fd + 512],
                         start=(fd == 0 and sip > 0),
                         stop=(fd == 7)).then_inc(s.c_dmf, (1 if dc == 4 * p + 3 else 0))

        def db_group(T, p):
            ntp = T % 4
            for dc in range(4 * p, 4 * p + 4):
                t.matmul(bank(ps, dc % 4),
                         AB[0:8, self.db + 128 * dc:self.db + 128 * dc + 128],
                         AB[0:8, self.gtr + 512 * ntp:self.gtr + 512 * ntp + 512],
                         start=True, stop=False)

        def sweep(Tp):
            sip = Tp // 4
            t.wait_ge(s.c_ya, 8 * Tp + 4)
            if sip == 0:
                db_group(Tp, 1)
            for fd in range(8):
                down_group(Tp, 1, fd, sip)

        for e in range(8):
            t.wait_ge(s.c_gte, 16)
            if e >= 1:
                t.wait_ge(s.c_tu, 64 * e)
            for nt in range(4):
                if nt >= 2 or e >= 1:
                    t.wait_ge(s.c_gbe, 4 * e + nt - 1)
                t.matmul(bank(ps, 6 + nt % 2),
                         AB[0:8, self.sel8 + 128 * e:self.sel8 + 128 * e + 128],
                         AB[0:8, self.gtr + 512 * nt:self.gtr + 512 * nt + 512],
                         start=True, stop=True).then_inc(s.c_gbm, 1)
            for fch in range(2):
                si = 2 * e + fch
                t.wait_ge(s.c_wl, 32 * si + 32)
                for nt in range(4):
                    T = 4 * si + nt
                    for fc in range(10):
                        if fc < 8:
                            cnt = 8 * T + fc
                            if cnt >= 2:
                                t.wait_ge(s.c_sil, cnt - 1)
                            if fch == 0 and nt == 0 and fc == 0:
                                t.wait_ge(s.c_gbe, 4 * (e + 1))
                            wslab = self.slab(3 * si) + 128 * fc
                            uslab = self.slab(3 * si + 1) + 128 * fc
                            for dc in range(8):
                                t.matmul(bank(ps, 4 + fc % 2),
                                         AB[0:128, wslab + 1024 * dc:wslab + 1024 * dc + 128],
                                         pr.xTb[0:128, 2048 * dc + 512 * nt:2048 * dc + 512 * nt + 512],
                                         start=(dc == 0), stop=(dc == 7)).then_inc(
                                             s.c_gmm, (1 if dc == 7 else 0))
                            if cnt >= 2:
                                t.wait_ge(s.c_tu, cnt - 1)
                            for dc in range(8):
                                t.matmul(bank(ps, 6 + fc % 2),
                                         AB[0:128, uslab + 1024 * dc:uslab + 1024 * dc + 128],
                                         pr.xTb[0:128, 2048 * dc + 512 * nt:2048 * dc + 512 * nt + 512],
                                         start=(dc == 0), stop=(dc == 7)).then_inc(
                                             s.c_umm, (1 if dc == 7 else 0))
                        if fc == 1 and T >= 1:
                            sweep(T - 1)
                        fd = fc - 2
                        if fd >= 0:
                            t.wait_ge(s.c_h, 8 * T + fd + 1)
                            if fd == 0:
                                t.wait_ge(s.c_ya, 8 * T)
                                if nt == 0:
                                    t.wait_ge(s.c_wdl, 16 * si + 16)
                                if si == 0:
                                    db_group(T, 0)
                            down_group(T, 0, fd, si)
        sweep(63)

        self.ln.et(t, gate=(s.c_rs, 8))

    def ev(self, v):
        pr, s = self.pr, self.pr.sem
        ps, AB, AF = pr.ps, pr.AB, pr.AF
        v.wait_ge(s.ld, pr.ld_val())
        if self.is_first:
            pr.emit_cval_sets(v)

        self.vsc = 0
        m8 = AF[0:128, self.m8:self.m8 + 8]
        for tt in range(16):
            mx = AF[0:128, self.sml + tt:self.sml + tt + 1]
            mxn = AF[0:128, self.sml + 16 + tt:self.sml + 16 + tt + 1]
            Z = AF[0:128, self.sml + 32 + tt:self.sml + 32 + tt + 1]
            thr = AF[0:128, self.sml + 48 + tt:self.sml + 48 + tt + 1]
            den = AF[0:128, self.sml + 64 + tt:self.sml + 64 + tt + 1]
            rden = AF[0:128, self.sml + 80 + tt:self.sml + 80 + tt + 1]
            gtt = AF[0:128, self.gtt + 8 * tt:self.gtt + 8 * tt + 8]
            rt = AF[0:128, self.rtl + 8 * tt:self.rtl + 8 * tt + 8]
            def vs(ins):
                self.vsc += 1
                ins.then_inc(s.c_vsy, 1)
                v.wait_ge(s.c_vsy, self.vsc)
            v.wait_ge(s.c_rmm, tt + 1)
            v.tensor_copy(rt, bank(ps, 7, 0, 128, 0, 8)).then_inc(s.c_rcp, 1)
            v.wait_ge(s.c_rcp, tt + 1)
            vs(v.reduce_max(mx, rt, axis=AX.X))
            v.tensor_scalar_mul(mxn, mx, -1.0).then_inc(s.c_rmx, 1)
            rte = AF[0:128, self.rte + 8 * tt:self.rte + 8 * tt + 8]
            v.wait_ge(s.c_rex, tt + 1)
            vs(v.max(out=m8, in_=rte))
            vs(v.tensor_copy(thr, AF[0:128, self.m8 + 1:self.m8 + 2]))
            vs(v.tensor_single_scalar(gtt, rte, thr, ALU.is_ge))
            vs(v.tensor_mul(gtt, gtt, rte))
            vs(v.reduce_sum(den, gtt, axis=AX.X))
            vs(v.tensor_scalar_mul(Z, Z, 1e-9))
            vs(v.tensor_add(den, den, Z))
            vs(v.reciprocal(rden, den))
            v.tensor_scalar_mul(gtt, gtt, rden).then_inc(s.c_rg, 1)
            j = tt - 1
            if j >= 0:
                v.wait_ge(s.c_gtm, j + 1)
                v.tensor_copy(AB[0:8, self.gtr + 128 * j:self.gtr + 128 * j + 128],
                              bank(ps, 4 + j % 2, 0, 8, 0, 128)).then_inc(s.c_gte, 1)
        v.wait_ge(s.c_gtm, 16)
        v.tensor_copy(AB[0:8, self.gtr + 128 * 15:self.gtr + 128 * 15 + 128],
                      bank(ps, 5, 0, 8, 0, 128)).then_inc(s.c_gte, 1)

        def y_evac(T, p):
            sip, ntp = divmod(T, 4)
            for dc in range(4 * p, 4 * p + 4):
                yv = AB[0:128, self.yac + 2048 * dc + 512 * ntp:
                        self.yac + 2048 * dc + 512 * ntp + 512]
                pv = bank(ps, dc % 4)
                if sip == 0:
                    v.tensor_copy(yv, pv).then_inc(s.c_ya, 1)
                else:
                    v.tensor_add(yv, yv, pv).then_inc(s.c_ya, 1)

        for e in range(8):
            for fch in range(2):
                si = 2 * e + fch
                for nt in range(4):
                    T = 4 * si + nt
                    v.wait_ge(s.c_gbe, 4 * e + nt + 1)
                    for fc in range(8):
                        cnt = 8 * T + fc
                        v.wait_ge(s.c_umm, cnt + 1)
                        tuv = AB[0:128, self.tu + 512 * (fc % 2):self.tu + 512 * (fc % 2) + 512]
                        v.scalar_tensor_tensor(tuv, bank(ps, 6 + fc % 2),
                                               AF[0:128, self.ubp + 16 * e + 8 * fch + fc:
                                                  self.ubp + 16 * e + 8 * fch + fc + 1],
                                               AB[0:128, self.gbs + 512 * nt:self.gbs + 512 * nt + 512],
                                               ALU.add, ALU.mult).then_inc(s.c_tu, 1)
                        v.wait_ge(s.c_tu, cnt + 1)
                        v.wait_ge(s.c_sil, cnt + 1)
                        htv = AB[0:128, self.ht + 512 * fc:self.ht + 512 * fc + 512]
                        v.tensor_mul(htv, htv, tuv).then_inc(s.c_h, 1)
                        if fc == 1 and T >= 1:
                            v.wait_ge(s.c_dmf, 16 * T)
                            y_evac(T - 1, 1)
                    v.wait_ge(s.c_dmf, 16 * T + 8)
                    y_evac(T, 0)
        v.wait_ge(s.c_dmf, 1024)
        y_evac(63, 1)

        for dc in range(8):
            v.wait_ge(s.c_ya, 512)
            v.tensor_add(pr.xTb[0:128, 2048 * dc:2048 * dc + 2048],
                         pr.xTb[0:128, 2048 * dc:2048 * dc + 2048],
                         AB[0:128, self.yac + 2048 * dc:self.yac + 2048 * dc + 2048]).then_inc(s.c_rs, 1)

        self.ln.ev(v, gate=(s.c_rs, 8))

    def es(self, sc):
        pr, s = self.pr, self.pr.sem
        ps, AB, AF = pr.ps, pr.AB, pr.AF
        sc.wait_ge(s.ld, pr.ld_val())
        sc.wait_ge(s.cvals, 1)
        for tt in range(16):
            rt = AF[0:128, self.rtl + 8 * tt:self.rtl + 8 * tt + 8]
            sc.wait_ge(s.c_rmx, tt + 1)
            sc.activation(AF[0:128, self.rte + 8 * tt:self.rte + 8 * tt + 8], rt, AFT.Exp,
                          bias=AF[0:128, self.sml + 16 + tt:self.sml + 16 + tt + 1],
                          accum_out=AF[0:128, self.sml + 32 + tt:self.sml + 32 + tt + 1]).then_inc(s.c_rex, 1)
        for e in range(8):
            for nt in range(4):
                sc.wait_ge(s.c_gbm, 4 * e + nt + 1)
                sc.copy(AB[0:128, self.gbs + 512 * nt:self.gbs + 512 * nt + 512],
                        bank(ps, 6 + nt % 2)).then_inc(s.c_gbe, 1)
            for fch in range(2):
                si = 2 * e + fch
                for nt in range(4):
                    T = 4 * si + nt
                    for fc in range(8):
                        cnt = 8 * T + fc
                        sc.wait_ge(s.c_gmm, cnt + 1)
                        if T >= 1:
                            sc.wait_ge(s.c_dmf, 16 * T + fc - 7)
                        sc.activation(AB[0:128, self.ht + 512 * fc:self.ht + 512 * fc + 512],
                                      bank(ps, 4 + fc % 2), AFT.Silu,
                                      bias=AF[0:128, self.gbp + 16 * e + 8 * fch + fc:
                                              self.gbp + 16 * e + 8 * fch + fc + 1]).then_inc(s.c_sil, 1)
        self.ln.es(sc)

    def eg(self, g):
        pr, s = self.pr, self.pr.sem
        AB = pr.AB
        if self.is_first:
            pr.emit_init_loads(g)
        g.dma_start(out=AB[0:128, self.wrt:self.wrt + 64],
                    in_=pr.h_wrtT.rearrange("(c p) e -> p c e", p=128)).then_inc(s.c_pl, 16)
        g.dma_start(out=pr.AF[0:128, self.gbp:self.gbp + 128], in_=pr.h_gb).then_inc(s.c_pl, 16)
        g.dma_start(out=pr.AF[0:128, self.ubp:self.ubp + 128], in_=pr.h_ub).then_inc(s.c_pl, 16)
        g.dma_start(out=AB[0:8, self.db:self.db + 1024], in_=pr.h_db).then_inc(s.c_pl, 16)
        g.dma_start(out=AB[0:8, self.sel8:self.sel8 + 1024],
                    in_=pr.h_sel8.rearrange("e k p -> k e p")).then_inc(s.c_pl, 16)
        for si in range(16):
            e, fch = divmod(si, 2)
            if si >= 2:
                g.wait_ge(s.c_umm, 32 * (si - 1))
            g.dma_start(out=AB[0:128, self.slab(3 * si):self.slab(3 * si) + 8192],
                        in_=pr.h_wg[1024 * e:1024 * e + 1024, 1024 * fch:1024 * fch + 1024]
                        .rearrange("(dc p) c -> p dc c", p=128)).then_inc(s.c_wl, 16)
            if si >= 2:
                g.wait_ge(s.c_dmf, 64 * (si - 1))
            g.dma_start(out=AB[0:128, self.slab(3 * si + 1):self.slab(3 * si + 1) + 8192],
                        in_=pr.h_wu[1024 * e:1024 * e + 1024, 1024 * fch:1024 * fch + 1024]
                        .rearrange("(dc p) c -> p dc c", p=128)).then_inc(s.c_wl, 16)
            if si >= 1:
                g.wait_ge(s.c_gmm, 32 * si)
            g.dma_start(out=AB[0:128, self.slab(3 * si + 2):self.slab(3 * si + 2) + 8192],
                        in_=pr.h_wd[2048 * e + 1024 * fch:2048 * e + 1024 * fch + 1024, :]
                        .rearrange("(fc p) c -> p fc c", p=128)).then_inc(s.c_wdl, 16)

    def esy(self, sy):
        pr, s = self.pr, self.pr.sem
        if DBG:
            sy.wait_ge(s.c_rs, 8)
            sy.dma_start(out=self.d_gtr, in_=pr.AB[0:8, self.gtr:self.gtr + 2048]).then_inc(s.c_dbg, 16)
            sy.dma_start(out=self.d_yac, in_=pr.AB[0:128, self.yac:self.yac + 16384]).then_inc(s.c_dbg, 16)
            sy.dma_start(out=self.d_gtt, in_=pr.AF[0:128, self.gtt:self.gtt + 128]).then_inc(s.c_dbg, 16)
            sy.dma_start(out=self.d_rtl, in_=pr.AF[0:128, self.rtl:self.rtl + 128]).then_inc(s.c_dbg, 16)
            sy.dma_start(out=self.d_rte, in_=pr.AF[0:128, self.rte:self.rte + 128]).then_inc(s.c_dbg, 16)
            sy.wait_ge(s.c_dbg, 80)
        if pr.last:
            for c in range(8):
                sy.wait_ge(s.c_lap, 4 * (c + 1))
                sy.dma_start(out=pr.h_out[128 * c:128 * c + 128, :],
                             in_=pr.AF[0:128, self.osc + 2048 * (c % 2):self.osc + 2048 * (c % 2) + 2048]).then_inc(s.c_ost, 16)
            sy.wait_ge(s.c_ost, 128)
        else:
            sy.wait_ge(s.c_lap, 32)
            sy.dma_start(out=pr.h_out, in_=pr.xTb[:, :]).then_inc(s.c_out, 16)
            sy.wait_ge(s.c_out, 16)



def kernel(**inputs):
    d = _prep(inputs)
    xs = d["xT"]
    cur = None
    out = None
    for gi, group in enumerate(PHASE_GROUPS()):
        pr = Prog(group)
        nc = pr.build()
        in_maps = []
        for c in range(8):
            m = {}
            for name in pr.inputs:
                if name == "xT":
                    m[name] = xs[c]
                elif name == "xh":
                    m[name] = cur[c]
                else:
                    m[name] = d[name]
            in_maps.append(m)
        res = run_bass_kernel_spmd(nc, in_maps, core_ids=list(range(8)))
        if group[-1] == "l1f":
            out = [res.results[c]["out"] for c in range(8)]
        else:
            cur = [res.results[c]["xho"] for c in range(8)]
    return np.stack([o.T for o in out], axis=0)


def PHASE_GROUPS():
    return GROUPS


# version 12
# speedup vs baseline: 1.0735x; 1.0735x over previous
import numpy as np
import ml_dtypes
import concourse.bass as bass
import concourse.mybir as mybir
from concourse.bass_utils import run_bass_kernel_spmd

F32 = mybir.dt.float32
BF16 = mybir.dt.bfloat16
AFT = mybir.ActivationFunctionType
ALU = mybir.AluOpType
AX = mybir.AxisListType

S = 2048
E = 1024
NHEAD = 16
HD = 64
NCHUNK = 8
NQT = 4
BF = ml_dtypes.bfloat16

GROUPS = [["l0a", "l0f", "l1a", "l1f"]]
DBG = False



def _prep(inp):
    d = {}
    x = np.asarray(inp["L0_attn_x"])
    d["xT"] = [np.ascontiguousarray(x[c].T) for c in range(8)]

    d["w0qT"] = np.ascontiguousarray(np.asarray(inp["L0_attn_Wq"]).T)
    d["w0kT"] = np.ascontiguousarray(np.asarray(inp["L0_attn_Wk"]).T)
    d["w0vT"] = np.ascontiguousarray(np.asarray(inp["L0_attn_Wv"]).T)
    d["g_rep"] = np.full((128, 1), np.float32(np.asarray(inp["L0_attn_g_scale"])), np.float32)

    d["w0f"] = np.ascontiguousarray(np.asarray(inp["L0_ffn_weight"]))
    d["bias0"] = np.asarray(inp["L0_ffn_bias"]).reshape(8, 128).T.copy()
    for br, tag in (("so", "so"), ("si", "si")):
        cw = np.asarray(inp[f"L0_ffn_{br}_conv_w"])
        bd = np.zeros((8, 128, 128), np.float32)
        for c in range(8):
            for gl in range(32):
                g = 32 * c + gl
                col = 32 * (c % 4) + gl
                bd[c, 4 * gl:4 * gl + 4, col] = cw[g]
        d[f"bd_{tag}"] = bd
        d[f"cb_{tag}"] = np.asarray(inp[f"L0_ffn_{br}_conv_b"]).reshape(2, 128).T.copy()
        d[f"gam_{tag}"] = np.asarray(inp[f"L0_ffn_{br}_gamma"]).reshape(2, 128).T.copy()
        d[f"bet_{tag}"] = np.asarray(inp[f"L0_ffn_{br}_beta"]).reshape(2, 128).T.copy()
        d[f"linT_{tag}"] = np.ascontiguousarray(np.asarray(inp[f"L0_ffn_{br}_lin_w"]).T)
        d[f"linb_{tag}"] = np.asarray(inp[f"L0_ffn_{br}_lin_b"]).reshape(1, 1024).copy()

    d["wqkvT"] = np.ascontiguousarray(np.asarray(inp["L1_attn_w_qkv"]).T)
    d["wprjT"] = np.ascontiguousarray(np.asarray(inp["L1_attn_w_proj"]).T)
    d["bprj"] = np.asarray(inp["L1_attn_b_proj"]).reshape(8, 128).T.copy()

    d["wrtT"] = np.ascontiguousarray(np.asarray(inp["L1_ffn_router_w"]).T)
    d["wg"] = np.ascontiguousarray(np.asarray(inp["L1_ffn_gate_w"]).reshape(8192, 2048))
    d["wu"] = np.ascontiguousarray(np.asarray(inp["L1_ffn_up_w"]).reshape(8192, 2048))
    d["wd"] = np.ascontiguousarray(np.asarray(inp["L1_ffn_down_w"]).reshape(16384, 1024))
    d["gb"] = np.asarray(inp["L1_ffn_gate_b"]).reshape(8, 16, 128).transpose(2, 0, 1).reshape(128, 128).copy()
    d["ub"] = np.asarray(inp["L1_ffn_up_b"]).reshape(8, 16, 128).transpose(2, 0, 1).reshape(128, 128).copy()
    d["db"] = np.ascontiguousarray(np.asarray(inp["L1_ffn_down_b"]))

    d["c_id"] = np.eye(128, dtype=np.float32)
    d["c_ones_row"] = np.ones((1, 128), np.float32)
    d["c_ones_col"] = np.ones((128, 1), np.float32)
    blk16 = np.zeros((8, 128, 16), np.float32)
    sel16 = np.zeros((8, 16, 128), np.float32)
    for m in range(8):
        blk16[m, 0:64, 2 * m] = 1.0
        blk16[m, 64:128, 2 * m + 1] = 1.0
        sel16[m, 2 * m, 0:64] = 1.0
        sel16[m, 2 * m + 1, 64:128] = 1.0
    d["c_blk16"] = blk16
    d["c_sel16"] = sel16
    sel8 = np.zeros((8, 8, 128), np.float32)
    for e in range(8):
        sel8[e, e, :] = 1.0
    d["c_sel8"] = sel8
    return d



class Sems:
    SHARED = ("ld", "cvals", "bar")

    def __init__(self, nc, pool_n=48):
        self.nc = nc
        self.m = {}
        self.pool = [nc.alloc_semaphore(f"pool{i}") for i in range(pool_n)]
        self.phase_maps = {}
        self.phase_map = {}
        self.pi_box = [0]

    def set_phase(self, k):
        if k not in self.phase_maps:
            self.phase_maps[k] = ({}, [0])
        self.phase_map, box = self.phase_maps[k]
        self.pi_box = box

    def next_phase(self):
        self.set_phase(len(self.phase_maps))

    def __getattr__(self, name):
        if name in ("nc", "m", "pool", "phase_map", "phase_maps", "pi", "pi_box"):
            raise AttributeError(name)
        if name in self.SHARED:
            if name not in self.m:
                self.m[name] = self.nc.alloc_semaphore(name)
            return self.m[name]
        if name not in self.phase_map:
            assert self.pi_box[0] < len(self.pool), "sem pool exhausted"
            self.phase_map[name] = self.pool[self.pi_box[0]]
            self.pi_box[0] += 1
        return self.phase_map[name]


class Arena:
    def __init__(self, t):
        self.t = t
        self.off = 0
        self.marks = []

    def take(self, cols):
        o = self.off
        self.off += cols
        assert self.off <= self.t.shape[1], f"arena overflow {self.off} > {self.t.shape[1]}"
        return o

    def push(self):
        self.marks.append(self.off)

    def pop(self):
        self.off = self.marks.pop()


class _MockInst:
    def __init__(self, tot):
        self.tot = tot

    def then_inc(self, sem, n=1, **kw):
        self.tot[sem.num] = self.tot.get(sem.num, 0) + n
        return self

    def annotate(self, *_):
        return self


class _MockEng:
    def __init__(self, tot):
        self.tot = tot

    def wait_ge(self, sem, val):
        pass

    def sem_inc(self, sem, n=1):
        self.tot[sem.num] = self.tot.get(sem.num, 0) + n
        return _MockInst(self.tot)

    def __getattr__(self, op):
        if op.startswith("_"):
            raise AttributeError(op)
        def f(*a, **kw):
            return _MockInst(self.tot)
        return f


class _EngProxy:
    def __init__(self, eng, base):
        object.__setattr__(self, "_e", eng)
        object.__setattr__(self, "_b", base)

    def wait_ge(self, sem, val):
        return self._e.wait_ge(sem, val + self._b.get(sem.num, 0))

    def __getattr__(self, n):
        return getattr(self._e, n)


def bank(ps, i, p0=0, p1=128, c0=0, c1=512):
    return ps[p0:p1, 512 * i + c0:512 * i + c1]




class LNBlock:

    def __init__(self, pr, pfx, sq4, mbf, rbf, tt2, final=False, out_scr=None):
        self.pr, self.pfx = pr, pfx
        self.sq4, self.mbf, self.rbf, self.tt2 = sq4, mbf, rbf, tt2
        self.final = final
        self.out_scr = out_scr

    def sm(self, name):
        return getattr(self.pr.sem, self.pfx + name)

    def et(self, t, gate=None):
        pr = self.pr
        ps, AB = pr.ps, pr.AB
        if gate is not None:
            t.wait_ge(*gate)
        for c in range(8):
            for qt in range(4):
                ins = t.matmul(bank(ps, qt, 0, 1), pr.c_oc_bf[:, :],
                               pr.xTb[0:128, 2048 * c + 512 * qt:2048 * c + 512 * qt + 512],
                               start=(c == 0), stop=(c == 7))
                if c == 7:
                    ins.then_inc(self.sm("stm"), 1)
                idx = c * 4 + qt
                t.wait_ge(self.sm("lsq"), idx + 1)
                t.matmul(bank(ps, 4 + qt, 0, 1), pr.c_oc_bf[:, :],
                         AB[0:128, self.sq4 + 512 * (idx % 4):self.sq4 + 512 * (idx % 4) + 512],
                         start=(c == 0), stop=(c == 7)).then_inc(self.sm("lnmc"), 1)
        if not self.final:
            for qt in range(4):
                t.wait_ge(self.sm("lnv"), 1)
                if qt >= 2:
                    t.wait_ge(self.sm("lap"), 8 * (qt - 1))
                t.matmul(bank(ps, qt % 2), pr.c_or_bf[:, :],
                         AB[0:1, self.mbf + 512 * qt:self.mbf + 512 * qt + 512],
                         start=True, stop=True)
                t.matmul(bank(ps, 2 + qt % 2), pr.c_or_bf[:, :],
                         AB[0:1, self.rbf + 512 * qt:self.rbf + 512 * qt + 512],
                         start=True, stop=True).then_inc(self.sm("lbc"), 1)
        else:
            t.wait_ge(self.sm("lnv"), 1)
            for qt in range(4):
                t.matmul(bank(ps, qt), pr.c_or_bf[:, :],
                         AB[0:1, self.mbf + 512 * qt:self.mbf + 512 * qt + 512],
                         start=True, stop=True)
                t.matmul(bank(ps, 4 + qt), pr.c_or_bf[:, :],
                         AB[0:1, self.rbf + 512 * qt:self.rbf + 512 * qt + 512],
                         start=True, stop=True).then_inc(self.sm("lbc"), 1)

    def ev(self, v, gate=None):
        pr = self.pr
        ps, AB, AF = pr.ps, pr.AB, pr.AF
        for idx in range(32):
            c, qt = divmod(idx, 4)
            if idx == 0 and gate is not None:
                v.wait_ge(*gate)
            if idx >= 4:
                v.wait_ge(self.sm("lnmc"), idx - 3)
            v.tensor_mul(AB[0:128, self.sq4 + 512 * (idx % 4):self.sq4 + 512 * (idx % 4) + 512],
                         pr.xTb[0:128, 2048 * c + 512 * qt:2048 * c + 512 * qt + 512],
                         pr.xTb[0:128, 2048 * c + 512 * qt:2048 * c + 512 * qt + 512]).then_inc(self.sm("lsq"), 1)
        v.wait_ge(self.sm("lnmc"), 32)
        v.wait_ge(self.sm("stm"), 4)
        A_ = AF[0:1, pr.lnva:pr.lnva + 2048]
        B_ = AF[0:1, pr.lnvb:pr.lnvb + 2048]
        v.tensor_scalar_mul(A_, pr.ps[0:1, 0:2048], 1.0 / 1024.0)
        v.tensor_copy(AB[0:1, self.mbf:self.mbf + 2048], A_)
        v.tensor_mul(A_, A_, A_)
        v.tensor_scalar_mul(B_, pr.ps[0:1, 2048:4096], 1.0 / 1024.0)
        v.tensor_sub(B_, B_, A_).then_inc(self.sm("var"), 1)
        v.wait_ge(self.sm("sd"), 1)
        v.reciprocal(B_, A_)
        v.tensor_copy(AB[0:1, self.rbf:self.rbf + 2048], B_).then_inc(self.sm("lnv"), 1)
        if not self.final:
            for qt in range(4):
                v.wait_ge(self.sm("lbc"), qt + 1)
                for c in range(8):
                    tt = self.tt2 + 512 * (c % 2)
                    xv = pr.xTb[0:128, 2048 * c + 512 * qt:2048 * c + 512 * qt + 512]
                    v.tensor_tensor(AB[0:128, tt:tt + 512], xv, bank(ps, qt % 2), ALU.subtract)
                    v.tensor_tensor(xv, AB[0:128, tt:tt + 512], bank(ps, 2 + qt % 2),
                                    ALU.mult).then_inc(self.sm("lap"), 1)
        else:
            v.wait_ge(self.sm("lbc"), 4)
            for c in range(8):
                if c >= 2:
                    v.wait_ge(self.sm("ost"), 16 * (c - 1))
                for qt in range(4):
                    tt = self.tt2 + 512 * (qt % 2)
                    xv = pr.xTb[0:128, 2048 * c + 512 * qt:2048 * c + 512 * qt + 512]
                    v.tensor_tensor(AB[0:128, tt:tt + 512], xv, bank(ps, qt), ALU.subtract)
                    v.tensor_tensor(AF[0:128, self.out_scr + 2048 * (c % 2) + 512 * qt:
                                       self.out_scr + 2048 * (c % 2) + 512 * qt + 512],
                                    AB[0:128, tt:tt + 512], bank(ps, 4 + qt),
                                    ALU.mult).then_inc(self.sm("lap"), 1)

    def es(self, sc):
        pr = self.pr
        AF = pr.AF
        sc.wait_ge(self.sm("var"), 1)
        sc.activation(AF[0:1, pr.lnva:pr.lnva + 2048],
                      AF[0:1, pr.lnvb:pr.lnvb + 2048],
                      AFT.Sqrt, bias=pr.cval(1e-5, 1)).then_inc(self.sm("sd"), 1)



class Prog:
    def __init__(self, phases):
        self.phases = phases
        nc = bass.Bass("TRN2", target_bir_lowering=False, debug=False)
        self.nc = nc
        self.sem = Sems(nc)
        self.first = phases[0] == "l0a"
        self.last = phases[-1] == "l1f"
        self.inputs = []

        def inp(name, shape, dt=F32):
            self.inputs.append(name)
            return nc.dram_tensor(name, list(shape), dt, kind="ExternalInput").ap()

        self.inp = inp
        if self.first:
            self.h_x = inp("xT", (1024, 2048))
        else:
            self.h_x = inp("xh", (128, 16384), BF16)
        if self.last:
            self.h_out = nc.dram_tensor("out", [1024, 2048], F32, kind="ExternalOutput").ap()
        else:
            self.h_out = nc.dram_tensor("xho", [128, 16384], BF16, kind="ExternalOutput").ap()

        self.h_id = inp("c_id", (128, 128))
        self.h_ones_row = inp("c_ones_row", (1, 128))
        self.h_ones_col = inp("c_ones_col", (128, 1))
        self.h_blk16 = inp("c_blk16", (8, 128, 16))
        self.h_sel16 = inp("c_sel16", (8, 16, 128))
        ph = set(phases)
        if "l0a" in ph:
            self.h_w0q = inp("w0qT", (1024, 1024))
            self.h_w0k = inp("w0kT", (1024, 1024))
            self.h_w0v = inp("w0vT", (1024, 1024))
            self.h_grep = inp("g_rep", (128, 1))
        if "l0f" in ph:
            self.h_w0f = inp("w0f", (1024, 1024))
            self.h_bias0 = inp("bias0", (128, 8))
            for tag in ("so", "si"):
                setattr(self, f"h_bd_{tag}", inp(f"bd_{tag}", (8, 128, 128)))
                setattr(self, f"h_cb_{tag}", inp(f"cb_{tag}", (128, 2)))
                setattr(self, f"h_gam_{tag}", inp(f"gam_{tag}", (128, 2)))
                setattr(self, f"h_bet_{tag}", inp(f"bet_{tag}", (128, 2)))
                setattr(self, f"h_linT_{tag}", inp(f"linT_{tag}", (256, 1024)))
                setattr(self, f"h_linb_{tag}", inp(f"linb_{tag}", (1, 1024)))
        if "l1a" in ph:
            self.h_wqkvT = inp("wqkvT", (1024, 3072))
            self.h_wprjT = inp("wprjT", (1024, 1024))
            self.h_bprj = inp("bprj", (128, 8))
        if "l1f" in ph:
            self.h_wrtT = inp("wrtT", (1024, 8))
            self.h_wg = inp("wg", (8192, 2048))
            self.h_wu = inp("wu", (8192, 2048))
            self.h_wd = inp("wd", (16384, 1024))
            self.h_gb = inp("gb", (128, 128))
            self.h_ub = inp("ub", (128, 128))
            self.h_db = inp("db", (8, 1024))
            self.h_sel8 = inp("c_sel8", (8, 8, 128))

        self.xTb = nc.alloc_sbuf_tensor("xTb", [128, 16384], BF16)
        self.c_id = nc.alloc_sbuf_tensor("cid", [128, 128], F32)
        self.c_or_bf = nc.alloc_sbuf_tensor("corb", [1, 128], BF16)
        self.c_oc_bf = nc.alloc_sbuf_tensor("cocb", [128, 1], BF16)
        self.c_blk16 = nc.alloc_sbuf_tensor("cblk", [128, 8 * 16], BF16)
        self.c_sel16 = nc.alloc_sbuf_tensor("csel", [16, 8 * 128], BF16)
        self.AB = nc.alloc_sbuf_tensor("AB", [128, 68672], BF16)
        self.AF = nc.alloc_sbuf_tensor("AF", [128, 8296], F32)
        self.ps = nc.alloc_psum_tensor("PSB", [128, 4096], F32)

        self.lnva = 8296 - 4096
        self.lnvb = 8296 - 2048
        self.CONSTS = [1e-30, 1e-5, 1e-9, 0.125, -0.125]
        self.cval_base = None
        self.cvt = nc.alloc_sbuf_tensor("cvt", [128, len(self.CONSTS)], F32)

        self.n_init_dma = 6

    def build(self):
        nc = self.nc
        emitters = {"l0a": L0A, "l0f": L0F, "l1a": L1A, "l1f": L1F}
        phase_objs = []
        for i, name in enumerate(self.phases):
            P = emitters[name](self)
            P.is_first = (i == 0)
            P.is_last = (i == len(self.phases) - 1)
            phase_objs.append(P)

        bar = self.sem.bar
        nph = len(self.phases)

        totals = []
        for k, P in enumerate(phase_objs):
            self.sem.set_phase(k)
            tot = {}
            me = _MockEng(tot)
            for fn in (P.et, P.ev, P.es, P.eg, P.esy):
                fn(me)
            totals.append(tot)
        pool_nums = {h.num for h in self.sem.pool}
        bases = []
        cum = {}
        for k in range(nph):
            bases.append(dict(cum))
            for num, n in totals[k].items():
                if num in pool_nums:
                    cum[num] = cum.get(num, 0) + n

        for k, (name, P) in enumerate(zip(self.phases, phase_objs)):
            self.sem.set_phase(k)
            with nc.allow_low_precision(reason="bf16 stream/operands by design"), \
                 nc.Block(f"ph_{name}") as blk:
                def mk(fn, k=k):
                    def run(eng):
                        pe = _EngProxy(eng, bases[k])
                        if k > 0:
                            eng.wait_ge(bar, 5 * k)
                        fn(pe)
                        eng.sem_inc(bar, 1)
                    return run
                blk.tensor(mk(P.et))
                blk.vector(mk(P.ev))
                blk.scalar(mk(P.es))
                blk.gpsimd(mk(P.eg))
                blk.sync(mk(P.esy))
        return nc

    def emit_init_loads(self, g):
        s = self.sem
        n0 = 6
        g.dma_start(out=self.c_id[:, :], in_=self.h_id).then_inc(s.ld, 16)
        g.dma_start(out=self.c_or_bf[:, :], in_=self.h_ones_row).then_inc(s.ld, 16)
        g.dma_start(out=self.c_oc_bf[:, :], in_=self.h_ones_col).then_inc(s.ld, 16)
        g.dma_start(out=self.c_blk16[:, :],
                    in_=self.h_blk16.rearrange("m p k -> p m k")).then_inc(s.ld, 16)
        g.dma_start(out=self.c_sel16[:, :],
                    in_=self.h_sel16.rearrange("m q k -> q m k")).then_inc(s.ld, 16)
        if self.first:
            g.dma_start(out=self.xTb[:, :],
                        in_=self.h_x.rearrange("(c p) t -> p c t", p=128)).then_inc(s.ld, 16)
        else:
            g.dma_start(out=self.xTb[:, :], in_=self.h_x).then_inc(s.ld, 16)
        assert self.n_init_dma == n0

    def ld_val(self):
        return 16 * self.n_init_dma

    def emit_cval_sets(self, v):
        for j, val in enumerate(self.CONSTS):
            ins = v.memset(self.cvt[:, j:j + 1], val)
        ins.then_inc(self.sem.cvals, 1)

    def cval(self, val, p=128):
        j = self.CONSTS.index(val)
        return self.cvt[0:p, j:j + 1]



class L0A:
    def __init__(self, pr: Prog):
        self.pr = pr
        nc = pr.nc
        A = Arena(pr.AB)
        self.WT = A.take(8192)
        self.qhT = A.take(16384)
        self.khT = A.take(16384)
        self.vaug = A.take(16640)
        self.PT = self.WT
        self.sq4 = A.take(2048)
        self.rstd2 = A.take(1024)
        self.rec2 = A.take(1024)
        self.rbsb2 = A.take(1024)
        self.t1b2 = A.take(1024)
        self.mbf = self.rstd2
        self.rbf = self.rbsb2
        self.tt2 = self.vaug
        self.ln = LNBlock(pr, "a_", self.sq4, self.mbf, self.rbf, self.tt2)
        assert A.off <= pr.AB.shape[1]

        Fa = Arena(pr.AF)
        self.vtmp = Fa.take(1024)
        self.ns2 = Fa.take(1024)
        self.grep = Fa.take(1)
        self.nabsg = Fa.take(1)
        self.lnva = Fa.take(2048)
        self.lnvb = Fa.take(2048)
        self.Fa = Fa

    def et(self, t):
        pr, s = self.pr, self.pr.sem
        nc = pr.nc
        ps = pr.ps
        AB, AF = pr.AB, pr.AF
        t.wait_ge(s.ld, pr.ld_val())

        def main_mm(a, w, qt, m):
            prior = a - 4
            if prior >= 0:
                if prior < 64:
                    t.wait_ge(s.a_ev, prior + 1)
                else:
                    t.wait_ge(s.a_vtmp, prior - 64 + 1)
            t.wait_ge(s.a_wl, 16 * (w + 1))
            for e in range(8):
                ins = t.matmul(bank(ps, a % 4),
                               AB[0:128, self.WT + 1024 * e + 128 * m:self.WT + 1024 * e + 128 * m + 128],
                               pr.xTb[0:128, 2048 * e + 512 * qt:2048 * e + 512 * qt + 512],
                               start=(e == 0), stop=(e == 7))
            ins.then_inc(s.a_mm, 1)

        def nmm(b, G, m):
            if m == 0 and G >= 2:
                t.wait_ge(s.a_sqrt, G - 1)
            t.wait_ge(s.a_sq, b + 1)
            t.matmul(bank(ps, 4 + G % 2, 0, 16),
                     pr.c_blk16[0:128, 16 * m:16 * m + 16],
                     AB[0:128, self.sq4 + 512 * (b % 4):self.sq4 + 512 * (b % 4) + 512],
                     start=(m == 0), stop=(m == 7)).then_inc(s.a_nmm, 1)

        def bmm(G, m):
            bm = G * 8 + m
            t.wait_ge(s.a_rcp, G + 1)
            if bm >= 2:
                t.wait_ge(s.a_nml, bm - 1)
            t.matmul(bank(ps, 6 + m % 2),
                     pr.c_sel16[0:16, 128 * m:128 * m + 128],
                     AB[0:16, self.rstd2 + 512 * (G % 2):self.rstd2 + 512 * (G % 2) + 512],
                     start=True, stop=True).then_inc(s.a_bmm, 1)

        nqk = 64
        for b in range(nqk + 9):
            if b < nqk:
                w, r = divmod(b, 32)
                qt, m = divmod(r, 8)
                main_mm(b, w, qt, m)
            if 1 <= b + 0 and b - 1 < nqk and b - 1 >= 0:
                G, m = divmod(b - 1, 8)
                nmm(b - 1, G, m)
            if b - 9 >= 0 and b - 9 < nqk:
                G, m = divmod(b - 9, 8)
                bmm(G, m)

        def v_tr(vi, qt, m):
            if vi < 2:
                t.wait_ge(s.a_sqrt, 7 + vi)
            t.wait_ge(s.a_vtmp, vi + 1)
            if vi >= 2:
                t.wait_ge(s.a_vaug, vi - 1)
            for j in range(4):
                ins = t.transpose(bank(ps, 4 + vi % 2, 0, 128, 128 * j, 128 * j + 128),
                                  AF[0:128, self.vtmp + 512 * (vi % 2) + 128 * j:
                                     self.vtmp + 512 * (vi % 2) + 128 * j + 128],
                                  pr.c_id[:, :])
            ins.then_inc(s.a_vtr, 1)

        for vi in range(32 + 2):
            if vi < 32:
                qt, m = divmod(vi, 8)
                main_mm(64 + vi, 2, qt, m)
            if vi - 2 >= 0:
                qt, m = divmod(vi - 2, 8)
                v_tr(vi - 2, qt, m)

        t.wait_ge(s.a_nml, 64)
        t.wait_ge(s.a_vaug, 32)

        def sc(j, ktg):
            h, qt = divmod(j, 4)
            p, hb = h // 2, 64 * (h % 2)
            g = 4 * j + ktg
            if g >= 1:
                t.wait_ge(s.a_exp, g)
            for kseq in range(4):
                kt = 4 * ktg + kseq
                ins = t.matmul(bank(ps, kseq),
                               AB[hb:hb + 64, self.khT + 2048 * p + 128 * kt:
                                  self.khT + 2048 * p + 128 * kt + 128],
                               AB[hb:hb + 64, self.qhT + 2048 * p + 512 * qt:
                                  self.qhT + 2048 * p + 512 * qt + 512],
                               start=True, stop=True, tile_position=(hb, 0))
            ins.then_inc(s.a_smm, 1)

        def av(j, ktg):
            h, qt = divmod(j, 4)
            p, half = h // 2, h % 2
            t.wait_ge(s.a_exp, 4 * j + ktg + 1)
            if ktg == 0 and j >= 2:
                t.wait_ge(s.a_t1, j - 1)
            for kseq in range(4):
                kt = 4 * ktg + kseq
                ins = t.matmul(bank(ps, 4 + 2 * (j % 2), 0, 65),
                               AB[0:128, self.vaug + 2080 * p + 130 * kt + 65 * half:
                                  self.vaug + 2080 * p + 130 * kt + 65 * half + 65],
                               AB[0:128, self.PT + 512 * kt:self.PT + 512 * kt + 512],
                               start=(ktg == 0 and kseq == 0), stop=(ktg == 3 and kseq == 3))
            ins.then_inc(s.a_avm, 1)

        def rb(j):
            t.wait_ge(s.a_rec, j + 1)
            if j >= 2:
                t.wait_ge(s.a_t1, j - 1)
            t.matmul(bank(ps, 5 + 2 * (j % 2), 0, 64),
                     pr.c_or_bf[0:1, 0:64],
                     AB[0:1, self.rec2 + 512 * (j % 2):self.rec2 + 512 * (j % 2) + 512],
                     start=True, stop=True).then_inc(s.a_rbm, 1)

        NT = 64
        for j in range(NT + 1):
            for ktg in range(4):
                if j - 1 >= 0:
                    av(j - 1, ktg)
                if j < NT:
                    sc(j, ktg)
            if j - 1 >= 0:
                rb(j - 1)

        self.ln.et(t, gate=(s.a_add, NT))

    def ev(self, v):
        pr, s = self.pr, self.pr.sem
        ps = pr.ps
        AB, AF = pr.AB, pr.AF
        v.wait_ge(s.ld, pr.ld_val())
        if self.is_first:
            pr.emit_cval_sets(v)
        v.wait_ge(s.a_gabs, 1)
        v.tensor_scalar_mul(AF[0:128, self.nabsg:self.nabsg + 1],
                            AF[0:128, self.nabsg:self.nabsg + 1], -1.0).then_inc(s.a_gneg, 1)

        vv = AB[0:128, self.vaug:self.vaug + 16640].rearrange("P (p k c) -> P p k c", p=8, c=130)
        v.memset(vv[:, :, :, 64:65], 1.0)
        v.memset(vv[:, :, :, 129:130], 1.0).then_inc(s.a_vone, 1)

        def copy_sq(b):
            w, r = divmod(b, 32)
            qt, m = divmod(r, 8)
            dst = (self.qhT if w == 0 else self.khT) + 2048 * m + 512 * qt
            v.wait_ge(s.a_mm, b + 1)
            v.tensor_copy(AB[0:128, dst:dst + 512], bank(ps, b % 4)).then_inc(s.a_ev, 1)
            if b >= 4:
                v.wait_ge(s.a_nmm, b - 3)
            v.tensor_mul(AB[0:128, self.sq4 + 512 * (b % 4):self.sq4 + 512 * (b % 4) + 512],
                         AB[0:128, dst:dst + 512],
                         AB[0:128, dst:dst + 512]).then_inc(s.a_sq, 1)

        def recip(G):
            v.wait_ge(s.a_sqrt, G + 1)
            if G >= 2:
                v.wait_ge(s.a_bmm, 8 * G - 8)
            v.reciprocal(AB[0:16, self.rstd2 + 512 * (G % 2):self.rstd2 + 512 * (G % 2) + 512],
                         AF[0:16, self.ns2 + 512 * (G % 2):self.ns2 + 512 * (G % 2) + 512]).then_inc(s.a_rcp, 1)

        def nmult(bm):
            G, m = divmod(bm, 8)
            w, r = divmod(bm, 32)
            qt = r // 8
            dst = (self.qhT if w == 0 else self.khT) + 2048 * m + 512 * qt
            v.wait_ge(s.a_bmm, bm + 1)
            v.tensor_tensor(AB[0:128, dst:dst + 512],
                            AB[0:128, dst:dst + 512],
                            bank(ps, 6 + m % 2), ALU.mult).then_inc(s.a_nml, 1)

        for b in range(64 + 9):
            if b < 64:
                copy_sq(b)
            if b >= 8 and (b % 8) == 0 and (b // 8) - 1 < 8:
                recip((b // 8) - 1)
            if b - 9 >= 0:
                nmult(b - 9)

        v.wait_ge(s.a_vone, 1)
        for vi in range(32):
            qt, m = divmod(vi, 8)
            v.wait_ge(s.a_vtr, vi + 1)
            for j in range(4):
                kt = 4 * qt + j
                base = self.vaug + 2080 * m + 130 * kt
                v.tensor_copy(AB[0:128, base:base + 64],
                              bank(ps, 4 + vi % 2, 0, 128, 128 * j, 128 * j + 64))
                ins = v.tensor_copy(AB[0:128, base + 65:base + 65 + 64],
                                    bank(ps, 4 + vi % 2, 0, 128, 128 * j + 64, 128 * j + 128))
            ins.then_inc(s.a_vaug, 1)

        def a2v(j):
            h, qt = divmod(j, 4)
            if j >= 2:
                v.wait_ge(s.a_rbm, j - 1)
            v.wait_ge(s.a_avm, 4 * j + 4)
            v.reciprocal(AB[0:1, self.rec2 + 512 * (j % 2):self.rec2 + 512 * (j % 2) + 512],
                         bank(ps, 4 + 2 * (j % 2), 64, 65)).then_inc(s.a_rec, 1)

        def a2t1(j):
            h, qt = divmod(j, 4)
            c, hb = h // 2, 64 * (h % 2)
            v.wait_ge(s.a_rbc, j + 1)
            t1 = AB[hb:hb + 64, self.t1b2 + 512 * (j % 2):self.t1b2 + 512 * (j % 2) + 512]
            v.tensor_tensor(t1,
                            bank(ps, 4 + 2 * (j % 2), 0, 64),
                            AB[0:64, self.rbsb2 + 512 * (j % 2):self.rbsb2 + 512 * (j % 2) + 512],
                            ALU.mult).then_inc(s.a_t1, 1)
            dst = 2048 * c + 512 * qt
            v.tensor_add(pr.xTb[hb:hb + 64, dst:dst + 512],
                         pr.xTb[hb:hb + 64, dst:dst + 512],
                         t1).then_inc(s.a_add, 1)

        for j in range(64 + 1):
            if j < 64:
                a2v(j)
            if j - 1 >= 0:
                a2t1(j - 1)

        self.ln.ev(v, gate=(s.a_add, 64))

    def es(self, sc):
        pr, s = self.pr, self.pr.sem
        ps = pr.ps
        AB, AF = pr.AB, pr.AF
        sc.wait_ge(s.ld, pr.ld_val())
        sc.wait_ge(s.a_pl, 16)
        sc.activation(AF[0:128, self.nabsg:self.nabsg + 1],
                      AF[0:128, self.grep:self.grep + 1], AFT.Abs).then_inc(s.a_gabs, 1)

        sc.wait_ge(s.cvals, 1)

        def sqrt_G(G):
            sc.wait_ge(s.a_nmm, 8 * G + 8)
            if G >= 2:
                sc.wait_ge(s.a_rcp, G - 1)
            sc.activation(AF[0:16, self.ns2 + 512 * (G % 2):self.ns2 + 512 * (G % 2) + 512],
                          bank(ps, 4 + G % 2, 0, 16), AFT.Sqrt, bias=pr.cval(1e-30, 16)).then_inc(s.a_sqrt, 1)

        for G in range(8):
            sqrt_G(G)

        def v_copy(vi):
            qt, m = divmod(vi, 8)
            a = 64 + vi
            sc.wait_ge(s.a_mm, a + 1)
            if vi >= 2:
                sc.wait_ge(s.a_vtr, vi - 1)
            sc.copy(AF[0:128, self.vtmp + 512 * (vi % 2):self.vtmp + 512 * (vi % 2) + 512],
                    bank(ps, a % 4)).then_inc(s.a_vtmp, 1)

        for vi in range(32):
            v_copy(vi)

        sc.wait_ge(s.a_gneg, 1)

        def expg(j, ktg):
            g = 4 * j + ktg
            sc.wait_ge(s.a_smm, g + 1)
            if j >= 1:
                sc.wait_ge(s.a_avm, 4 * (j - 1) + ktg + 1)
            sc.activation(AB[0:128, self.PT + 2048 * ktg:self.PT + 2048 * ktg + 2048],
                          pr.ps[0:128, 0:2048], AFT.Exp,
                          bias=AF[0:128, self.nabsg:self.nabsg + 1],
                          scale=AF[0:128, self.grep:self.grep + 1]).then_inc(s.a_exp, 1)

        def rbcopy(j):
            sc.wait_ge(s.a_rbm, j + 1)
            sc.copy(AB[0:64, self.rbsb2 + 512 * (j % 2):self.rbsb2 + 512 * (j % 2) + 512],
                    bank(ps, 5 + 2 * (j % 2), 0, 64)).then_inc(s.a_rbc, 1)

        for j in range(64 + 1):
            if j < 64:
                for ktg in range(4):
                    expg(j, ktg)
            if j - 1 >= 0:
                rbcopy(j - 1)

        self.ln.es(sc)

    def eg(self, g):
        pr, s = self.pr, self.pr.sem
        if self.is_first:
            pr.emit_init_loads(g)
        g.dma_start(out=pr.AF[0:128, self.grep:self.grep + 1], in_=pr.h_grep).then_inc(s.a_pl, 16)
        g.dma_start(out=pr.AB[0:128, self.WT:self.WT + 8192],
                    in_=pr.h_w0q.rearrange("(c p) f -> p c f", p=128)).then_inc(s.a_wl, 16)
        g.wait_ge(s.a_mm, 32)
        g.dma_start(out=pr.AB[0:128, self.WT:self.WT + 8192],
                    in_=pr.h_w0k.rearrange("(c p) f -> p c f", p=128)).then_inc(s.a_wl, 16)
        g.wait_ge(s.a_mm, 64)
        g.dma_start(out=pr.AB[0:128, self.WT:self.WT + 8192],
                    in_=pr.h_w0v.rearrange("(c p) f -> p c f", p=128)).then_inc(s.a_wl, 16)

    def esy(self, sy):
        pr, s = self.pr, self.pr.sem
        if self.is_last:
            sy.wait_ge(s.a_lap, 32)
            sy.dma_start(out=pr.h_out, in_=pr.xTb[:, :]).then_inc(s.a_out, 16)
            sy.wait_ge(s.a_out, 16)




class L0F:

    def __init__(self, pr: Prog):
        self.pr = pr
        nc = pr.nc
        A = Arena(pr.AB)
        self.WT = A.take(8192)
        self.linT = A.take(4096)
        self.bd = A.take(2048)
        self.xc = A.take(8192)
        self.xa = A.take(8192)
        self.full = A.take(16384)
        self.xsp = A.take(16384)
        self.sqb = A.take(2048)
        self.linb = self.sqb
        self.prm = A.take(256)
        self.cmb2 = A.take(1024)
        self.mbf = self.xa + 0
        self.rbf = self.xa + 2048
        self.tt2 = self.xa + 4096
        self.sq4 = self.xa + 5120
        self.ln = LNBlock(pr, "f_", self.sq4, self.mbf, self.rbf, self.tt2)
        assert A.off <= pr.AB.shape[1], A.off

        Fa = Arena(pr.AF)
        self.lg = Fa.take(2048)
        self.zap = Fa.take(2048)
        self.sml = Fa.take(64)
        self.parts = Fa.take(8)
        self.allr = Fa.take(8)
        self.bnv = Fa.take(16)
        assert Fa.off <= pr.lnva, Fa.off

        self.ccin = nc.dram_tensor("ccin", [128, 8], F32, kind="Internal")
        self.ccout = nc.dram_tensor("ccout", [128, 8], F32, kind="Internal")
        if DBG:
            self.d_xsp = nc.dram_tensor("d_xsp", [128, 16384], BF16, kind="ExternalOutput").ap()
            self.d_full = nc.dram_tensor("d_full", [128, 16384], BF16, kind="ExternalOutput").ap()
            self.d_xc = nc.dram_tensor("d_xc", [128, 8192], BF16, kind="ExternalOutput").ap()
            self.d_xa = nc.dram_tensor("d_xa", [128, 8192], BF16, kind="ExternalOutput").ap()
            self.d_lg = nc.dram_tensor("d_lg", [128, 2048], F32, kind="ExternalOutput").ap()
            self.d_zap = nc.dram_tensor("d_zap", [128, 2048], F32, kind="ExternalOutput").ap()
            self.d_bnv = nc.dram_tensor("d_bnv", [128, 16], F32, kind="ExternalOutput").ap()
            self.d_sml = nc.dram_tensor("d_sml", [128, 64], F32, kind="ExternalOutput").ap()

        self.vsc = 0

    def smlv(self, off, p=128, w=1):
        return self.pr.AF[0:p, self.sml + off:self.sml + off + w]

    def sem(self, name, br):
        return getattr(self.pr.sem, f"f_{name}{br}")

    def et(self, t):
        pr, s = self.pr, self.pr.sem
        ps, AB = pr.ps, pr.AB
        t.wait_ge(s.ld, pr.ld_val())

        for idx in range(16):
            br, r = divmod(idx, 8)
            ti, qt = divmod(r, 4)
            t.wait_ge(s.f_wsm, 176)
            if idx >= 2:
                t.wait_ge(s.f_cev, idx - 1)
            for cc in range(4):
                c = 4 * ti + cc
                t.matmul(bank(ps, idx % 2),
                         AB[0:128, self.bd + 1024 * br + 128 * c:self.bd + 1024 * br + 128 * c + 128],
                         pr.xTb[0:128, 2048 * c + 512 * qt:2048 * c + 512 * qt + 512],
                         start=(cc == 0), stop=(cc == 3)).then_inc(s.f_cmm, (1 if cc == 3 else 0))

        for idx in range(32):
            m, qt = divmod(idx, 4)
            t.wait_ge(s.f_wl, 16)
            if idx >= 2:
                t.wait_ge(s.f_fme, idx - 1)
            for e in range(8):
                t.matmul(bank(ps, 2 + idx % 2),
                         AB[0:128, self.WT + 1024 * e + 128 * m:self.WT + 1024 * e + 128 * m + 128],
                         pr.xTb[0:128, 2048 * e + 512 * qt:2048 * e + 512 * qt + 512],
                         start=(e == 0), stop=(e == 7)).then_inc(s.f_fmm, (1 if e == 7 else 0))

        def lmm(j):
            nt, br = divmod(j, 2)
            if nt == 0:
                t.wait_ge(s.f_gel, 2 * br + 2)
                t.wait_ge(s.f_lbl, 32)
                if br == 0:
                    t.wait_ge(s.f_cev, 16)
                else:
                    t.wait_ge(s.f_fme, 32)
            else:
                t.wait_ge(self.sem("exp", br), nt)
            for o in range(2):
                bk = bank(ps, 2 * br + o)
                t.matmul(bk, pr.c_or_bf[0:1, 0:128],
                         AB[0:1, self.linb + 1024 * br + 512 * o:
                            self.linb + 1024 * br + 512 * o + 512],
                         start=True, stop=False)
                for ti in range(2):
                    ins = t.matmul(bk,
                                   AB[0:128, self.xa + 4096 * br + 2048 * ti + 128 * nt:
                                      self.xa + 4096 * br + 2048 * ti + 128 * nt + 128],
                                   AB[0:128, self.linT + 2048 * br + 1024 * ti + 512 * o:
                                      self.linT + 2048 * br + 1024 * ti + 512 * o + 512],
                                   start=False, stop=(ti == 1))
            ins.then_inc(self.sem("lmm", br), 1)

        def mtr(j):
            nt, br = divmod(j, 2)
            t.wait_ge(self.sem("zs", br), nt + 1)
            if nt >= 1:
                t.wait_ge(self.sem("mev", br), nt)
            for q in range(8):
                bk = 4 + 2 * br + q // 4
                ins = t.transpose(bank(ps, bk, 0, 128, 128 * (q % 4), 128 * (q % 4) + 128),
                                  pr.AF[0:128, self.zap + 1024 * br + 128 * q:
                                        self.zap + 1024 * br + 128 * q + 128],
                                  pr.c_id[:, :])
            ins.then_inc(self.sem("mtr", br), 1)

        for k in range(34):
            if k < 32:
                lmm(k)
            if 0 <= k - 2 < 32:
                mtr(k - 2)

        for idx in range(32):
            m, qt = divmod(idx, 4)
            if idx == 0:
                t.wait_ge(s.f_mev0, 16)
                t.wait_ge(s.f_mev1, 16)
            if idx >= 4:
                t.wait_ge(s.f_cmb, idx - 3)
            for e in range(8):
                t.matmul(bank(ps, 4 + idx % 4),
                         AB[0:128, self.WT + 1024 * e + 128 * m:self.WT + 1024 * e + 128 * m + 128],
                         AB[0:128, self.xsp + 2048 * e + 512 * qt:self.xsp + 2048 * e + 512 * qt + 512],
                         start=(e == 0), stop=(e == 7)).then_inc(s.f_imm, (1 if e == 7 else 0))

        self.ln.et(t, gate=(s.f_cmb, 32))

    def ev(self, v):
        pr, s = self.pr, self.pr.sem
        ps, AB, AF = pr.ps, pr.AB, pr.AF
        v.wait_ge(s.ld, pr.ld_val())
        if self.is_first:
            pr.emit_cval_sets(v)
        self.vsc = 0

        for br in range(2):
            for ti in range(2):
                v.wait_ge(s.f_cev, 8 * br + 4 * ti + 4)
                xcv = AB[0:128, self.xc + 4096 * br + 2048 * ti:self.xc + 4096 * br + 2048 * ti + 2048]
                v.reduce_sum(AF[0:128, self.parts + 4 * br + ti:self.parts + 4 * br + ti + 1],
                             xcv, axis=AX.X)
                v.tensor_mul(AB[0:128, self.sqb:self.sqb + 2048], xcv, xcv)
                v.reduce_sum(AF[0:128, self.parts + 4 * br + 2 + ti:self.parts + 4 * br + 2 + ti + 1],
                             AB[0:128, self.sqb:self.sqb + 2048], axis=AX.X).then_inc(s.f_bnp, 1)

        v.wait_ge(s.f_ccd, 32)
        cN = 1.0 / 16384.0
        def vsn(ins):
            self.vsc += 1
            ins.then_inc(s.f_vsy, 1)
            return self.vsc
        def vs(ins):
            n = vsn(ins)
            v.wait_ge(s.f_vsy, n)
        for br in range(2):
            b0 = self.bnv + 8 * br
            mean = AF[0:128, b0:b0 + 2]
            var = AF[0:128, b0 + 2:b0 + 4]
            scl = AF[0:128, b0 + 4:b0 + 6]
            shf = AF[0:128, b0 + 6:b0 + 8]
            ar_s = AF[0:128, self.allr + 4 * br:self.allr + 4 * br + 2]
            ar_q = AF[0:128, self.allr + 4 * br + 2:self.allr + 4 * br + 4]
            vs(v.tensor_scalar_mul(mean, ar_s, cN))
            vs(v.tensor_scalar_mul(var, ar_q, cN))
            vs(v.tensor_mul(scl, mean, mean))
            v.tensor_sub(var, var, scl).then_inc(s.f_bvar, 1)
            v.wait_ge(s.f_bsd, br + 1)
            vs(v.reciprocal(scl, var))
            vs(v.tensor_mul(scl, scl, AB[0:128, self.prm + 4 + 8 * br:self.prm + 4 + 8 * br + 2]))
            vs(v.tensor_mul(shf, mean, scl))
            v.tensor_sub(shf, AB[0:128, self.prm + 8 + 8 * br:self.prm + 8 + 8 * br + 2],
                         shf).then_inc(s.f_bn, 1)

        self.r4_cnt = {}

        def Va(nt, br):
            v.wait_ge(self.sem("lmm", br), nt + 1)
            v.reduce_max(self.smlv(0 + br), ps[0:128, 1024 * br:1024 * br + 1024], axis=AX.X)
            v.tensor_scalar_mul(self.smlv(2 + br), self.smlv(0 + br),
                                -1.0).then_inc(self.sem("mx", br), 1)

        def rzR(nt, br):
            v.wait_ge(self.sem("exp", br), nt + 1)
            if nt >= 1:
                v.wait_ge(self.sem("zs", br), nt)
            v.reciprocal(self.smlv(6 + br), self.smlv(4 + br)).then_inc(self.sem("rz", br), 1)

        def Vc(nt, br):
            v.wait_ge(self.sem("mtr", br), nt + 1)
            for q in range(8):
                bk = 4 + 2 * br + q // 4
                bkv = bank(ps, bk, 0, 128, 128 * (q % 4), 128 * (q % 4) + 128)
                if br == 0:
                    ins = v.tensor_tensor(
                        AB[0:128, self.xsp + 2048 * q + 128 * nt:self.xsp + 2048 * q + 128 * nt + 128],
                        bkv, pr.xTb[0:128, 2048 * q + 128 * nt:2048 * q + 128 * nt + 128], ALU.mult)
                else:
                    fv = AB[0:128, self.full + 2048 * q + 128 * nt:self.full + 2048 * q + 128 * nt + 128]
                    ins = v.tensor_tensor(fv, bkv, fv, ALU.mult)
            ins.then_inc(self.sem("mev", br), 1)

        for nt in range(16):
            for br in range(2):
                Va(nt, br)
            if nt >= 1:
                for br in range(2):
                    Vc(nt - 1, br)
            lgv = [AF[0:128, self.lg + 1024 * b:self.lg + 1024 * b + 1024] for b in range(2)]
            zp = [AF[0:128, self.zap + 1024 * b:self.zap + 1024 * b + 1024] for b in range(2)]
            m8 = [self.smlv(8 + 8 * b, 128, 8) for b in range(2)]
            cnt = {}
            for half in range(4):
                for br in range(2):
                    if half == 0:
                        v.wait_ge(self.sem("exp", br), nt + 1)
                        if nt >= 1:
                            v.wait_ge(self.sem("mtr", br), nt)
                        src = lgv[br]
                    else:
                        v.wait_ge(s.f_vsy, cnt[(br, half - 1, 1)])
                        src = zp[br]
                    cnt[(br, half, 0)] = vsn(v.max(out=m8[br], in_=src))
                for br in range(2):
                    v.wait_ge(s.f_vsy, cnt[(br, half, 0)])
                    src = lgv[br] if half == 0 else zp[br]
                    cnt[(br, half, 1)] = vsn(
                        v.match_replace(out=zp[br], in_to_replace=m8[br],
                                        in_values=src, imm_value=0.0))
                    if half == 3:
                        self.r4_cnt[(nt, br)] = cnt[(br, 3, 1)]
            for br in range(2):
                rzR(nt, br)
            for br in range(2):
                lgv = AF[0:128, self.lg + 1024 * br:self.lg + 1024 * br + 1024]
                zp = AF[0:128, self.zap + 1024 * br:self.zap + 1024 * br + 1024]
                v.wait_ge(s.f_vsy, self.r4_cnt[(nt, br)])
                vs(v.tensor_sub(zp, lgv, zp))
                v.tensor_scalar_mul(zp, zp, self.smlv(6 + br)).then_inc(self.sem("zs", br), 1)
        for br in range(2):
            Vc(15, br)

        for idx in range(32):
            m, qt = divmod(idx, 4)
            v.wait_ge(s.f_imm, idx + 1)
            cm = AB[0:128, self.cmb2 + 512 * (idx % 2):self.cmb2 + 512 * (idx % 2) + 512]
            fv = AB[0:128, self.full + 2048 * m + 512 * qt:self.full + 2048 * m + 512 * qt + 512]
            v.scalar_tensor_tensor(cm, bank(ps, 4 + idx % 4),
                                   AB[0:128, self.prm + 16 + m:self.prm + 16 + m + 1],
                                   fv, ALU.add, ALU.add)
            xv = pr.xTb[0:128, 2048 * m + 512 * qt:2048 * m + 512 * qt + 512]
            v.tensor_add(xv, xv, cm).then_inc(s.f_cmb, 1)

        self.ln.ev(v, gate=(s.f_cmb, 32))

    def es(self, sc):
        pr, s = self.pr, self.pr.sem
        ps, AB, AF = pr.ps, pr.AB, pr.AF
        sc.wait_ge(s.ld, pr.ld_val())
        sc.wait_ge(s.cvals, 1)

        for idx in range(16):
            br, r = divmod(idx, 8)
            ti, qt = divmod(r, 4)
            sc.wait_ge(s.f_cmm, idx + 1)
            sc.add(AB[0:128, self.xc + 4096 * br + 2048 * ti + 512 * qt:
                      self.xc + 4096 * br + 2048 * ti + 512 * qt + 512],
                   bank(ps, idx % 2),
                   AB[0:128, self.prm + 0 + 8 * br + ti:self.prm + 0 + 8 * br + ti + 1]).then_inc(s.f_cev, 1)

        for idx in range(32):
            m, qt = divmod(idx, 4)
            sc.wait_ge(s.f_fmm, idx + 1)
            sc.copy(AB[0:128, self.full + 2048 * m + 512 * qt:self.full + 2048 * m + 512 * qt + 512],
                    bank(ps, 2 + idx % 2)).then_inc(s.f_fme, 1)

        for br in range(2):
            b0 = self.bnv + 8 * br
            sc.wait_ge(s.f_bvar, br + 1)
            sc.activation(AF[0:128, b0 + 2:b0 + 4], AF[0:128, b0 + 2:b0 + 4],
                          AFT.Sqrt, bias=pr.cval(1e-5, 128)).then_inc(s.f_bsd, 1)
        for br in range(2):
            sc.wait_ge(s.f_bn, br + 1)
            b0 = self.bnv + 8 * br
            for ti in range(2):
                sc.activation(AB[0:128, self.xa + 4096 * br + 2048 * ti:
                                 self.xa + 4096 * br + 2048 * ti + 2048],
                              AB[0:128, self.xc + 4096 * br + 2048 * ti:
                                 self.xc + 4096 * br + 2048 * ti + 2048],
                              AFT.Gelu,
                              bias=AF[0:128, b0 + 6 + ti:b0 + 6 + ti + 1],
                              scale=AF[0:128, b0 + 4 + ti:b0 + 4 + ti + 1]).then_inc(s.f_gel, 1)

        for nt in range(16):
            for br in range(2):
                sc.wait_ge(self.sem("mx", br), nt + 1)
                if nt >= 1:
                    sc.wait_ge(self.sem("zs", br), nt)
                    sc.wait_ge(self.sem("rz", br), nt)
                lgv = AF[0:128, self.lg + 1024 * br:self.lg + 1024 * br + 1024]
                sc.activation(lgv, ps[0:128, 1024 * br:1024 * br + 1024], AFT.Exp,
                              bias=self.smlv(2 + br),
                              accum_out=self.smlv(4 + br)).then_inc(self.sem("exp", br), 1)

        self.ln.es(sc)

    def eg(self, g):
        pr, s = self.pr, self.pr.sem
        AB, AF = pr.AB, pr.AF
        if self.is_first:
            pr.emit_init_loads(g)
        g.dma_start(out=AB[0:128, self.WT:self.WT + 8192],
                    in_=pr.h_w0f.rearrange("(c p) f -> p c f", p=128)).then_inc(s.f_wl, 16)
        g.dma_start(out=AB[0:128, self.prm + 0:self.prm + 2], in_=pr.h_cb_si).then_inc(s.f_wsm, 16)
        g.dma_start(out=AB[0:128, self.prm + 8:self.prm + 10], in_=pr.h_cb_so).then_inc(s.f_wsm, 16)
        g.dma_start(out=AB[0:128, self.prm + 4:self.prm + 6], in_=pr.h_gam_si).then_inc(s.f_wsm, 16)
        g.dma_start(out=AB[0:128, self.prm + 12:self.prm + 14], in_=pr.h_gam_so).then_inc(s.f_wsm, 16)
        g.dma_start(out=AB[0:128, self.prm + 6:self.prm + 8], in_=pr.h_bet_si).then_inc(s.f_wsm, 16)
        g.dma_start(out=AB[0:128, self.prm + 14:self.prm + 16], in_=pr.h_bet_so).then_inc(s.f_wsm, 16)
        g.dma_start(out=AB[0:128, self.prm + 16:self.prm + 24], in_=pr.h_bias0).then_inc(s.f_wsm, 16)
        g.dma_start(out=AB[0:128, self.bd:self.bd + 1024],
                    in_=pr.h_bd_si.rearrange("c p k -> p c k")).then_inc(s.f_wsm, 16)
        g.dma_start(out=AB[0:128, self.bd + 1024:self.bd + 2048],
                    in_=pr.h_bd_so.rearrange("c p k -> p c k")).then_inc(s.f_wsm, 16)
        g.dma_start(out=AB[0:128, self.linT:self.linT + 2048],
                    in_=pr.h_linT_si.rearrange("(c p) f -> p c f", p=128)).then_inc(s.f_wsm, 16)
        g.dma_start(out=AB[0:128, self.linT + 2048:self.linT + 4096],
                    in_=pr.h_linT_so.rearrange("(c p) f -> p c f", p=128)).then_inc(s.f_wsm, 16)

        g.wait_ge(s.f_bnp, 4)
        g.dma_start(out=AB[0:1, self.linb:self.linb + 1024],
                    in_=pr.h_linb_si).then_inc(s.f_lbl, 16)
        g.dma_start(out=AB[0:1, self.linb + 1024:self.linb + 2048],
                    in_=pr.h_linb_so).then_inc(s.f_lbl, 16)
        g.dma_start(out=self.ccin.ap(), in_=AF[0:128, self.parts:self.parts + 8]).then_inc(s.f_ccd, 16)
        g.wait_ge(s.f_ccd, 16)
        g.collective_compute("AllReduce", ALU.add,
                             replica_groups=[list(range(8))],
                             ins=[self.ccin.ap().opt()],
                             outs=[self.ccout.ap().opt()]).then_inc(s.f_cc, 1)
        g.wait_ge(s.f_cc, 1)
        g.dma_start(out=AF[0:128, self.allr:self.allr + 8], in_=self.ccout.ap()).then_inc(s.f_ccd, 16)


    def esy(self, sy):
        pr, s = self.pr, self.pr.sem
        if DBG:
            sy.wait_ge(s.f_cmb, 32)
            AB, AF = pr.AB, pr.AF
            sy.dma_start(out=self.d_xsp, in_=AB[0:128, self.xsp:self.xsp + 16384]).then_inc(s.f_dbg, 16)
            sy.dma_start(out=self.d_full, in_=AB[0:128, self.full:self.full + 16384]).then_inc(s.f_dbg, 16)
            sy.dma_start(out=self.d_xc, in_=AB[0:128, self.xc:self.xc + 8192]).then_inc(s.f_dbg, 16)
            sy.dma_start(out=self.d_xa, in_=AB[0:128, self.xa:self.xa + 8192]).then_inc(s.f_dbg, 16)
            sy.dma_start(out=self.d_lg, in_=AF[0:128, self.lg:self.lg + 2048]).then_inc(s.f_dbg, 16)
            sy.dma_start(out=self.d_zap, in_=AF[0:128, self.zap:self.zap + 2048]).then_inc(s.f_dbg, 16)
            sy.dma_start(out=self.d_bnv, in_=AF[0:128, self.bnv:self.bnv + 16]).then_inc(s.f_dbg, 16)
            sy.dma_start(out=self.d_sml, in_=AF[0:128, self.sml:self.sml + 64]).then_inc(s.f_dbg, 16)
            sy.wait_ge(s.f_dbg, 128)
        if self.is_last:
            sy.wait_ge(s.f_lap, 32)
            sy.dma_start(out=pr.h_out, in_=pr.xTb[:, :]).then_inc(s.f_out, 16)
            sy.wait_ge(s.f_out, 16)



class _PH:
    def __init__(self, pr):
        self.pr = pr
    def et(self, t): pass
    def ev(self, v): pass
    def es(self, sc): pass
    def eg(self, g):
        if self.is_first:
            self.pr.emit_init_loads(g)
    def esy(self, sy): pass



class L1A:

    def __init__(self, pr: Prog):
        self.pr = pr
        A = Arena(pr.AB)
        self.kT = A.take(16384)
        self.vtok = A.take(16384)
        self.Ost = A.take(16384)
        self.qTb = A.take(4096)
        self.PT = A.take(8192)
        self.wsl = A.take(4096)
        self.prm = A.take(16)
        self.mbf = self.PT + 0
        self.rbf = self.PT + 2048
        self.tt2 = self.PT + 4096
        self.sq4 = self.PT + 5120
        self.ln = LNBlock(pr, "b_", self.sq4, self.mbf, self.rbf, self.tt2)
        assert A.off <= pr.AB.shape[1], A.off

        Fa = Arena(pr.AF)
        self.P2 = Fa.take(4096)
        self.sml = Fa.take(64)
        assert Fa.off <= pr.lnva, Fa.off

    def slab_buf(self, i):
        if i < 4 and i % 2 == 1:
            return self.qTb
        return self.wsl

    def et(self, t):
        pr, s = self.pr, self.pr.sem
        ps, AB, AF = pr.ps, pr.AB, pr.AF
        t.wait_ge(s.ld, pr.ld_val())

        kv_tile = 0
        for sl in range(4):
            kind, half = divmod(sl, 2)
            buf = self.slab_buf(sl)
            t.wait_ge(s.b_wl, 16 * (sl + 1))
            if kind == 0:
                for mm in range(4):
                    m = 4 * half + mm
                    for qt in range(4):
                        if kv_tile >= 4:
                            t.wait_ge(s.b_kev, kv_tile - 3)
                        for e in range(8):
                            t.matmul(bank(ps, kv_tile % 4),
                                     AB[0:128, buf + 512 * e + 128 * mm:buf + 512 * e + 128 * mm + 128],
                                     pr.xTb[0:128, 2048 * e + 512 * qt:2048 * e + 512 * qt + 512],
                                     start=(e == 0), stop=(e == 7)).then_inc(
                                         s.b_kvm, (1 if e == 7 else 0))
                        kv_tile += 1
            else:
                for tt in range(16):
                    if kv_tile >= 4:
                        t.wait_ge(s.b_kev, kv_tile - 3)
                    for e in range(8):
                        t.matmul(bank(ps, kv_tile % 4),
                                 pr.xTb[0:128, 2048 * e + 128 * tt:2048 * e + 128 * tt + 128],
                                 AB[0:128, buf + 512 * e:buf + 512 * e + 512],
                                 start=(e == 0), stop=(e == 7)).then_inc(
                                     s.b_kvm, (1 if e == 7 else 0))
                    kv_tile += 1
            t.sem_inc(s.b_sdone, 1)

        for qb in range(4):
            for half in range(2):
                sl = 4 + 2 * qb + half
                t.wait_ge(s.b_wl, 16 * (sl + 1))
                for mm in range(4):
                    m = 4 * half + mm
                    if 8 * qb + m >= 2:
                        t.wait_ge(s.b_qev, 8 * qb + m - 1)
                    for e in range(8):
                        t.matmul(bank(ps, 4 + m % 2),
                                 AB[0:128, self.slab_buf(sl) + 512 * e + 128 * mm:
                                    self.slab_buf(sl) + 512 * e + 128 * mm + 128],
                                 pr.xTb[0:128, 2048 * e + 512 * qb:2048 * e + 512 * qb + 512],
                                 start=(e == 0), stop=(e == 7)).then_inc(
                                     s.b_qmm, (1 if e == 7 else 0))
                t.sem_inc(s.b_sdone, 1)
            for il in range(4):
                qg = 4 * qb + il
                t.wait_ge(s.b_qev, 8 * qb + 8)
                if qg == 0:
                    t.wait_ge(s.b_kev, 64)
                if qg >= 1:
                    t.wait_ge(s.b_exp, qg)
                for ks in range(4):
                    for m in range(8):
                        t.matmul(bank(ps, ks),
                                 AB[0:128, self.qTb + 512 * m + 128 * il:
                                    self.qTb + 512 * m + 128 * il + 128],
                                 AB[0:128, self.kT + 2048 * m + 512 * ks:
                                    self.kT + 2048 * m + 512 * ks + 512],
                                 start=(m == 0), stop=(m == 7)).then_inc(
                                     s.b_smm, (1 if (ks == 3 and m == 7) else 0))
                t.wait_ge(s.b_exp, qg + 1)
                if qg >= 1:
                    t.wait_ge(s.b_ptv, qg)
                for jt in range(16):
                    bk = 4 + jt // 4
                    ins = t.transpose(bank(ps, bk, 0, 128, 128 * (jt % 4), 128 * (jt % 4) + 128),
                                      AF[0:128, self.P2 + 2048 * (qg % 2) + 128 * jt:
                                         self.P2 + 2048 * (qg % 2) + 128 * jt + 128],
                                      pr.c_id[:, :])
                ins.then_inc(s.b_ptr, 1)
            t.wait_ge(s.b_ptv, 4 * qb + 4)
            for il in range(4):
                for hs in range(2):
                    gidx = 8 * qb + 2 * il + hs
                    if gidx >= 2:
                        t.wait_ge(s.b_oev, gidx - 1)
                    for jt in range(16):
                        t.matmul(bank(ps, gidx % 2),
                                 AB[0:128, self.PT + 512 * jt + 128 * il:
                                    self.PT + 512 * jt + 128 * il + 128],
                                 AB[0:128, self.vtok + 1024 * jt + 512 * hs:
                                    self.vtok + 1024 * jt + 512 * hs + 512],
                                 start=(jt == 0), stop=(jt == 15)).then_inc(
                                     s.b_avm, (1 if jt == 15 else 0))

        re = AB[0:128, self.Ost:self.Ost + 16384].rearrange("p (b rest) -> p rest b", b=2)
        t.wait_ge(s.b_oev, 32)
        for half in range(2):
            sl = 12 + half
            t.wait_ge(s.b_wl, 16 * (sl + 1))
            for mm in range(4):
                m = 4 * half + mm
                for st in range(4):
                    idx = 16 * half + 4 * mm + st
                    if idx >= 4:
                        t.wait_ge(s.b_pja, idx - 3)
                    for c in range(8):
                        t.matmul(bank(ps, idx % 4),
                                 AB[0:128, self.wsl + 512 * c + 128 * mm:
                                    self.wsl + 512 * c + 128 * mm + 128],
                                 re[:, 1024 * c + 256 * st:1024 * c + 256 * st + 256, :],
                                 start=(c == 0), stop=(c == 7)).then_inc(
                                     s.b_pjm, (1 if c == 7 else 0))
            t.sem_inc(s.b_sdone, 1)

        self.ln.et(t, gate=(s.b_pja, 32))

    def ev(self, v):
        pr, s = self.pr, self.pr.sem
        ps, AB, AF = pr.ps, pr.AB, pr.AF
        v.wait_ge(s.ld, pr.ld_val())
        if self.is_first:
            pr.emit_cval_sets(v)

        kv_tile = 0
        for sl in range(4):
            kind, half = divmod(sl, 2)
            if kind == 0:
                for mm in range(4):
                    m = 4 * half + mm
                    for qt in range(4):
                        v.wait_ge(s.b_kvm, kv_tile + 1)
                        v.tensor_copy(AB[0:128, self.kT + 2048 * m + 512 * qt:
                                         self.kT + 2048 * m + 512 * qt + 512],
                                      bank(ps, kv_tile % 4)).then_inc(s.b_kev, 1)
                        kv_tile += 1
            else:
                for tt in range(16):
                    v.wait_ge(s.b_kvm, kv_tile + 1)
                    v.tensor_copy(AB[0:128, self.vtok + 1024 * tt + 512 * half:
                                     self.vtok + 1024 * tt + 512 * half + 512],
                                  bank(ps, kv_tile % 4)).then_inc(s.b_kev, 1)
                    kv_tile += 1

        for qb in range(4):
            for m in range(8):
                if m == 0 and qb >= 1:
                    v.wait_ge(s.b_smm, 4 * qb)
                v.wait_ge(s.b_qmm, 8 * qb + m + 1)
                v.tensor_copy(AB[0:128, self.qTb + 512 * m:self.qTb + 512 * m + 512],
                              bank(ps, 4 + m % 2)).then_inc(s.b_qev, 1)
            for il in range(4):
                qg = 4 * qb + il
                v.wait_ge(s.b_smm, qg + 1)
                v.reduce_max(AF[0:128, self.sml + qg:self.sml + qg + 1],
                             ps[0:128, 0:2048], axis=AX.X)
                v.tensor_scalar_mul(AF[0:128, self.sml + 16 + qg:self.sml + 16 + qg + 1],
                                    AF[0:128, self.sml + qg:self.sml + qg + 1],
                                    -0.125).then_inc(s.b_mx, 1)
                v.wait_ge(s.b_ptr, qg + 1)
                if qb >= 1 and il == 0:
                    v.wait_ge(s.b_avm, 8 * qb)
                for jt in range(16):
                    bk = 4 + jt // 4
                    ins = v.tensor_copy(AB[0:128, self.PT + 512 * jt + 128 * il:
                                           self.PT + 512 * jt + 128 * il + 128],
                                        bank(ps, bk, 0, 128, 128 * (jt % 4), 128 * (jt % 4) + 128))
                ins.then_inc(s.b_ptv, 1)
                v.wait_ge(s.b_exp, qg + 1)
                v.reciprocal(AF[0:128, self.sml + 48 + qg:self.sml + 48 + qg + 1],
                             AF[0:128, self.sml + 32 + qg:self.sml + 32 + qg + 1]).then_inc(s.b_rz, 1)
            for il in range(4):
                for hs in range(2):
                    gidx = 8 * qb + 2 * il + hs
                    qg = 4 * qb + il
                    v.wait_ge(s.b_rz, qg + 1)
                    v.wait_ge(s.b_avm, gidx + 1)
                    tt = 4 * qb + il
                    v.tensor_scalar_mul(AB[0:128, self.Ost + 1024 * tt + 512 * hs:
                                           self.Ost + 1024 * tt + 512 * hs + 512],
                                        bank(ps, gidx % 2),
                                        AF[0:128, self.sml + 48 + qg:self.sml + 48 + qg + 1]).then_inc(s.b_oev, 1)

        v.wait_ge(s.b_pl, 16)
        for half in range(2):
            for mm in range(4):
                m = 4 * half + mm
                for st in range(4):
                    idx = 16 * half + 4 * mm + st
                    v.wait_ge(s.b_pjm, idx + 1)
                    xv = pr.xTb[0:128, 2048 * m + 512 * st:2048 * m + 512 * st + 512]
                    v.scalar_tensor_tensor(xv, bank(ps, idx % 4),
                                           AB[0:128, self.prm + m:self.prm + m + 1],
                                           xv, ALU.add, ALU.add).then_inc(s.b_pja, 1)

        self.ln.ev(v, gate=(s.b_pja, 32))

    def es(self, sc):
        pr, s = self.pr, self.pr.sem
        ps, AF = pr.ps, pr.AF
        sc.wait_ge(s.ld, pr.ld_val())
        sc.wait_ge(s.cvals, 1)
        for qg in range(16):
            sc.wait_ge(s.b_mx, qg + 1)
            if qg >= 2:
                sc.wait_ge(s.b_ptr, qg - 1)
            sc.activation(AF[0:128, self.P2 + 2048 * (qg % 2):self.P2 + 2048 * (qg % 2) + 2048],
                          ps[0:128, 0:2048], AFT.Exp,
                          bias=AF[0:128, self.sml + 16 + qg:self.sml + 16 + qg + 1],
                          scale=pr.cval(0.125, 128),
                          accum_out=AF[0:128, self.sml + 32 + qg:self.sml + 32 + qg + 1]).then_inc(s.b_exp, 1)
        self.ln.es(sc)

    def eg(self, g):
        pr, s = self.pr, self.pr.sem
        AB = pr.AB
        if self.is_first:
            pr.emit_init_loads(g)
        g.dma_start(out=AB[0:128, self.prm:self.prm + 8], in_=pr.h_bprj).then_inc(s.b_pl, 16)
        def load_slab(i, src_ap):
            buf = self.slab_buf(i)
            g.dma_start(out=AB[0:128, buf:buf + 4096], in_=src_ap).then_inc(s.b_wl, 16)
        order = []
        order.append(pr.h_wqkvT[:, 1024:1536])
        order.append(pr.h_wqkvT[:, 1536:2048])
        order.append(pr.h_wqkvT[:, 2048:2560])
        order.append(pr.h_wqkvT[:, 2560:3072])
        for qb in range(4):
            order.append(pr.h_wqkvT[:, 0:512])
            order.append(pr.h_wqkvT[:, 512:1024])
        order.append(pr.h_wprjT[:, 0:512])
        order.append(pr.h_wprjT[:, 512:1024])
        for i, src in enumerate(order):
            if i in (2, 3):
                g.wait_ge(s.b_sdone, i - 1)
            elif i >= 4:
                g.wait_ge(s.b_sdone, i)
            load_slab(i, src.rearrange("(c p) f -> p c f", p=128))

    def esy(self, sy):
        pr, s = self.pr, self.pr.sem
        if self.is_last:
            sy.wait_ge(s.b_lap, 32)
            sy.dma_start(out=pr.h_out, in_=pr.xTb[:, :]).then_inc(s.b_out, 16)
            sy.wait_ge(s.b_out, 16)



class L1F:
    def __init__(self, pr: Prog):
        self.pr = pr
        A = Arena(pr.AB)
        self.RING = A.take(40960)
        self.yac = A.take(16384)
        self.gtr = A.take(2048)
        self.gbs = A.take(2048)
        self.ht = A.take(4096)
        self.tu = A.take(1024)

        self.db = A.take(1024)
        self.sel8 = A.take(1024)
        self.wrt = A.take(64)
        self.mbf = self.RING + 0
        self.rbf = self.RING + 2048
        self.tt2 = self.RING + 4096
        self.sq4 = self.RING + 5120
        self.ln = LNBlock(pr, "c_", self.sq4, self.mbf, self.rbf, self.tt2,
                          final=pr.last, out_scr=None)
        assert A.off <= pr.AB.shape[1], A.off

        Fa = Arena(pr.AF)
        if DBG:
            self.d_gtr = pr.nc.dram_tensor("d_gtr", [8, 2048], BF16, kind="ExternalOutput").ap()
            self.d_yac = pr.nc.dram_tensor("d_yac", [128, 16384], BF16, kind="ExternalOutput").ap()
            self.d_gtt = pr.nc.dram_tensor("d_gtt", [128, 128], F32, kind="ExternalOutput").ap()
            self.d_rtl = pr.nc.dram_tensor("d_rtl", [128, 128], F32, kind="ExternalOutput").ap()
            self.d_rte = pr.nc.dram_tensor("d_rte", [128, 128], F32, kind="ExternalOutput").ap()
        self.rtl = Fa.take(128)
        self.rte = Fa.take(128)
        self.sml = Fa.take(96)
        self.m8 = Fa.take(8)
        self.gtt = Fa.take(128)
        self.gbp = Fa.take(128)
        self.ubp = Fa.take(128)
        self.osc = pr.lnva
        self.ln.out_scr = self.osc
        assert Fa.off <= pr.lnva, Fa.off

    def slab(self, k):
        return self.RING + 8192 * (k % 5)

    def et(self, t):
        pr, s = self.pr, self.pr.sem
        ps, AB, AF = pr.ps, pr.AB, pr.AF
        t.wait_ge(s.ld, pr.ld_val())

        for tt in range(16):
            t.wait_ge(s.c_pl, 16 * 5)
            if tt >= 1:
                t.wait_ge(s.c_rcp, tt)
            for e8 in range(8):
                t.matmul(bank(ps, 7, 0, 128, 0, 8),
                         pr.xTb[0:128, 2048 * e8 + 128 * tt:2048 * e8 + 128 * tt + 128],
                         AB[0:128, self.wrt + 8 * e8:self.wrt + 8 * e8 + 8],
                         start=(e8 == 0), stop=(e8 == 7)).then_inc(
                             s.c_rmm, (1 if e8 == 7 else 0))
            j = tt - 1
            if j >= 0:
                t.wait_ge(s.c_rg, j + 1)
                if j >= 2:
                    t.wait_ge(s.c_gte, j - 1)
                t.transpose(bank(ps, 4 + j % 2, 0, 8, 0, 128),
                            AF[0:128, self.gtt + 8 * j:self.gtt + 8 * j + 8],
                            pr.c_id[:, :]).then_inc(s.c_gtm, 1)
        t.wait_ge(s.c_rg, 16)
        t.wait_ge(s.c_gte, 15)
        t.transpose(bank(ps, 5, 0, 8, 0, 128),
                    AF[0:128, self.gtt + 8 * 15:self.gtt + 8 * 15 + 8],
                    pr.c_id[:, :]).then_inc(s.c_gtm, 1)

        def down_group(T, p, fd, sip):
            dslab = self.slab(3 * sip + 2) + 1024 * fd
            for dc in range(4 * p, 4 * p + 4):
                t.matmul(bank(ps, dc % 4),
                         AB[0:128, dslab + 128 * dc:dslab + 128 * dc + 128],
                         AB[0:128, self.ht + 512 * fd:self.ht + 512 * fd + 512],
                         start=(fd == 0 and sip > 0),
                         stop=(fd == 7)).then_inc(s.c_dmf, (1 if dc == 4 * p + 3 else 0))

        def db_group(T, p):
            ntp = T % 4
            for dc in range(4 * p, 4 * p + 4):
                t.matmul(bank(ps, dc % 4),
                         AB[0:8, self.db + 128 * dc:self.db + 128 * dc + 128],
                         AB[0:8, self.gtr + 512 * ntp:self.gtr + 512 * ntp + 512],
                         start=True, stop=False)

        def sweep(Tp):
            sip = Tp // 4
            t.wait_ge(s.c_ya, 8 * Tp + 4)
            if sip == 0:
                db_group(Tp, 1)
            for fd in range(8):
                down_group(Tp, 1, fd, sip)

        for e in range(8):
            t.wait_ge(s.c_gte, 16)
            if e >= 1:
                t.wait_ge(s.c_tu, 64 * e)
            for nt in range(4):
                if nt >= 2 or e >= 1:
                    t.wait_ge(s.c_gbe, 4 * e + nt - 1)
                t.matmul(bank(ps, 6 + nt % 2),
                         AB[0:8, self.sel8 + 128 * e:self.sel8 + 128 * e + 128],
                         AB[0:8, self.gtr + 512 * nt:self.gtr + 512 * nt + 512],
                         start=True, stop=True).then_inc(s.c_gbm, 1)
            for fch in range(2):
                si = 2 * e + fch
                t.wait_ge(s.c_wl, 32 * si + 32)
                for nt in range(4):
                    T = 4 * si + nt
                    for fc in range(10):
                        if fc < 8:
                            cnt = 8 * T + fc
                            if cnt >= 2:
                                t.wait_ge(s.c_sil, cnt - 1)
                            if fch == 0 and nt == 0 and fc == 0:
                                t.wait_ge(s.c_gbe, 4 * (e + 1))
                            wslab = self.slab(3 * si) + 128 * fc
                            uslab = self.slab(3 * si + 1) + 128 * fc
                            for dc in range(8):
                                t.matmul(bank(ps, 4 + fc % 2),
                                         AB[0:128, wslab + 1024 * dc:wslab + 1024 * dc + 128],
                                         pr.xTb[0:128, 2048 * dc + 512 * nt:2048 * dc + 512 * nt + 512],
                                         start=(dc == 0), stop=(dc == 7)).then_inc(
                                             s.c_gmm, (1 if dc == 7 else 0))
                            if cnt >= 2:
                                t.wait_ge(s.c_tu, cnt - 1)
                            for dc in range(8):
                                t.matmul(bank(ps, 6 + fc % 2),
                                         AB[0:128, uslab + 1024 * dc:uslab + 1024 * dc + 128],
                                         pr.xTb[0:128, 2048 * dc + 512 * nt:2048 * dc + 512 * nt + 512],
                                         start=(dc == 0), stop=(dc == 7)).then_inc(
                                             s.c_umm, (1 if dc == 7 else 0))
                        if fc == 1 and T >= 1:
                            sweep(T - 1)
                        fd = fc - 2
                        if fd >= 0:
                            t.wait_ge(s.c_h, 8 * T + fd + 1)
                            if fd == 0:
                                t.wait_ge(s.c_ya, 8 * T)
                                if nt == 0:
                                    t.wait_ge(s.c_wdl, 16 * si + 16)
                                if si == 0:
                                    db_group(T, 0)
                            down_group(T, 0, fd, si)
        sweep(63)

        self.ln.et(t, gate=(s.c_rs, 8))

    def ev(self, v):
        pr, s = self.pr, self.pr.sem
        ps, AB, AF = pr.ps, pr.AB, pr.AF
        v.wait_ge(s.ld, pr.ld_val())
        if self.is_first:
            pr.emit_cval_sets(v)

        self.vsc = 0
        m8 = AF[0:128, self.m8:self.m8 + 8]
        for tt in range(16):
            mx = AF[0:128, self.sml + tt:self.sml + tt + 1]
            mxn = AF[0:128, self.sml + 16 + tt:self.sml + 16 + tt + 1]
            Z = AF[0:128, self.sml + 32 + tt:self.sml + 32 + tt + 1]
            thr = AF[0:128, self.sml + 48 + tt:self.sml + 48 + tt + 1]
            den = AF[0:128, self.sml + 64 + tt:self.sml + 64 + tt + 1]
            rden = AF[0:128, self.sml + 80 + tt:self.sml + 80 + tt + 1]
            gtt = AF[0:128, self.gtt + 8 * tt:self.gtt + 8 * tt + 8]
            rt = AF[0:128, self.rtl + 8 * tt:self.rtl + 8 * tt + 8]
            def vs(ins):
                self.vsc += 1
                ins.then_inc(s.c_vsy, 1)
                v.wait_ge(s.c_vsy, self.vsc)
            v.wait_ge(s.c_rmm, tt + 1)
            v.tensor_copy(rt, bank(ps, 7, 0, 128, 0, 8)).then_inc(s.c_rcp, 1)
            v.wait_ge(s.c_rcp, tt + 1)
            vs(v.reduce_max(mx, rt, axis=AX.X))
            v.tensor_scalar_mul(mxn, mx, -1.0).then_inc(s.c_rmx, 1)
            rte = AF[0:128, self.rte + 8 * tt:self.rte + 8 * tt + 8]
            v.wait_ge(s.c_rex, tt + 1)
            vs(v.max(out=m8, in_=rte))
            vs(v.tensor_copy(thr, AF[0:128, self.m8 + 1:self.m8 + 2]))
            vs(v.tensor_single_scalar(gtt, rte, thr, ALU.is_ge))
            vs(v.tensor_mul(gtt, gtt, rte))
            vs(v.reduce_sum(den, gtt, axis=AX.X))
            vs(v.tensor_scalar_mul(Z, Z, 1e-9))
            vs(v.tensor_add(den, den, Z))
            vs(v.reciprocal(rden, den))
            v.tensor_scalar_mul(gtt, gtt, rden).then_inc(s.c_rg, 1)
            j = tt - 1
            if j >= 0:
                v.wait_ge(s.c_gtm, j + 1)
                v.tensor_copy(AB[0:8, self.gtr + 128 * j:self.gtr + 128 * j + 128],
                              bank(ps, 4 + j % 2, 0, 8, 0, 128)).then_inc(s.c_gte, 1)
        v.wait_ge(s.c_gtm, 16)
        v.tensor_copy(AB[0:8, self.gtr + 128 * 15:self.gtr + 128 * 15 + 128],
                      bank(ps, 5, 0, 8, 0, 128)).then_inc(s.c_gte, 1)

        def y_evac(T, p):
            sip, ntp = divmod(T, 4)
            for dc in range(4 * p, 4 * p + 4):
                yv = AB[0:128, self.yac + 2048 * dc + 512 * ntp:
                        self.yac + 2048 * dc + 512 * ntp + 512]
                pv = bank(ps, dc % 4)
                if sip == 0:
                    v.tensor_copy(yv, pv).then_inc(s.c_ya, 1)
                else:
                    v.tensor_add(yv, yv, pv).then_inc(s.c_ya, 1)

        for e in range(8):
            for fch in range(2):
                si = 2 * e + fch
                for nt in range(4):
                    T = 4 * si + nt
                    v.wait_ge(s.c_gbe, 4 * e + nt + 1)
                    for fc in range(8):
                        cnt = 8 * T + fc
                        v.wait_ge(s.c_umm, cnt + 1)
                        tuv = AB[0:128, self.tu + 512 * (fc % 2):self.tu + 512 * (fc % 2) + 512]
                        v.scalar_tensor_tensor(tuv, bank(ps, 6 + fc % 2),
                                               AF[0:128, self.ubp + 16 * e + 8 * fch + fc:
                                                  self.ubp + 16 * e + 8 * fch + fc + 1],
                                               AB[0:128, self.gbs + 512 * nt:self.gbs + 512 * nt + 512],
                                               ALU.add, ALU.mult).then_inc(s.c_tu, 1)
                        v.wait_ge(s.c_tu, cnt + 1)
                        v.wait_ge(s.c_sil, cnt + 1)
                        htv = AB[0:128, self.ht + 512 * fc:self.ht + 512 * fc + 512]
                        v.tensor_mul(htv, htv, tuv).then_inc(s.c_h, 1)
                        if fc == 1 and T >= 1:
                            v.wait_ge(s.c_dmf, 16 * T)
                            y_evac(T - 1, 1)
                    v.wait_ge(s.c_dmf, 16 * T + 8)
                    y_evac(T, 0)
        v.wait_ge(s.c_dmf, 1024)
        y_evac(63, 1)

        for dc in range(8):
            v.wait_ge(s.c_ya, 512)
            v.tensor_add(pr.xTb[0:128, 2048 * dc:2048 * dc + 2048],
                         pr.xTb[0:128, 2048 * dc:2048 * dc + 2048],
                         AB[0:128, self.yac + 2048 * dc:self.yac + 2048 * dc + 2048]).then_inc(s.c_rs, 1)

        self.ln.ev(v, gate=(s.c_rs, 8))

    def es(self, sc):
        pr, s = self.pr, self.pr.sem
        ps, AB, AF = pr.ps, pr.AB, pr.AF
        sc.wait_ge(s.ld, pr.ld_val())
        sc.wait_ge(s.cvals, 1)
        for tt in range(16):
            rt = AF[0:128, self.rtl + 8 * tt:self.rtl + 8 * tt + 8]
            sc.wait_ge(s.c_rmx, tt + 1)
            sc.activation(AF[0:128, self.rte + 8 * tt:self.rte + 8 * tt + 8], rt, AFT.Exp,
                          bias=AF[0:128, self.sml + 16 + tt:self.sml + 16 + tt + 1],
                          accum_out=AF[0:128, self.sml + 32 + tt:self.sml + 32 + tt + 1]).then_inc(s.c_rex, 1)
        for e in range(8):
            for nt in range(4):
                sc.wait_ge(s.c_gbm, 4 * e + nt + 1)
                sc.copy(AB[0:128, self.gbs + 512 * nt:self.gbs + 512 * nt + 512],
                        bank(ps, 6 + nt % 2)).then_inc(s.c_gbe, 1)
            for fch in range(2):
                si = 2 * e + fch
                for nt in range(4):
                    T = 4 * si + nt
                    for fc in range(8):
                        cnt = 8 * T + fc
                        sc.wait_ge(s.c_gmm, cnt + 1)
                        if T >= 1:
                            sc.wait_ge(s.c_dmf, 16 * T + fc - 7)
                        sc.activation(AB[0:128, self.ht + 512 * fc:self.ht + 512 * fc + 512],
                                      bank(ps, 4 + fc % 2), AFT.Silu,
                                      bias=AF[0:128, self.gbp + 16 * e + 8 * fch + fc:
                                              self.gbp + 16 * e + 8 * fch + fc + 1]).then_inc(s.c_sil, 1)
        self.ln.es(sc)

    def eg(self, g):
        pr, s = self.pr, self.pr.sem
        AB = pr.AB
        if self.is_first:
            pr.emit_init_loads(g)
        g.dma_start(out=AB[0:128, self.wrt:self.wrt + 64],
                    in_=pr.h_wrtT.rearrange("(c p) e -> p c e", p=128)).then_inc(s.c_pl, 16)
        g.dma_start(out=pr.AF[0:128, self.gbp:self.gbp + 128], in_=pr.h_gb).then_inc(s.c_pl, 16)
        g.dma_start(out=pr.AF[0:128, self.ubp:self.ubp + 128], in_=pr.h_ub).then_inc(s.c_pl, 16)
        g.dma_start(out=AB[0:8, self.db:self.db + 1024], in_=pr.h_db).then_inc(s.c_pl, 16)
        g.dma_start(out=AB[0:8, self.sel8:self.sel8 + 1024],
                    in_=pr.h_sel8.rearrange("e k p -> k e p")).then_inc(s.c_pl, 16)
        for si in range(16):
            e, fch = divmod(si, 2)
            if si >= 2:
                g.wait_ge(s.c_umm, 32 * (si - 1))
            g.dma_start(out=AB[0:128, self.slab(3 * si):self.slab(3 * si) + 8192],
                        in_=pr.h_wg[1024 * e:1024 * e + 1024, 1024 * fch:1024 * fch + 1024]
                        .rearrange("(dc p) c -> p dc c", p=128)).then_inc(s.c_wl, 16)
            if si >= 2:
                g.wait_ge(s.c_dmf, 64 * (si - 1))
            g.dma_start(out=AB[0:128, self.slab(3 * si + 1):self.slab(3 * si + 1) + 8192],
                        in_=pr.h_wu[1024 * e:1024 * e + 1024, 1024 * fch:1024 * fch + 1024]
                        .rearrange("(dc p) c -> p dc c", p=128)).then_inc(s.c_wl, 16)
            if si >= 1:
                g.wait_ge(s.c_gmm, 32 * si)
            g.dma_start(out=AB[0:128, self.slab(3 * si + 2):self.slab(3 * si + 2) + 8192],
                        in_=pr.h_wd[2048 * e + 1024 * fch:2048 * e + 1024 * fch + 1024, :]
                        .rearrange("(fc p) c -> p fc c", p=128)).then_inc(s.c_wdl, 16)

    def esy(self, sy):
        pr, s = self.pr, self.pr.sem
        if DBG:
            sy.wait_ge(s.c_rs, 8)
            sy.dma_start(out=self.d_gtr, in_=pr.AB[0:8, self.gtr:self.gtr + 2048]).then_inc(s.c_dbg, 16)
            sy.dma_start(out=self.d_yac, in_=pr.AB[0:128, self.yac:self.yac + 16384]).then_inc(s.c_dbg, 16)
            sy.dma_start(out=self.d_gtt, in_=pr.AF[0:128, self.gtt:self.gtt + 128]).then_inc(s.c_dbg, 16)
            sy.dma_start(out=self.d_rtl, in_=pr.AF[0:128, self.rtl:self.rtl + 128]).then_inc(s.c_dbg, 16)
            sy.dma_start(out=self.d_rte, in_=pr.AF[0:128, self.rte:self.rte + 128]).then_inc(s.c_dbg, 16)
            sy.wait_ge(s.c_dbg, 80)
        if pr.last:
            for c in range(8):
                sy.wait_ge(s.c_lap, 4 * (c + 1))
                sy.dma_start(out=pr.h_out[128 * c:128 * c + 128, :],
                             in_=pr.AF[0:128, self.osc + 2048 * (c % 2):self.osc + 2048 * (c % 2) + 2048]).then_inc(s.c_ost, 16)
            sy.wait_ge(s.c_ost, 128)
        else:
            sy.wait_ge(s.c_lap, 32)
            sy.dma_start(out=pr.h_out, in_=pr.xTb[:, :]).then_inc(s.c_out, 16)
            sy.wait_ge(s.c_out, 16)



def kernel(**inputs):
    d = _prep(inputs)
    xs = d["xT"]
    cur = None
    out = None
    for gi, group in enumerate(PHASE_GROUPS()):
        pr = Prog(group)
        nc = pr.build()
        in_maps = []
        for c in range(8):
            m = {}
            for name in pr.inputs:
                if name == "xT":
                    m[name] = xs[c]
                elif name == "xh":
                    m[name] = cur[c]
                else:
                    m[name] = d[name]
            in_maps.append(m)
        res = run_bass_kernel_spmd(nc, in_maps, core_ids=list(range(8)))
        if group[-1] == "l1f":
            out = [res.results[c]["out"] for c in range(8)]
        else:
            cur = [res.results[c]["xho"] for c in range(8)]
    return np.stack([o.T for o in out], axis=0)


def PHASE_GROUPS():
    return GROUPS


# version 13
# speedup vs baseline: 1.0796x; 1.0057x over previous
import numpy as np
import ml_dtypes
import concourse.bass as bass
import concourse.mybir as mybir
from concourse.bass_utils import run_bass_kernel_spmd

F32 = mybir.dt.float32
BF16 = mybir.dt.bfloat16
AFT = mybir.ActivationFunctionType
ALU = mybir.AluOpType
AX = mybir.AxisListType

S = 2048
E = 1024
NHEAD = 16
HD = 64
NCHUNK = 8
NQT = 4
BF = ml_dtypes.bfloat16

GROUPS = [["l0a", "l0f", "l1a", "l1f"]]
DBG = False



def _prep(inp):
    d = {}
    x = np.asarray(inp["L0_attn_x"])
    d["xT"] = [np.ascontiguousarray(x[c].T) for c in range(8)]

    d["w0qT"] = np.ascontiguousarray(np.asarray(inp["L0_attn_Wq"]).T)
    d["w0kT"] = np.ascontiguousarray(np.asarray(inp["L0_attn_Wk"]).T)
    d["w0vT"] = np.ascontiguousarray(np.asarray(inp["L0_attn_Wv"]).T)
    d["g_rep"] = np.full((128, 1), np.float32(np.asarray(inp["L0_attn_g_scale"])), np.float32)

    d["w0f"] = np.ascontiguousarray(np.asarray(inp["L0_ffn_weight"]))
    d["bias0"] = np.asarray(inp["L0_ffn_bias"]).reshape(8, 128).T.copy()
    for br, tag in (("so", "so"), ("si", "si")):
        cw = np.asarray(inp[f"L0_ffn_{br}_conv_w"])
        bd = np.zeros((8, 128, 128), np.float32)
        for c in range(8):
            for gl in range(32):
                g = 32 * c + gl
                col = 32 * (c % 4) + gl
                bd[c, 4 * gl:4 * gl + 4, col] = cw[g]
        d[f"bd_{tag}"] = bd
        d[f"cb_{tag}"] = np.asarray(inp[f"L0_ffn_{br}_conv_b"]).reshape(2, 128).T.copy()
        d[f"gam_{tag}"] = np.asarray(inp[f"L0_ffn_{br}_gamma"]).reshape(2, 128).T.copy()
        d[f"bet_{tag}"] = np.asarray(inp[f"L0_ffn_{br}_beta"]).reshape(2, 128).T.copy()
        d[f"linT_{tag}"] = np.ascontiguousarray(np.asarray(inp[f"L0_ffn_{br}_lin_w"]).T)
        d[f"linb_{tag}"] = np.asarray(inp[f"L0_ffn_{br}_lin_b"]).reshape(1, 1024).copy()

    d["wqkvT"] = np.ascontiguousarray(np.asarray(inp["L1_attn_w_qkv"]).T)
    d["wprjT"] = np.ascontiguousarray(np.asarray(inp["L1_attn_w_proj"]).T)
    d["bprj"] = np.asarray(inp["L1_attn_b_proj"]).reshape(8, 128).T.copy()

    d["wrtT"] = np.ascontiguousarray(np.asarray(inp["L1_ffn_router_w"]).T)
    d["wg"] = np.ascontiguousarray(np.asarray(inp["L1_ffn_gate_w"]).reshape(8192, 2048))
    d["wu"] = np.ascontiguousarray(np.asarray(inp["L1_ffn_up_w"]).reshape(8192, 2048))
    d["wd"] = np.ascontiguousarray(np.asarray(inp["L1_ffn_down_w"]).reshape(16384, 1024))
    d["gb"] = np.asarray(inp["L1_ffn_gate_b"]).reshape(8, 16, 128).transpose(2, 0, 1).reshape(128, 128).copy()
    d["ub"] = np.asarray(inp["L1_ffn_up_b"]).reshape(8, 16, 128).transpose(2, 0, 1).reshape(128, 128).copy()
    d["db"] = np.ascontiguousarray(np.asarray(inp["L1_ffn_down_b"]))

    d["c_id"] = np.eye(128, dtype=np.float32)
    d["c_ones_row"] = np.ones((1, 128), np.float32)
    d["c_ones_col"] = np.ones((128, 1), np.float32)
    blk16 = np.zeros((8, 128, 16), np.float32)
    sel16 = np.zeros((8, 16, 128), np.float32)
    for m in range(8):
        blk16[m, 0:64, 2 * m] = 1.0
        blk16[m, 64:128, 2 * m + 1] = 1.0
        sel16[m, 2 * m, 0:64] = 1.0
        sel16[m, 2 * m + 1, 64:128] = 1.0
    d["c_blk16"] = blk16
    d["c_sel16"] = sel16
    sel8 = np.zeros((8, 8, 128), np.float32)
    for e in range(8):
        sel8[e, e, :] = 1.0
    d["c_sel8"] = sel8
    return d



class Sems:
    SHARED = ("ld", "cvals", "bar")

    def __init__(self, nc, pool_n=48):
        self.nc = nc
        self.m = {}
        self.pool = [nc.alloc_semaphore(f"pool{i}") for i in range(pool_n)]
        self.phase_maps = {}
        self.phase_map = {}
        self.pi_box = [0]

    def set_phase(self, k):
        if k not in self.phase_maps:
            self.phase_maps[k] = ({}, [0])
        self.phase_map, box = self.phase_maps[k]
        self.pi_box = box

    def next_phase(self):
        self.set_phase(len(self.phase_maps))

    def __getattr__(self, name):
        if name in ("nc", "m", "pool", "phase_map", "phase_maps", "pi", "pi_box"):
            raise AttributeError(name)
        if name in self.SHARED:
            if name not in self.m:
                self.m[name] = self.nc.alloc_semaphore(name)
            return self.m[name]
        if name not in self.phase_map:
            assert self.pi_box[0] < len(self.pool), "sem pool exhausted"
            self.phase_map[name] = self.pool[self.pi_box[0]]
            self.pi_box[0] += 1
        return self.phase_map[name]


class Arena:
    def __init__(self, t):
        self.t = t
        self.off = 0
        self.marks = []

    def take(self, cols):
        o = self.off
        self.off += cols
        assert self.off <= self.t.shape[1], f"arena overflow {self.off} > {self.t.shape[1]}"
        return o

    def push(self):
        self.marks.append(self.off)

    def pop(self):
        self.off = self.marks.pop()


class _MockInst:
    def __init__(self, tot):
        self.tot = tot

    def then_inc(self, sem, n=1, **kw):
        self.tot[sem.num] = self.tot.get(sem.num, 0) + n
        return self

    def annotate(self, *_):
        return self


class _MockEng:
    def __init__(self, tot):
        self.tot = tot

    def wait_ge(self, sem, val):
        pass

    def sem_inc(self, sem, n=1):
        self.tot[sem.num] = self.tot.get(sem.num, 0) + n
        return _MockInst(self.tot)

    def __getattr__(self, op):
        if op.startswith("_"):
            raise AttributeError(op)
        def f(*a, **kw):
            return _MockInst(self.tot)
        return f


class _EngProxy:
    def __init__(self, eng, base):
        object.__setattr__(self, "_e", eng)
        object.__setattr__(self, "_b", base)

    def wait_ge(self, sem, val):
        return self._e.wait_ge(sem, val + self._b.get(sem.num, 0))

    def __getattr__(self, n):
        return getattr(self._e, n)


def bank(ps, i, p0=0, p1=128, c0=0, c1=512):
    return ps[p0:p1, 512 * i + c0:512 * i + c1]




class LNBlock:

    def __init__(self, pr, pfx, sq4, mbf, rbf, tt2, final=False, out_scr=None):
        self.pr, self.pfx = pr, pfx
        self.sq4, self.mbf, self.rbf, self.tt2 = sq4, mbf, rbf, tt2
        self.final = final
        self.out_scr = out_scr

    def sm(self, name):
        return getattr(self.pr.sem, self.pfx + name)

    def et(self, t, gate=None):
        pr = self.pr
        ps, AB = pr.ps, pr.AB
        if gate is not None:
            t.wait_ge(*gate)
        for c in range(8):
            for qt in range(4):
                ins = t.matmul(bank(ps, qt, 0, 1), pr.c_oc_bf[:, :],
                               pr.xTb[0:128, 2048 * c + 512 * qt:2048 * c + 512 * qt + 512],
                               start=(c == 0), stop=(c == 7))
                if c == 7:
                    ins.then_inc(self.sm("stm"), 1)
                idx = c * 4 + qt
                t.wait_ge(self.sm("lsq"), idx + 1)
                t.matmul(bank(ps, 4 + qt, 0, 1), pr.c_oc_bf[:, :],
                         AB[0:128, self.sq4 + 512 * (idx % 4):self.sq4 + 512 * (idx % 4) + 512],
                         start=(c == 0), stop=(c == 7)).then_inc(self.sm("lnmc"), 1)
        if not self.final:
            for qt in range(4):
                t.wait_ge(self.sm("lnv"), 1)
                if qt >= 2:
                    t.wait_ge(self.sm("lap"), 8 * (qt - 1))
                t.matmul(bank(ps, qt % 2), pr.c_or_bf[:, :],
                         AB[0:1, self.mbf + 512 * qt:self.mbf + 512 * qt + 512],
                         start=True, stop=True)
                t.matmul(bank(ps, 2 + qt % 2), pr.c_or_bf[:, :],
                         AB[0:1, self.rbf + 512 * qt:self.rbf + 512 * qt + 512],
                         start=True, stop=True).then_inc(self.sm("lbc"), 1)
        else:
            t.wait_ge(self.sm("lnv"), 1)
            for qt in range(4):
                t.matmul(bank(ps, qt), pr.c_or_bf[:, :],
                         AB[0:1, self.mbf + 512 * qt:self.mbf + 512 * qt + 512],
                         start=True, stop=True)
                t.matmul(bank(ps, 4 + qt), pr.c_or_bf[:, :],
                         AB[0:1, self.rbf + 512 * qt:self.rbf + 512 * qt + 512],
                         start=True, stop=True).then_inc(self.sm("lbc"), 1)

    def ev(self, v, gate=None):
        pr = self.pr
        ps, AB, AF = pr.ps, pr.AB, pr.AF
        for idx in range(32):
            c, qt = divmod(idx, 4)
            if idx == 0 and gate is not None:
                v.wait_ge(*gate)
            if idx >= 4:
                v.wait_ge(self.sm("lnmc"), idx - 3)
            v.tensor_mul(AB[0:128, self.sq4 + 512 * (idx % 4):self.sq4 + 512 * (idx % 4) + 512],
                         pr.xTb[0:128, 2048 * c + 512 * qt:2048 * c + 512 * qt + 512],
                         pr.xTb[0:128, 2048 * c + 512 * qt:2048 * c + 512 * qt + 512]).then_inc(self.sm("lsq"), 1)
        v.wait_ge(self.sm("lnmc"), 32)
        v.wait_ge(self.sm("stm"), 4)
        A_ = AF[0:1, pr.lnva:pr.lnva + 2048]
        B_ = AF[0:1, pr.lnvb:pr.lnvb + 2048]
        v.tensor_scalar_mul(A_, pr.ps[0:1, 0:2048], 1.0 / 1024.0)
        v.tensor_copy(AB[0:1, self.mbf:self.mbf + 2048], A_)
        v.tensor_mul(A_, A_, A_)
        v.tensor_scalar_mul(B_, pr.ps[0:1, 2048:4096], 1.0 / 1024.0)
        v.tensor_sub(B_, B_, A_).then_inc(self.sm("var"), 1)
        v.wait_ge(self.sm("sd"), 1)
        v.reciprocal(B_, A_)
        v.tensor_copy(AB[0:1, self.rbf:self.rbf + 2048], B_).then_inc(self.sm("lnv"), 1)
        if not self.final:
            for qt in range(4):
                v.wait_ge(self.sm("lbc"), qt + 1)
                for c in range(8):
                    tt = self.tt2 + 512 * (c % 2)
                    xv = pr.xTb[0:128, 2048 * c + 512 * qt:2048 * c + 512 * qt + 512]
                    v.tensor_tensor(AB[0:128, tt:tt + 512], xv, bank(ps, qt % 2), ALU.subtract)
                    v.tensor_tensor(xv, AB[0:128, tt:tt + 512], bank(ps, 2 + qt % 2),
                                    ALU.mult).then_inc(self.sm("lap"), 1)
        else:
            v.wait_ge(self.sm("lbc"), 4)
            for c in range(8):
                if c >= 2:
                    v.wait_ge(self.sm("ost"), 16 * (c - 1))
                for qt in range(4):
                    tt = self.tt2 + 512 * (qt % 2)
                    xv = pr.xTb[0:128, 2048 * c + 512 * qt:2048 * c + 512 * qt + 512]
                    v.tensor_tensor(AB[0:128, tt:tt + 512], xv, bank(ps, qt), ALU.subtract)
                    v.tensor_tensor(AF[0:128, self.out_scr + 2048 * (c % 2) + 512 * qt:
                                       self.out_scr + 2048 * (c % 2) + 512 * qt + 512],
                                    AB[0:128, tt:tt + 512], bank(ps, 4 + qt),
                                    ALU.mult).then_inc(self.sm("lap"), 1)

    def es(self, sc):
        pr = self.pr
        AF = pr.AF
        sc.wait_ge(self.sm("var"), 1)
        sc.activation(AF[0:1, pr.lnva:pr.lnva + 2048],
                      AF[0:1, pr.lnvb:pr.lnvb + 2048],
                      AFT.Sqrt, bias=pr.cval(1e-5, 1)).then_inc(self.sm("sd"), 1)



class Prog:
    def __init__(self, phases):
        self.phases = phases
        nc = bass.Bass("TRN2", target_bir_lowering=False, debug=False)
        self.nc = nc
        self.sem = Sems(nc)
        self.first = phases[0] == "l0a"
        self.last = phases[-1] == "l1f"
        self.inputs = []

        def inp(name, shape, dt=F32):
            self.inputs.append(name)
            return nc.dram_tensor(name, list(shape), dt, kind="ExternalInput").ap()

        self.inp = inp
        if self.first:
            self.h_x = inp("xT", (1024, 2048))
        else:
            self.h_x = inp("xh", (128, 16384), BF16)
        if self.last:
            self.h_out = nc.dram_tensor("out", [1024, 2048], F32, kind="ExternalOutput").ap()
        else:
            self.h_out = nc.dram_tensor("xho", [128, 16384], BF16, kind="ExternalOutput").ap()

        self.h_id = inp("c_id", (128, 128))
        self.h_ones_row = inp("c_ones_row", (1, 128))
        self.h_ones_col = inp("c_ones_col", (128, 1))
        self.h_blk16 = inp("c_blk16", (8, 128, 16))
        self.h_sel16 = inp("c_sel16", (8, 16, 128))
        ph = set(phases)
        if "l0a" in ph:
            self.h_w0q = inp("w0qT", (1024, 1024))
            self.h_w0k = inp("w0kT", (1024, 1024))
            self.h_w0v = inp("w0vT", (1024, 1024))
            self.h_grep = inp("g_rep", (128, 1))
        if "l0f" in ph:
            self.h_w0f = inp("w0f", (1024, 1024))
            self.h_bias0 = inp("bias0", (128, 8))
            for tag in ("so", "si"):
                setattr(self, f"h_bd_{tag}", inp(f"bd_{tag}", (8, 128, 128)))
                setattr(self, f"h_cb_{tag}", inp(f"cb_{tag}", (128, 2)))
                setattr(self, f"h_gam_{tag}", inp(f"gam_{tag}", (128, 2)))
                setattr(self, f"h_bet_{tag}", inp(f"bet_{tag}", (128, 2)))
                setattr(self, f"h_linT_{tag}", inp(f"linT_{tag}", (256, 1024)))
                setattr(self, f"h_linb_{tag}", inp(f"linb_{tag}", (1, 1024)))
        if "l1a" in ph:
            self.h_wqkvT = inp("wqkvT", (1024, 3072))
            self.h_wprjT = inp("wprjT", (1024, 1024))
            self.h_bprj = inp("bprj", (128, 8))
        if "l1f" in ph:
            self.h_wrtT = inp("wrtT", (1024, 8))
            self.h_wg = inp("wg", (8192, 2048))
            self.h_wu = inp("wu", (8192, 2048))
            self.h_wd = inp("wd", (16384, 1024))
            self.h_gb = inp("gb", (128, 128))
            self.h_ub = inp("ub", (128, 128))
            self.h_db = inp("db", (8, 1024))
            self.h_sel8 = inp("c_sel8", (8, 8, 128))

        self.xTb = nc.alloc_sbuf_tensor("xTb", [128, 16384], BF16)
        self.c_id = nc.alloc_sbuf_tensor("cid", [128, 128], F32)
        self.c_or_bf = nc.alloc_sbuf_tensor("corb", [1, 128], BF16)
        self.c_oc_bf = nc.alloc_sbuf_tensor("cocb", [128, 1], BF16)
        self.c_blk16 = nc.alloc_sbuf_tensor("cblk", [128, 8 * 16], BF16)
        self.c_sel16 = nc.alloc_sbuf_tensor("csel", [16, 8 * 128], BF16)
        self.AB = nc.alloc_sbuf_tensor("AB", [128, 68672], BF16)
        self.AF = nc.alloc_sbuf_tensor("AF", [128, 8296], F32)
        self.ps = nc.alloc_psum_tensor("PSB", [128, 4096], F32)

        self.lnva = 8296 - 4096
        self.lnvb = 8296 - 2048
        self.CONSTS = [1e-30, 1e-5, 1e-9, 0.125, -0.125]
        self.cval_base = None
        self.cvt = nc.alloc_sbuf_tensor("cvt", [128, len(self.CONSTS)], F32)

        self.n_init_dma = 6

    def build(self):
        nc = self.nc
        emitters = {"l0a": L0A, "l0f": L0F, "l1a": L1A, "l1f": L1F}
        phase_objs = []
        for i, name in enumerate(self.phases):
            P = emitters[name](self)
            P.is_first = (i == 0)
            P.is_last = (i == len(self.phases) - 1)
            phase_objs.append(P)

        bar = self.sem.bar
        nph = len(self.phases)

        totals = []
        for k, P in enumerate(phase_objs):
            self.sem.set_phase(k)
            tot = {}
            me = _MockEng(tot)
            for fn in (P.et, P.ev, P.es, P.eg, P.esy):
                fn(me)
            totals.append(tot)
        pool_nums = {h.num for h in self.sem.pool}
        bases = []
        cum = {}
        for k in range(nph):
            bases.append(dict(cum))
            for num, n in totals[k].items():
                if num in pool_nums:
                    cum[num] = cum.get(num, 0) + n

        for k, (name, P) in enumerate(zip(self.phases, phase_objs)):
            self.sem.set_phase(k)
            with nc.allow_low_precision(reason="bf16 stream/operands by design"), \
                 nc.Block(f"ph_{name}") as blk:
                def mk(fn, k=k):
                    def run(eng):
                        pe = _EngProxy(eng, bases[k])
                        if k > 0:
                            eng.wait_ge(bar, 5 * k)
                        fn(pe)
                        eng.sem_inc(bar, 1)
                    return run
                blk.tensor(mk(P.et))
                blk.vector(mk(P.ev))
                blk.scalar(mk(P.es))
                blk.gpsimd(mk(P.eg))
                blk.sync(mk(P.esy))
        return nc

    def emit_init_loads(self, g):
        s = self.sem
        n0 = 6
        g.dma_start(out=self.c_id[:, :], in_=self.h_id).then_inc(s.ld, 16)
        g.dma_start(out=self.c_or_bf[:, :], in_=self.h_ones_row).then_inc(s.ld, 16)
        g.dma_start(out=self.c_oc_bf[:, :], in_=self.h_ones_col).then_inc(s.ld, 16)
        g.dma_start(out=self.c_blk16[:, :],
                    in_=self.h_blk16.rearrange("m p k -> p m k")).then_inc(s.ld, 16)
        g.dma_start(out=self.c_sel16[:, :],
                    in_=self.h_sel16.rearrange("m q k -> q m k")).then_inc(s.ld, 16)
        if self.first:
            g.dma_start(out=self.xTb[:, :],
                        in_=self.h_x.rearrange("(c p) t -> p c t", p=128)).then_inc(s.ld, 16)
        else:
            g.dma_start(out=self.xTb[:, :], in_=self.h_x).then_inc(s.ld, 16)
        assert self.n_init_dma == n0

    def ld_val(self):
        return 16 * self.n_init_dma

    def emit_cval_sets(self, v):
        for j, val in enumerate(self.CONSTS):
            ins = v.memset(self.cvt[:, j:j + 1], val)
        ins.then_inc(self.sem.cvals, 1)

    def cval(self, val, p=128):
        j = self.CONSTS.index(val)
        return self.cvt[0:p, j:j + 1]



class L0A:
    def __init__(self, pr: Prog):
        self.pr = pr
        nc = pr.nc
        A = Arena(pr.AB)
        self.WT = A.take(8192)
        self.qhT = A.take(16384)
        self.khT = A.take(16384)
        self.vaug = A.take(16640)
        self.PT = self.WT
        self.sq4 = A.take(2048)
        self.rstd2 = A.take(1024)
        self.rec2 = A.take(1024)
        self.rbsb2 = A.take(1024)
        self.t1b2 = A.take(1024)
        self.mbf = self.rstd2
        self.rbf = self.rbsb2
        self.tt2 = self.vaug
        self.ln = LNBlock(pr, "a_", self.sq4, self.mbf, self.rbf, self.tt2)
        assert A.off <= pr.AB.shape[1]

        Fa = Arena(pr.AF)
        self.vtmp = Fa.take(1024)
        self.ns2 = Fa.take(1024)
        self.grep = Fa.take(1)
        self.nabsg = Fa.take(1)
        self.lnva = Fa.take(2048)
        self.lnvb = Fa.take(2048)
        self.Fa = Fa

    def et(self, t):
        pr, s = self.pr, self.pr.sem
        nc = pr.nc
        ps = pr.ps
        AB, AF = pr.AB, pr.AF
        t.wait_ge(s.ld, pr.ld_val())

        def main_mm(a, w, qt, m):
            prior = a - 4
            if prior >= 0:
                if prior < 64:
                    t.wait_ge(s.a_ev, prior + 1)
                else:
                    t.wait_ge(s.a_vtmp, prior - 64 + 1)
            t.wait_ge(s.a_wl, 16 * (w + 1))
            for e in range(8):
                ins = t.matmul(bank(ps, a % 4),
                               AB[0:128, self.WT + 1024 * e + 128 * m:self.WT + 1024 * e + 128 * m + 128],
                               pr.xTb[0:128, 2048 * e + 512 * qt:2048 * e + 512 * qt + 512],
                               start=(e == 0), stop=(e == 7))
            ins.then_inc(s.a_mm, 1)

        def nmm(b, G, m):
            if m == 0 and G >= 2:
                t.wait_ge(s.a_sqrt, G - 1)
            t.wait_ge(s.a_sq, b + 1)
            t.matmul(bank(ps, 4 + G % 2, 0, 16),
                     pr.c_blk16[0:128, 16 * m:16 * m + 16],
                     AB[0:128, self.sq4 + 512 * (b % 4):self.sq4 + 512 * (b % 4) + 512],
                     start=(m == 0), stop=(m == 7)).then_inc(s.a_nmm, 1)

        def bmm(G, m):
            bm = G * 8 + m
            t.wait_ge(s.a_rcp, G + 1)
            if bm >= 2:
                t.wait_ge(s.a_nml, bm - 1)
            t.matmul(bank(ps, 6 + m % 2),
                     pr.c_sel16[0:16, 128 * m:128 * m + 128],
                     AB[0:16, self.rstd2 + 512 * (G % 2):self.rstd2 + 512 * (G % 2) + 512],
                     start=True, stop=True).then_inc(s.a_bmm, 1)

        nqk = 64
        for b in range(nqk + 9):
            if b < nqk:
                w, r = divmod(b, 32)
                qt, m = divmod(r, 8)
                main_mm(b, w, qt, m)
            if 1 <= b + 0 and b - 1 < nqk and b - 1 >= 0:
                G, m = divmod(b - 1, 8)
                nmm(b - 1, G, m)
            if b - 9 >= 0 and b - 9 < nqk:
                G, m = divmod(b - 9, 8)
                bmm(G, m)

        def v_tr(vi, qt, m):
            if vi < 2:
                t.wait_ge(s.a_sqrt, 7 + vi)
            t.wait_ge(s.a_vtmp, vi + 1)
            if vi >= 2:
                t.wait_ge(s.a_vaug, vi - 1)
            for j in range(4):
                ins = t.transpose(bank(ps, 4 + vi % 2, 0, 128, 128 * j, 128 * j + 128),
                                  AF[0:128, self.vtmp + 512 * (vi % 2) + 128 * j:
                                     self.vtmp + 512 * (vi % 2) + 128 * j + 128],
                                  pr.c_id[:, :])
            ins.then_inc(s.a_vtr, 1)

        for vi in range(32 + 2):
            if vi < 32:
                qt, m = divmod(vi, 8)
                main_mm(64 + vi, 2, qt, m)
            if vi - 2 >= 0:
                qt, m = divmod(vi - 2, 8)
                v_tr(vi - 2, qt, m)

        t.wait_ge(s.a_nml, 64)
        t.wait_ge(s.a_vaug, 32)

        def sc(j, ktg):
            h, qt = divmod(j, 4)
            p, hb = h // 2, 64 * (h % 2)
            g = 4 * j + ktg
            if g >= 1:
                t.wait_ge(s.a_exp, g)
            for kseq in range(4):
                kt = 4 * ktg + kseq
                ins = t.matmul(bank(ps, kseq),
                               AB[hb:hb + 64, self.khT + 2048 * p + 128 * kt:
                                  self.khT + 2048 * p + 128 * kt + 128],
                               AB[hb:hb + 64, self.qhT + 2048 * p + 512 * qt:
                                  self.qhT + 2048 * p + 512 * qt + 512],
                               start=True, stop=True, tile_position=(hb, 0))
            ins.then_inc(s.a_smm, 1)

        def av(j, ktg):
            h, qt = divmod(j, 4)
            p, half = h // 2, h % 2
            t.wait_ge(s.a_exp, 4 * j + ktg + 1)
            if ktg == 0 and j >= 2:
                t.wait_ge(s.a_t1, j - 1)
            for kseq in range(4):
                kt = 4 * ktg + kseq
                ins = t.matmul(bank(ps, 4 + 2 * (j % 2), 0, 65),
                               AB[0:128, self.vaug + 2080 * p + 130 * kt + 65 * half:
                                  self.vaug + 2080 * p + 130 * kt + 65 * half + 65],
                               AB[0:128, self.PT + 512 * kt:self.PT + 512 * kt + 512],
                               start=(ktg == 0 and kseq == 0), stop=(ktg == 3 and kseq == 3))
            ins.then_inc(s.a_avm, 1)

        def rb(j):
            t.wait_ge(s.a_rec, j + 1)
            if j >= 2:
                t.wait_ge(s.a_t1, j - 1)
            t.matmul(bank(ps, 5 + 2 * (j % 2), 0, 64),
                     pr.c_or_bf[0:1, 0:64],
                     AB[0:1, self.rec2 + 512 * (j % 2):self.rec2 + 512 * (j % 2) + 512],
                     start=True, stop=True).then_inc(s.a_rbm, 1)

        NT = 64
        for j in range(NT + 1):
            for ktg in range(4):
                if j - 1 >= 0:
                    av(j - 1, ktg)
                if j < NT:
                    sc(j, ktg)
            if j - 1 >= 0:
                rb(j - 1)

        self.ln.et(t, gate=(s.a_add, NT))

    def ev(self, v):
        pr, s = self.pr, self.pr.sem
        ps = pr.ps
        AB, AF = pr.AB, pr.AF
        v.wait_ge(s.ld, pr.ld_val())
        if self.is_first:
            pr.emit_cval_sets(v)
        v.wait_ge(s.a_gabs, 1)
        v.tensor_scalar_mul(AF[0:128, self.nabsg:self.nabsg + 1],
                            AF[0:128, self.nabsg:self.nabsg + 1], -1.0).then_inc(s.a_gneg, 1)

        vv = AB[0:128, self.vaug:self.vaug + 16640].rearrange("P (p k c) -> P p k c", p=8, c=130)
        v.memset(vv[:, :, :, 64:65], 1.0)
        v.memset(vv[:, :, :, 129:130], 1.0).then_inc(s.a_vone, 1)

        def copy_sq(b):
            w, r = divmod(b, 32)
            qt, m = divmod(r, 8)
            dst = (self.qhT if w == 0 else self.khT) + 2048 * m + 512 * qt
            v.wait_ge(s.a_mm, b + 1)
            v.tensor_copy(AB[0:128, dst:dst + 512], bank(ps, b % 4)).then_inc(s.a_ev, 1)
            if b >= 4:
                v.wait_ge(s.a_nmm, b - 3)
            v.wait_ge(s.a_ev, b + 1)
            v.tensor_mul(AB[0:128, self.sq4 + 512 * (b % 4):self.sq4 + 512 * (b % 4) + 512],
                         AB[0:128, dst:dst + 512],
                         AB[0:128, dst:dst + 512]).then_inc(s.a_sq, 1)

        def recip(G):
            v.wait_ge(s.a_sqrt, G + 1)
            if G >= 2:
                v.wait_ge(s.a_bmm, 8 * G - 8)
            v.reciprocal(AB[0:16, self.rstd2 + 512 * (G % 2):self.rstd2 + 512 * (G % 2) + 512],
                         AF[0:16, self.ns2 + 512 * (G % 2):self.ns2 + 512 * (G % 2) + 512]).then_inc(s.a_rcp, 1)

        def nmult(bm):
            G, m = divmod(bm, 8)
            w, r = divmod(bm, 32)
            qt = r // 8
            dst = (self.qhT if w == 0 else self.khT) + 2048 * m + 512 * qt
            v.wait_ge(s.a_bmm, bm + 1)
            v.tensor_tensor(AB[0:128, dst:dst + 512],
                            AB[0:128, dst:dst + 512],
                            bank(ps, 6 + m % 2), ALU.mult).then_inc(s.a_nml, 1)

        for b in range(64 + 9):
            if b < 64:
                copy_sq(b)
            if b >= 8 and (b % 8) == 0 and (b // 8) - 1 < 8:
                recip((b // 8) - 1)
            if b - 9 >= 0:
                nmult(b - 9)

        v.wait_ge(s.a_vone, 1)
        for vi in range(32):
            qt, m = divmod(vi, 8)
            v.wait_ge(s.a_vtr, vi + 1)
            for j in range(4):
                kt = 4 * qt + j
                base = self.vaug + 2080 * m + 130 * kt
                v.tensor_copy(AB[0:128, base:base + 64],
                              bank(ps, 4 + vi % 2, 0, 128, 128 * j, 128 * j + 64))
                ins = v.tensor_copy(AB[0:128, base + 65:base + 65 + 64],
                                    bank(ps, 4 + vi % 2, 0, 128, 128 * j + 64, 128 * j + 128))
            ins.then_inc(s.a_vaug, 1)

        def a2v(j):
            h, qt = divmod(j, 4)
            if j >= 2:
                v.wait_ge(s.a_rbm, j - 1)
            v.wait_ge(s.a_avm, 4 * j + 4)
            v.reciprocal(AB[0:1, self.rec2 + 512 * (j % 2):self.rec2 + 512 * (j % 2) + 512],
                         bank(ps, 4 + 2 * (j % 2), 64, 65)).then_inc(s.a_rec, 1)

        def a2t1(j):
            h, qt = divmod(j, 4)
            c, hb = h // 2, 64 * (h % 2)
            v.wait_ge(s.a_rbc, j + 1)
            t1 = AB[hb:hb + 64, self.t1b2 + 512 * (j % 2):self.t1b2 + 512 * (j % 2) + 512]
            v.tensor_tensor(t1,
                            bank(ps, 4 + 2 * (j % 2), 0, 64),
                            AB[0:64, self.rbsb2 + 512 * (j % 2):self.rbsb2 + 512 * (j % 2) + 512],
                            ALU.mult).then_inc(s.a_t1, 1)
            dst = 2048 * c + 512 * qt
            v.tensor_add(pr.xTb[hb:hb + 64, dst:dst + 512],
                         pr.xTb[hb:hb + 64, dst:dst + 512],
                         t1).then_inc(s.a_add, 1)

        for j in range(64 + 1):
            if j < 64:
                a2v(j)
            if j - 1 >= 0:
                a2t1(j - 1)

        self.ln.ev(v, gate=(s.a_add, 64))

    def es(self, sc):
        pr, s = self.pr, self.pr.sem
        ps = pr.ps
        AB, AF = pr.AB, pr.AF
        sc.wait_ge(s.ld, pr.ld_val())
        sc.wait_ge(s.a_pl, 16)
        sc.activation(AF[0:128, self.nabsg:self.nabsg + 1],
                      AF[0:128, self.grep:self.grep + 1], AFT.Abs).then_inc(s.a_gabs, 1)

        sc.wait_ge(s.cvals, 1)

        def sqrt_G(G):
            sc.wait_ge(s.a_nmm, 8 * G + 8)
            if G >= 2:
                sc.wait_ge(s.a_rcp, G - 1)
            sc.activation(AF[0:16, self.ns2 + 512 * (G % 2):self.ns2 + 512 * (G % 2) + 512],
                          bank(ps, 4 + G % 2, 0, 16), AFT.Sqrt, bias=pr.cval(1e-30, 16)).then_inc(s.a_sqrt, 1)

        for G in range(8):
            sqrt_G(G)

        def v_copy(vi):
            qt, m = divmod(vi, 8)
            a = 64 + vi
            sc.wait_ge(s.a_mm, a + 1)
            if vi >= 2:
                sc.wait_ge(s.a_vtr, vi - 1)
            sc.copy(AF[0:128, self.vtmp + 512 * (vi % 2):self.vtmp + 512 * (vi % 2) + 512],
                    bank(ps, a % 4)).then_inc(s.a_vtmp, 1)

        for vi in range(32):
            v_copy(vi)

        sc.wait_ge(s.a_gneg, 1)

        def expg(j, ktg):
            g = 4 * j + ktg
            sc.wait_ge(s.a_smm, g + 1)
            if j >= 1:
                sc.wait_ge(s.a_avm, 4 * (j - 1) + ktg + 1)
            sc.activation(AB[0:128, self.PT + 2048 * ktg:self.PT + 2048 * ktg + 2048],
                          pr.ps[0:128, 0:2048], AFT.Exp,
                          bias=AF[0:128, self.nabsg:self.nabsg + 1],
                          scale=AF[0:128, self.grep:self.grep + 1]).then_inc(s.a_exp, 1)

        def rbcopy(j):
            sc.wait_ge(s.a_rbm, j + 1)
            sc.copy(AB[0:64, self.rbsb2 + 512 * (j % 2):self.rbsb2 + 512 * (j % 2) + 512],
                    bank(ps, 5 + 2 * (j % 2), 0, 64)).then_inc(s.a_rbc, 1)

        for j in range(64 + 1):
            if j < 64:
                for ktg in range(4):
                    expg(j, ktg)
            if j - 1 >= 0:
                rbcopy(j - 1)

        self.ln.es(sc)

    def eg(self, g):
        pr, s = self.pr, self.pr.sem
        if self.is_first:
            pr.emit_init_loads(g)
        g.dma_start(out=pr.AF[0:128, self.grep:self.grep + 1], in_=pr.h_grep).then_inc(s.a_pl, 16)
        g.dma_start(out=pr.AB[0:128, self.WT:self.WT + 8192],
                    in_=pr.h_w0q.rearrange("(c p) f -> p c f", p=128)).then_inc(s.a_wl, 16)
        g.wait_ge(s.a_mm, 32)
        g.dma_start(out=pr.AB[0:128, self.WT:self.WT + 8192],
                    in_=pr.h_w0k.rearrange("(c p) f -> p c f", p=128)).then_inc(s.a_wl, 16)
        g.wait_ge(s.a_mm, 64)
        g.dma_start(out=pr.AB[0:128, self.WT:self.WT + 8192],
                    in_=pr.h_w0v.rearrange("(c p) f -> p c f", p=128)).then_inc(s.a_wl, 16)

    def esy(self, sy):
        pr, s = self.pr, self.pr.sem
        if self.is_last:
            sy.wait_ge(s.a_lap, 32)
            sy.dma_start(out=pr.h_out, in_=pr.xTb[:, :]).then_inc(s.a_out, 16)
            sy.wait_ge(s.a_out, 16)




class L0F:

    def __init__(self, pr: Prog):
        self.pr = pr
        nc = pr.nc
        A = Arena(pr.AB)
        self.WT = A.take(8192)
        self.linT = A.take(4096)
        self.bd = A.take(2048)
        self.xc = A.take(8192)
        self.xa = A.take(8192)
        self.full = A.take(16384)
        self.xsp = A.take(16384)
        self.sqb = A.take(2048)
        self.linb = self.sqb
        self.prm = A.take(256)
        self.cmb2 = A.take(1024)
        self.mbf = self.xa + 0
        self.rbf = self.xa + 2048
        self.tt2 = self.xa + 4096
        self.sq4 = self.xa + 5120
        self.ln = LNBlock(pr, "f_", self.sq4, self.mbf, self.rbf, self.tt2)
        assert A.off <= pr.AB.shape[1], A.off

        Fa = Arena(pr.AF)
        self.lg = Fa.take(2048)
        self.zap = Fa.take(2048)
        self.sml = Fa.take(64)
        self.parts = Fa.take(8)
        self.allr = Fa.take(8)
        self.bnv = Fa.take(16)
        assert Fa.off <= pr.lnva, Fa.off

        self.ccin = nc.dram_tensor("ccin", [128, 8], F32, kind="Internal")
        self.ccout = nc.dram_tensor("ccout", [128, 8], F32, kind="Internal")
        if DBG:
            self.d_xsp = nc.dram_tensor("d_xsp", [128, 16384], BF16, kind="ExternalOutput").ap()
            self.d_full = nc.dram_tensor("d_full", [128, 16384], BF16, kind="ExternalOutput").ap()
            self.d_xc = nc.dram_tensor("d_xc", [128, 8192], BF16, kind="ExternalOutput").ap()
            self.d_xa = nc.dram_tensor("d_xa", [128, 8192], BF16, kind="ExternalOutput").ap()
            self.d_lg = nc.dram_tensor("d_lg", [128, 2048], F32, kind="ExternalOutput").ap()
            self.d_zap = nc.dram_tensor("d_zap", [128, 2048], F32, kind="ExternalOutput").ap()
            self.d_bnv = nc.dram_tensor("d_bnv", [128, 16], F32, kind="ExternalOutput").ap()
            self.d_sml = nc.dram_tensor("d_sml", [128, 64], F32, kind="ExternalOutput").ap()

        self.vsc = 0

    def smlv(self, off, p=128, w=1):
        return self.pr.AF[0:p, self.sml + off:self.sml + off + w]

    def sem(self, name, br):
        return getattr(self.pr.sem, f"f_{name}{br}")

    def et(self, t):
        pr, s = self.pr, self.pr.sem
        ps, AB = pr.ps, pr.AB
        t.wait_ge(s.ld, pr.ld_val())

        for idx in range(16):
            br, r = divmod(idx, 8)
            ti, qt = divmod(r, 4)
            t.wait_ge(s.f_wsm, 176)
            if idx >= 2:
                t.wait_ge(s.f_cev, idx - 1)
            for cc in range(4):
                c = 4 * ti + cc
                t.matmul(bank(ps, idx % 2),
                         AB[0:128, self.bd + 1024 * br + 128 * c:self.bd + 1024 * br + 128 * c + 128],
                         pr.xTb[0:128, 2048 * c + 512 * qt:2048 * c + 512 * qt + 512],
                         start=(cc == 0), stop=(cc == 3)).then_inc(s.f_cmm, (1 if cc == 3 else 0))

        for idx in range(32):
            m, qt = divmod(idx, 4)
            t.wait_ge(s.f_wl, 16)
            if idx >= 2:
                t.wait_ge(s.f_fme, idx - 1)
            for e in range(8):
                t.matmul(bank(ps, 2 + idx % 2),
                         AB[0:128, self.WT + 1024 * e + 128 * m:self.WT + 1024 * e + 128 * m + 128],
                         pr.xTb[0:128, 2048 * e + 512 * qt:2048 * e + 512 * qt + 512],
                         start=(e == 0), stop=(e == 7)).then_inc(s.f_fmm, (1 if e == 7 else 0))

        def lmm(j):
            nt, br = divmod(j, 2)
            if nt == 0:
                t.wait_ge(s.f_gel, 2 * br + 2)
                t.wait_ge(s.f_lbl, 32)
                if br == 0:
                    t.wait_ge(s.f_cev, 16)
                else:
                    t.wait_ge(s.f_fme, 32)
            else:
                t.wait_ge(self.sem("exp", br), nt)
            for o in range(2):
                bk = bank(ps, 2 * br + o)
                t.matmul(bk, pr.c_or_bf[0:1, 0:128],
                         AB[0:1, self.linb + 1024 * br + 512 * o:
                            self.linb + 1024 * br + 512 * o + 512],
                         start=True, stop=False)
                for ti in range(2):
                    ins = t.matmul(bk,
                                   AB[0:128, self.xa + 4096 * br + 2048 * ti + 128 * nt:
                                      self.xa + 4096 * br + 2048 * ti + 128 * nt + 128],
                                   AB[0:128, self.linT + 2048 * br + 1024 * ti + 512 * o:
                                      self.linT + 2048 * br + 1024 * ti + 512 * o + 512],
                                   start=False, stop=(ti == 1))
            ins.then_inc(self.sem("lmm", br), 1)

        def mtr(j):
            nt, br = divmod(j, 2)
            t.wait_ge(self.sem("zs", br), nt + 1)
            if nt >= 1:
                t.wait_ge(self.sem("mev", br), nt)
            for q in range(8):
                bk = 4 + 2 * br + q // 4
                ins = t.transpose(bank(ps, bk, 0, 128, 128 * (q % 4), 128 * (q % 4) + 128),
                                  pr.AF[0:128, self.zap + 1024 * br + 128 * q:
                                        self.zap + 1024 * br + 128 * q + 128],
                                  pr.c_id[:, :])
            ins.then_inc(self.sem("mtr", br), 1)

        for k in range(34):
            if k < 32:
                lmm(k)
            if 0 <= k - 2 < 32:
                mtr(k - 2)

        for idx in range(32):
            m, qt = divmod(idx, 4)
            if idx == 0:
                t.wait_ge(s.f_mev0, 16)
                t.wait_ge(s.f_mev1, 16)
            if idx >= 4:
                t.wait_ge(s.f_cmb, idx - 3)
            for e in range(8):
                t.matmul(bank(ps, 4 + idx % 4),
                         AB[0:128, self.WT + 1024 * e + 128 * m:self.WT + 1024 * e + 128 * m + 128],
                         AB[0:128, self.xsp + 2048 * e + 512 * qt:self.xsp + 2048 * e + 512 * qt + 512],
                         start=(e == 0), stop=(e == 7)).then_inc(s.f_imm, (1 if e == 7 else 0))

        self.ln.et(t, gate=(s.f_cmb, 32))

    def ev(self, v):
        pr, s = self.pr, self.pr.sem
        ps, AB, AF = pr.ps, pr.AB, pr.AF
        v.wait_ge(s.ld, pr.ld_val())
        if self.is_first:
            pr.emit_cval_sets(v)
        self.vsc = 0

        for br in range(2):
            for ti in range(2):
                v.wait_ge(s.f_cev, 8 * br + 4 * ti + 4)
                xcv = AB[0:128, self.xc + 4096 * br + 2048 * ti:self.xc + 4096 * br + 2048 * ti + 2048]
                v.reduce_sum(AF[0:128, self.parts + 4 * br + ti:self.parts + 4 * br + ti + 1],
                             xcv, axis=AX.X)
                v.tensor_mul(AB[0:128, self.sqb:self.sqb + 2048], xcv, xcv)
                v.reduce_sum(AF[0:128, self.parts + 4 * br + 2 + ti:self.parts + 4 * br + 2 + ti + 1],
                             AB[0:128, self.sqb:self.sqb + 2048], axis=AX.X).then_inc(s.f_bnp, 1)

        v.wait_ge(s.f_ccd, 32)
        cN = 1.0 / 16384.0
        def vsn(ins):
            self.vsc += 1
            ins.then_inc(s.f_vsy, 1)
            return self.vsc
        def vs(ins):
            n = vsn(ins)
            v.wait_ge(s.f_vsy, n)
        for br in range(2):
            b0 = self.bnv + 8 * br
            mean = AF[0:128, b0:b0 + 2]
            var = AF[0:128, b0 + 2:b0 + 4]
            scl = AF[0:128, b0 + 4:b0 + 6]
            shf = AF[0:128, b0 + 6:b0 + 8]
            ar_s = AF[0:128, self.allr + 4 * br:self.allr + 4 * br + 2]
            ar_q = AF[0:128, self.allr + 4 * br + 2:self.allr + 4 * br + 4]
            vs(v.tensor_scalar_mul(mean, ar_s, cN))
            vs(v.tensor_scalar_mul(var, ar_q, cN))
            vs(v.tensor_mul(scl, mean, mean))
            v.tensor_sub(var, var, scl).then_inc(s.f_bvar, 1)
            v.wait_ge(s.f_bsd, br + 1)
            vs(v.reciprocal(scl, var))
            vs(v.tensor_mul(scl, scl, AB[0:128, self.prm + 4 + 8 * br:self.prm + 4 + 8 * br + 2]))
            vs(v.tensor_mul(shf, mean, scl))
            v.tensor_sub(shf, AB[0:128, self.prm + 8 + 8 * br:self.prm + 8 + 8 * br + 2],
                         shf).then_inc(s.f_bn, 1)

        self.r4_cnt = {}

        def Va(nt, br):
            v.wait_ge(self.sem("lmm", br), nt + 1)
            v.reduce_max(self.smlv(0 + br), ps[0:128, 1024 * br:1024 * br + 1024], axis=AX.X)
            v.tensor_scalar_mul(self.smlv(2 + br), self.smlv(0 + br),
                                -1.0).then_inc(self.sem("mx", br), 1)

        def rzR(nt, br):
            v.wait_ge(self.sem("exp", br), nt + 1)
            if nt >= 1:
                v.wait_ge(self.sem("zs", br), nt)
            v.reciprocal(self.smlv(6 + br), self.smlv(4 + br)).then_inc(self.sem("rz", br), 1)

        def Vc(nt, br):
            v.wait_ge(self.sem("mtr", br), nt + 1)
            for q in range(8):
                bk = 4 + 2 * br + q // 4
                bkv = bank(ps, bk, 0, 128, 128 * (q % 4), 128 * (q % 4) + 128)
                if br == 0:
                    ins = v.tensor_tensor(
                        AB[0:128, self.xsp + 2048 * q + 128 * nt:self.xsp + 2048 * q + 128 * nt + 128],
                        bkv, pr.xTb[0:128, 2048 * q + 128 * nt:2048 * q + 128 * nt + 128], ALU.mult)
                else:
                    fv = AB[0:128, self.full + 2048 * q + 128 * nt:self.full + 2048 * q + 128 * nt + 128]
                    ins = v.tensor_tensor(fv, bkv, fv, ALU.mult)
            ins.then_inc(self.sem("mev", br), 1)

        for nt in range(16):
            for br in range(2):
                Va(nt, br)
            if nt >= 1:
                for br in range(2):
                    Vc(nt - 1, br)
            lgv = [AF[0:128, self.lg + 1024 * b:self.lg + 1024 * b + 1024] for b in range(2)]
            zp = [AF[0:128, self.zap + 1024 * b:self.zap + 1024 * b + 1024] for b in range(2)]
            m8 = [self.smlv(8 + 8 * b, 128, 8) for b in range(2)]
            cnt = {}
            for half in range(4):
                for br in range(2):
                    if half == 0:
                        v.wait_ge(self.sem("exp", br), nt + 1)
                        if nt >= 1:
                            v.wait_ge(self.sem("mtr", br), nt)
                        src = lgv[br]
                    else:
                        v.wait_ge(s.f_vsy, cnt[(br, half - 1, 1)])
                        src = zp[br]
                    cnt[(br, half, 0)] = vsn(v.max(out=m8[br], in_=src))
                for br in range(2):
                    v.wait_ge(s.f_vsy, cnt[(br, half, 0)])
                    src = lgv[br] if half == 0 else zp[br]
                    cnt[(br, half, 1)] = vsn(
                        v.match_replace(out=zp[br], in_to_replace=m8[br],
                                        in_values=src, imm_value=0.0))
                    if half == 3:
                        self.r4_cnt[(nt, br)] = cnt[(br, 3, 1)]
            for br in range(2):
                rzR(nt, br)
            for br in range(2):
                lgv = AF[0:128, self.lg + 1024 * br:self.lg + 1024 * br + 1024]
                zp = AF[0:128, self.zap + 1024 * br:self.zap + 1024 * br + 1024]
                v.wait_ge(s.f_vsy, self.r4_cnt[(nt, br)])
                vs(v.tensor_sub(zp, lgv, zp))
                v.tensor_scalar_mul(zp, zp, self.smlv(6 + br)).then_inc(self.sem("zs", br), 1)
        for br in range(2):
            Vc(15, br)

        for idx in range(32):
            m, qt = divmod(idx, 4)
            v.wait_ge(s.f_imm, idx + 1)
            cm = AB[0:128, self.cmb2 + 512 * (idx % 2):self.cmb2 + 512 * (idx % 2) + 512]
            fv = AB[0:128, self.full + 2048 * m + 512 * qt:self.full + 2048 * m + 512 * qt + 512]
            v.scalar_tensor_tensor(cm, bank(ps, 4 + idx % 4),
                                   AB[0:128, self.prm + 16 + m:self.prm + 16 + m + 1],
                                   fv, ALU.add, ALU.add)
            xv = pr.xTb[0:128, 2048 * m + 512 * qt:2048 * m + 512 * qt + 512]
            v.tensor_add(xv, xv, cm).then_inc(s.f_cmb, 1)

        self.ln.ev(v, gate=(s.f_cmb, 32))

    def es(self, sc):
        pr, s = self.pr, self.pr.sem
        ps, AB, AF = pr.ps, pr.AB, pr.AF
        sc.wait_ge(s.ld, pr.ld_val())
        sc.wait_ge(s.cvals, 1)

        for idx in range(16):
            br, r = divmod(idx, 8)
            ti, qt = divmod(r, 4)
            sc.wait_ge(s.f_cmm, idx + 1)
            sc.add(AB[0:128, self.xc + 4096 * br + 2048 * ti + 512 * qt:
                      self.xc + 4096 * br + 2048 * ti + 512 * qt + 512],
                   bank(ps, idx % 2),
                   AB[0:128, self.prm + 0 + 8 * br + ti:self.prm + 0 + 8 * br + ti + 1]).then_inc(s.f_cev, 1)

        for idx in range(32):
            m, qt = divmod(idx, 4)
            sc.wait_ge(s.f_fmm, idx + 1)
            sc.copy(AB[0:128, self.full + 2048 * m + 512 * qt:self.full + 2048 * m + 512 * qt + 512],
                    bank(ps, 2 + idx % 2)).then_inc(s.f_fme, 1)

        for br in range(2):
            b0 = self.bnv + 8 * br
            sc.wait_ge(s.f_bvar, br + 1)
            sc.activation(AF[0:128, b0 + 2:b0 + 4], AF[0:128, b0 + 2:b0 + 4],
                          AFT.Sqrt, bias=pr.cval(1e-5, 128)).then_inc(s.f_bsd, 1)
        for br in range(2):
            sc.wait_ge(s.f_bn, br + 1)
            b0 = self.bnv + 8 * br
            for ti in range(2):
                sc.activation(AB[0:128, self.xa + 4096 * br + 2048 * ti:
                                 self.xa + 4096 * br + 2048 * ti + 2048],
                              AB[0:128, self.xc + 4096 * br + 2048 * ti:
                                 self.xc + 4096 * br + 2048 * ti + 2048],
                              AFT.Gelu,
                              bias=AF[0:128, b0 + 6 + ti:b0 + 6 + ti + 1],
                              scale=AF[0:128, b0 + 4 + ti:b0 + 4 + ti + 1]).then_inc(s.f_gel, 1)

        for nt in range(16):
            for br in range(2):
                sc.wait_ge(self.sem("mx", br), nt + 1)
                if nt >= 1:
                    sc.wait_ge(self.sem("zs", br), nt)
                    sc.wait_ge(self.sem("rz", br), nt)
                lgv = AF[0:128, self.lg + 1024 * br:self.lg + 1024 * br + 1024]
                sc.activation(lgv, ps[0:128, 1024 * br:1024 * br + 1024], AFT.Exp,
                              bias=self.smlv(2 + br),
                              accum_out=self.smlv(4 + br)).then_inc(self.sem("exp", br), 1)

        self.ln.es(sc)

    def eg(self, g):
        pr, s = self.pr, self.pr.sem
        AB, AF = pr.AB, pr.AF
        if self.is_first:
            pr.emit_init_loads(g)
        g.dma_start(out=AB[0:128, self.WT:self.WT + 8192],
                    in_=pr.h_w0f.rearrange("(c p) f -> p c f", p=128)).then_inc(s.f_wl, 16)
        g.dma_start(out=AB[0:128, self.prm + 0:self.prm + 2], in_=pr.h_cb_si).then_inc(s.f_wsm, 16)
        g.dma_start(out=AB[0:128, self.prm + 8:self.prm + 10], in_=pr.h_cb_so).then_inc(s.f_wsm, 16)
        g.dma_start(out=AB[0:128, self.prm + 4:self.prm + 6], in_=pr.h_gam_si).then_inc(s.f_wsm, 16)
        g.dma_start(out=AB[0:128, self.prm + 12:self.prm + 14], in_=pr.h_gam_so).then_inc(s.f_wsm, 16)
        g.dma_start(out=AB[0:128, self.prm + 6:self.prm + 8], in_=pr.h_bet_si).then_inc(s.f_wsm, 16)
        g.dma_start(out=AB[0:128, self.prm + 14:self.prm + 16], in_=pr.h_bet_so).then_inc(s.f_wsm, 16)
        g.dma_start(out=AB[0:128, self.prm + 16:self.prm + 24], in_=pr.h_bias0).then_inc(s.f_wsm, 16)
        g.dma_start(out=AB[0:128, self.bd:self.bd + 1024],
                    in_=pr.h_bd_si.rearrange("c p k -> p c k")).then_inc(s.f_wsm, 16)
        g.dma_start(out=AB[0:128, self.bd + 1024:self.bd + 2048],
                    in_=pr.h_bd_so.rearrange("c p k -> p c k")).then_inc(s.f_wsm, 16)
        g.dma_start(out=AB[0:128, self.linT:self.linT + 2048],
                    in_=pr.h_linT_si.rearrange("(c p) f -> p c f", p=128)).then_inc(s.f_wsm, 16)
        g.dma_start(out=AB[0:128, self.linT + 2048:self.linT + 4096],
                    in_=pr.h_linT_so.rearrange("(c p) f -> p c f", p=128)).then_inc(s.f_wsm, 16)

        g.wait_ge(s.f_bnp, 4)
        g.dma_start(out=AB[0:1, self.linb:self.linb + 1024],
                    in_=pr.h_linb_si).then_inc(s.f_lbl, 16)
        g.dma_start(out=AB[0:1, self.linb + 1024:self.linb + 2048],
                    in_=pr.h_linb_so).then_inc(s.f_lbl, 16)
        g.dma_start(out=self.ccin.ap(), in_=AF[0:128, self.parts:self.parts + 8]).then_inc(s.f_ccd, 16)
        g.wait_ge(s.f_ccd, 16)
        g.collective_compute("AllReduce", ALU.add,
                             replica_groups=[list(range(8))],
                             ins=[self.ccin.ap().opt()],
                             outs=[self.ccout.ap().opt()]).then_inc(s.f_cc, 1)
        g.wait_ge(s.f_cc, 1)
        g.dma_start(out=AF[0:128, self.allr:self.allr + 8], in_=self.ccout.ap()).then_inc(s.f_ccd, 16)


    def esy(self, sy):
        pr, s = self.pr, self.pr.sem
        if DBG:
            sy.wait_ge(s.f_cmb, 32)
            AB, AF = pr.AB, pr.AF
            sy.dma_start(out=self.d_xsp, in_=AB[0:128, self.xsp:self.xsp + 16384]).then_inc(s.f_dbg, 16)
            sy.dma_start(out=self.d_full, in_=AB[0:128, self.full:self.full + 16384]).then_inc(s.f_dbg, 16)
            sy.dma_start(out=self.d_xc, in_=AB[0:128, self.xc:self.xc + 8192]).then_inc(s.f_dbg, 16)
            sy.dma_start(out=self.d_xa, in_=AB[0:128, self.xa:self.xa + 8192]).then_inc(s.f_dbg, 16)
            sy.dma_start(out=self.d_lg, in_=AF[0:128, self.lg:self.lg + 2048]).then_inc(s.f_dbg, 16)
            sy.dma_start(out=self.d_zap, in_=AF[0:128, self.zap:self.zap + 2048]).then_inc(s.f_dbg, 16)
            sy.dma_start(out=self.d_bnv, in_=AF[0:128, self.bnv:self.bnv + 16]).then_inc(s.f_dbg, 16)
            sy.dma_start(out=self.d_sml, in_=AF[0:128, self.sml:self.sml + 64]).then_inc(s.f_dbg, 16)
            sy.wait_ge(s.f_dbg, 128)
        if self.is_last:
            sy.wait_ge(s.f_lap, 32)
            sy.dma_start(out=pr.h_out, in_=pr.xTb[:, :]).then_inc(s.f_out, 16)
            sy.wait_ge(s.f_out, 16)



class _PH:
    def __init__(self, pr):
        self.pr = pr
    def et(self, t): pass
    def ev(self, v): pass
    def es(self, sc): pass
    def eg(self, g):
        if self.is_first:
            self.pr.emit_init_loads(g)
    def esy(self, sy): pass



class L1A:

    def __init__(self, pr: Prog):
        self.pr = pr
        A = Arena(pr.AB)
        self.kT = A.take(16384)
        self.vtok = A.take(16384)
        self.Ost = A.take(16384)
        self.qTb = A.take(4096)
        self.PT = A.take(8192)
        self.wsl = A.take(4096)
        self.prm = A.take(16)
        self.mbf = self.PT + 0
        self.rbf = self.PT + 2048
        self.tt2 = self.PT + 4096
        self.sq4 = self.PT + 5120
        self.ln = LNBlock(pr, "b_", self.sq4, self.mbf, self.rbf, self.tt2)
        assert A.off <= pr.AB.shape[1], A.off

        Fa = Arena(pr.AF)
        self.P2 = Fa.take(4096)
        self.sml = Fa.take(64)
        assert Fa.off <= pr.lnva, Fa.off

    def slab_buf(self, i):
        if i < 4 and i % 2 == 1:
            return self.qTb
        return self.wsl

    def et(self, t):
        pr, s = self.pr, self.pr.sem
        ps, AB, AF = pr.ps, pr.AB, pr.AF
        t.wait_ge(s.ld, pr.ld_val())

        kv_tile = 0
        for sl in range(4):
            kind, half = divmod(sl, 2)
            buf = self.slab_buf(sl)
            t.wait_ge(s.b_wl, 16 * (sl + 1))
            if kind == 0:
                for mm in range(4):
                    m = 4 * half + mm
                    for qt in range(4):
                        if kv_tile >= 4:
                            t.wait_ge(s.b_kev, kv_tile - 3)
                        for e in range(8):
                            t.matmul(bank(ps, kv_tile % 4),
                                     AB[0:128, buf + 512 * e + 128 * mm:buf + 512 * e + 128 * mm + 128],
                                     pr.xTb[0:128, 2048 * e + 512 * qt:2048 * e + 512 * qt + 512],
                                     start=(e == 0), stop=(e == 7)).then_inc(
                                         s.b_kvm, (1 if e == 7 else 0))
                        kv_tile += 1
            else:
                for tt in range(16):
                    if kv_tile >= 4:
                        t.wait_ge(s.b_kev, kv_tile - 3)
                    for e in range(8):
                        t.matmul(bank(ps, kv_tile % 4),
                                 pr.xTb[0:128, 2048 * e + 128 * tt:2048 * e + 128 * tt + 128],
                                 AB[0:128, buf + 512 * e:buf + 512 * e + 512],
                                 start=(e == 0), stop=(e == 7)).then_inc(
                                     s.b_kvm, (1 if e == 7 else 0))
                    kv_tile += 1
            t.sem_inc(s.b_sdone, 1)

        for qb in range(4):
            for half in range(2):
                sl = 4 + 2 * qb + half
                t.wait_ge(s.b_wl, 16 * (sl + 1))
                for mm in range(4):
                    m = 4 * half + mm
                    if 8 * qb + m >= 2:
                        t.wait_ge(s.b_qev, 8 * qb + m - 1)
                    for e in range(8):
                        t.matmul(bank(ps, 4 + m % 2),
                                 AB[0:128, self.slab_buf(sl) + 512 * e + 128 * mm:
                                    self.slab_buf(sl) + 512 * e + 128 * mm + 128],
                                 pr.xTb[0:128, 2048 * e + 512 * qb:2048 * e + 512 * qb + 512],
                                 start=(e == 0), stop=(e == 7)).then_inc(
                                     s.b_qmm, (1 if e == 7 else 0))
                t.sem_inc(s.b_sdone, 1)
            for il in range(4):
                qg = 4 * qb + il
                t.wait_ge(s.b_qev, 8 * qb + 8)
                if qg == 0:
                    t.wait_ge(s.b_kev, 64)
                if qg >= 1:
                    t.wait_ge(s.b_exp, qg)
                for ks in range(4):
                    for m in range(8):
                        t.matmul(bank(ps, ks),
                                 AB[0:128, self.qTb + 512 * m + 128 * il:
                                    self.qTb + 512 * m + 128 * il + 128],
                                 AB[0:128, self.kT + 2048 * m + 512 * ks:
                                    self.kT + 2048 * m + 512 * ks + 512],
                                 start=(m == 0), stop=(m == 7)).then_inc(
                                     s.b_smm, (1 if (ks == 3 and m == 7) else 0))
                t.wait_ge(s.b_exp, qg + 1)
                if qg >= 1:
                    t.wait_ge(s.b_ptv, qg)
                for jt in range(16):
                    bk = 4 + jt // 4
                    ins = t.transpose(bank(ps, bk, 0, 128, 128 * (jt % 4), 128 * (jt % 4) + 128),
                                      AF[0:128, self.P2 + 2048 * (qg % 2) + 128 * jt:
                                         self.P2 + 2048 * (qg % 2) + 128 * jt + 128],
                                      pr.c_id[:, :])
                ins.then_inc(s.b_ptr, 1)
            t.wait_ge(s.b_ptv, 4 * qb + 4)
            for il in range(4):
                for hs in range(2):
                    gidx = 8 * qb + 2 * il + hs
                    if gidx >= 2:
                        t.wait_ge(s.b_oev, gidx - 1)
                    for jt in range(16):
                        t.matmul(bank(ps, gidx % 2),
                                 AB[0:128, self.PT + 512 * jt + 128 * il:
                                    self.PT + 512 * jt + 128 * il + 128],
                                 AB[0:128, self.vtok + 1024 * jt + 512 * hs:
                                    self.vtok + 1024 * jt + 512 * hs + 512],
                                 start=(jt == 0), stop=(jt == 15)).then_inc(
                                     s.b_avm, (1 if jt == 15 else 0))

        re = AB[0:128, self.Ost:self.Ost + 16384].rearrange("p (b rest) -> p rest b", b=2)
        t.wait_ge(s.b_oev, 32)
        for half in range(2):
            sl = 12 + half
            t.wait_ge(s.b_wl, 16 * (sl + 1))
            for mm in range(4):
                m = 4 * half + mm
                for st in range(4):
                    idx = 16 * half + 4 * mm + st
                    if idx >= 4:
                        t.wait_ge(s.b_pja, idx - 3)
                    for c in range(8):
                        t.matmul(bank(ps, idx % 4),
                                 AB[0:128, self.wsl + 512 * c + 128 * mm:
                                    self.wsl + 512 * c + 128 * mm + 128],
                                 re[:, 1024 * c + 256 * st:1024 * c + 256 * st + 256, :],
                                 start=(c == 0), stop=(c == 7)).then_inc(
                                     s.b_pjm, (1 if c == 7 else 0))
            t.sem_inc(s.b_sdone, 1)

        self.ln.et(t, gate=(s.b_pja, 32))

    def ev(self, v):
        pr, s = self.pr, self.pr.sem
        ps, AB, AF = pr.ps, pr.AB, pr.AF
        v.wait_ge(s.ld, pr.ld_val())
        if self.is_first:
            pr.emit_cval_sets(v)

        kv_tile = 0
        for sl in range(4):
            kind, half = divmod(sl, 2)
            if kind == 0:
                for mm in range(4):
                    m = 4 * half + mm
                    for qt in range(4):
                        v.wait_ge(s.b_kvm, kv_tile + 1)
                        v.tensor_copy(AB[0:128, self.kT + 2048 * m + 512 * qt:
                                         self.kT + 2048 * m + 512 * qt + 512],
                                      bank(ps, kv_tile % 4)).then_inc(s.b_kev, 1)
                        kv_tile += 1
            else:
                for tt in range(16):
                    v.wait_ge(s.b_kvm, kv_tile + 1)
                    v.tensor_copy(AB[0:128, self.vtok + 1024 * tt + 512 * half:
                                     self.vtok + 1024 * tt + 512 * half + 512],
                                  bank(ps, kv_tile % 4)).then_inc(s.b_kev, 1)
                    kv_tile += 1

        for qb in range(4):
            for m in range(8):
                if m == 0 and qb >= 1:
                    v.wait_ge(s.b_smm, 4 * qb)
                v.wait_ge(s.b_qmm, 8 * qb + m + 1)
                v.tensor_copy(AB[0:128, self.qTb + 512 * m:self.qTb + 512 * m + 512],
                              bank(ps, 4 + m % 2)).then_inc(s.b_qev, 1)
            for il in range(4):
                qg = 4 * qb + il
                v.wait_ge(s.b_smm, qg + 1)
                v.reduce_max(AF[0:128, self.sml + qg:self.sml + qg + 1],
                             ps[0:128, 0:2048], axis=AX.X)
                v.tensor_scalar_mul(AF[0:128, self.sml + 16 + qg:self.sml + 16 + qg + 1],
                                    AF[0:128, self.sml + qg:self.sml + qg + 1],
                                    -0.125).then_inc(s.b_mx, 1)
                v.wait_ge(s.b_ptr, qg + 1)
                if qb >= 1 and il == 0:
                    v.wait_ge(s.b_avm, 8 * qb)
                for jt in range(16):
                    bk = 4 + jt // 4
                    ins = v.tensor_copy(AB[0:128, self.PT + 512 * jt + 128 * il:
                                           self.PT + 512 * jt + 128 * il + 128],
                                        bank(ps, bk, 0, 128, 128 * (jt % 4), 128 * (jt % 4) + 128))
                ins.then_inc(s.b_ptv, 1)
                v.wait_ge(s.b_exp, qg + 1)
                v.reciprocal(AF[0:128, self.sml + 48 + qg:self.sml + 48 + qg + 1],
                             AF[0:128, self.sml + 32 + qg:self.sml + 32 + qg + 1]).then_inc(s.b_rz, 1)
            for il in range(4):
                for hs in range(2):
                    gidx = 8 * qb + 2 * il + hs
                    qg = 4 * qb + il
                    v.wait_ge(s.b_rz, qg + 1)
                    v.wait_ge(s.b_avm, gidx + 1)
                    tt = 4 * qb + il
                    v.tensor_scalar_mul(AB[0:128, self.Ost + 1024 * tt + 512 * hs:
                                           self.Ost + 1024 * tt + 512 * hs + 512],
                                        bank(ps, gidx % 2),
                                        AF[0:128, self.sml + 48 + qg:self.sml + 48 + qg + 1]).then_inc(s.b_oev, 1)

        v.wait_ge(s.b_pl, 16)
        for half in range(2):
            for mm in range(4):
                m = 4 * half + mm
                for st in range(4):
                    idx = 16 * half + 4 * mm + st
                    v.wait_ge(s.b_pjm, idx + 1)
                    xv = pr.xTb[0:128, 2048 * m + 512 * st:2048 * m + 512 * st + 512]
                    v.scalar_tensor_tensor(xv, bank(ps, idx % 4),
                                           AB[0:128, self.prm + m:self.prm + m + 1],
                                           xv, ALU.add, ALU.add).then_inc(s.b_pja, 1)

        self.ln.ev(v, gate=(s.b_pja, 32))

    def es(self, sc):
        pr, s = self.pr, self.pr.sem
        ps, AF = pr.ps, pr.AF
        sc.wait_ge(s.ld, pr.ld_val())
        sc.wait_ge(s.cvals, 1)
        for qg in range(16):
            sc.wait_ge(s.b_mx, qg + 1)
            if qg >= 2:
                sc.wait_ge(s.b_ptr, qg - 1)
            sc.activation(AF[0:128, self.P2 + 2048 * (qg % 2):self.P2 + 2048 * (qg % 2) + 2048],
                          ps[0:128, 0:2048], AFT.Exp,
                          bias=AF[0:128, self.sml + 16 + qg:self.sml + 16 + qg + 1],
                          scale=pr.cval(0.125, 128),
                          accum_out=AF[0:128, self.sml + 32 + qg:self.sml + 32 + qg + 1]).then_inc(s.b_exp, 1)
        self.ln.es(sc)

    def eg(self, g):
        pr, s = self.pr, self.pr.sem
        AB = pr.AB
        if self.is_first:
            pr.emit_init_loads(g)
        g.dma_start(out=AB[0:128, self.prm:self.prm + 8], in_=pr.h_bprj).then_inc(s.b_pl, 16)
        def load_slab(i, src_ap):
            buf = self.slab_buf(i)
            g.dma_start(out=AB[0:128, buf:buf + 4096], in_=src_ap).then_inc(s.b_wl, 16)
        order = []
        order.append(pr.h_wqkvT[:, 1024:1536])
        order.append(pr.h_wqkvT[:, 1536:2048])
        order.append(pr.h_wqkvT[:, 2048:2560])
        order.append(pr.h_wqkvT[:, 2560:3072])
        for qb in range(4):
            order.append(pr.h_wqkvT[:, 0:512])
            order.append(pr.h_wqkvT[:, 512:1024])
        order.append(pr.h_wprjT[:, 0:512])
        order.append(pr.h_wprjT[:, 512:1024])
        for i, src in enumerate(order):
            if i in (2, 3):
                g.wait_ge(s.b_sdone, i - 1)
            elif i >= 4:
                g.wait_ge(s.b_sdone, i)
            load_slab(i, src.rearrange("(c p) f -> p c f", p=128))

    def esy(self, sy):
        pr, s = self.pr, self.pr.sem
        if self.is_last:
            sy.wait_ge(s.b_lap, 32)
            sy.dma_start(out=pr.h_out, in_=pr.xTb[:, :]).then_inc(s.b_out, 16)
            sy.wait_ge(s.b_out, 16)



class L1F:
    def __init__(self, pr: Prog):
        self.pr = pr
        A = Arena(pr.AB)
        self.RING = A.take(40960)
        self.yac = A.take(16384)
        self.gtr = A.take(2048)
        self.gbs = A.take(2048)
        self.ht = A.take(4096)
        self.tu = A.take(1024)

        self.db = A.take(1024)
        self.sel8 = A.take(1024)
        self.wrt = A.take(64)
        self.mbf = self.RING + 0
        self.rbf = self.RING + 2048
        self.tt2 = self.RING + 4096
        self.sq4 = self.RING + 5120
        self.ln = LNBlock(pr, "c_", self.sq4, self.mbf, self.rbf, self.tt2,
                          final=pr.last, out_scr=None)
        assert A.off <= pr.AB.shape[1], A.off

        Fa = Arena(pr.AF)
        if DBG:
            self.d_gtr = pr.nc.dram_tensor("d_gtr", [8, 2048], BF16, kind="ExternalOutput").ap()
            self.d_yac = pr.nc.dram_tensor("d_yac", [128, 16384], BF16, kind="ExternalOutput").ap()
            self.d_gtt = pr.nc.dram_tensor("d_gtt", [128, 128], F32, kind="ExternalOutput").ap()
            self.d_rtl = pr.nc.dram_tensor("d_rtl", [128, 128], F32, kind="ExternalOutput").ap()
            self.d_rte = pr.nc.dram_tensor("d_rte", [128, 128], F32, kind="ExternalOutput").ap()
        self.rtl = Fa.take(128)
        self.rte = Fa.take(128)
        self.sml = Fa.take(96)
        self.m8 = Fa.take(8)
        self.gtt = Fa.take(128)
        self.gbp = Fa.take(128)
        self.ubp = Fa.take(128)
        self.osc = pr.lnva
        self.ln.out_scr = self.osc
        assert Fa.off <= pr.lnva, Fa.off

    def slab(self, k):
        return self.RING + 8192 * (k % 5)

    def et(self, t):
        pr, s = self.pr, self.pr.sem
        ps, AB, AF = pr.ps, pr.AB, pr.AF
        t.wait_ge(s.ld, pr.ld_val())

        for tt in range(16):
            t.wait_ge(s.c_pl, 16 * 5)
            if tt >= 1:
                t.wait_ge(s.c_rcp, tt)
            for e8 in range(8):
                t.matmul(bank(ps, 7, 0, 128, 0, 8),
                         pr.xTb[0:128, 2048 * e8 + 128 * tt:2048 * e8 + 128 * tt + 128],
                         AB[0:128, self.wrt + 8 * e8:self.wrt + 8 * e8 + 8],
                         start=(e8 == 0), stop=(e8 == 7)).then_inc(
                             s.c_rmm, (1 if e8 == 7 else 0))
            j = tt - 1
            if j >= 0:
                t.wait_ge(s.c_rg, j + 1)
                if j >= 2:
                    t.wait_ge(s.c_gte, j - 1)
                t.transpose(bank(ps, 4 + j % 2, 0, 8, 0, 128),
                            AF[0:128, self.gtt + 8 * j:self.gtt + 8 * j + 8],
                            pr.c_id[:, :]).then_inc(s.c_gtm, 1)
        t.wait_ge(s.c_rg, 16)
        t.wait_ge(s.c_gte, 15)
        t.transpose(bank(ps, 5, 0, 8, 0, 128),
                    AF[0:128, self.gtt + 8 * 15:self.gtt + 8 * 15 + 8],
                    pr.c_id[:, :]).then_inc(s.c_gtm, 1)

        def down_group(T, p, fd, sip):
            dslab = self.slab(3 * sip + 2) + 1024 * fd
            for dc in range(4 * p, 4 * p + 4):
                t.matmul(bank(ps, dc % 4),
                         AB[0:128, dslab + 128 * dc:dslab + 128 * dc + 128],
                         AB[0:128, self.ht + 512 * fd:self.ht + 512 * fd + 512],
                         start=(fd == 0 and sip > 0),
                         stop=(fd == 7)).then_inc(s.c_dmf, (1 if dc == 4 * p + 3 else 0))

        def db_group(T, p):
            ntp = T % 4
            for dc in range(4 * p, 4 * p + 4):
                t.matmul(bank(ps, dc % 4),
                         AB[0:8, self.db + 128 * dc:self.db + 128 * dc + 128],
                         AB[0:8, self.gtr + 512 * ntp:self.gtr + 512 * ntp + 512],
                         start=True, stop=False)

        def sweep(Tp):
            sip = Tp // 4
            t.wait_ge(s.c_ya, 8 * Tp + 4)
            if sip == 0:
                db_group(Tp, 1)
            for fd in range(8):
                down_group(Tp, 1, fd, sip)

        for e in range(8):
            t.wait_ge(s.c_gte, 16)
            if e >= 1:
                t.wait_ge(s.c_tu, 64 * e)
            for nt in range(4):
                if nt >= 2 or e >= 1:
                    t.wait_ge(s.c_gbe, 4 * e + nt - 1)
                t.matmul(bank(ps, 6 + nt % 2),
                         AB[0:8, self.sel8 + 128 * e:self.sel8 + 128 * e + 128],
                         AB[0:8, self.gtr + 512 * nt:self.gtr + 512 * nt + 512],
                         start=True, stop=True).then_inc(s.c_gbm, 1)
            for fch in range(2):
                si = 2 * e + fch
                t.wait_ge(s.c_wl, 32 * si + 32)
                for nt in range(4):
                    T = 4 * si + nt
                    for fc in range(10):
                        if fc < 8:
                            cnt = 8 * T + fc
                            if cnt >= 2:
                                t.wait_ge(s.c_sil, cnt - 1)
                            if fch == 0 and nt == 0 and fc == 0:
                                t.wait_ge(s.c_gbe, 4 * (e + 1))
                            wslab = self.slab(3 * si) + 128 * fc
                            uslab = self.slab(3 * si + 1) + 128 * fc
                            for dc in range(8):
                                t.matmul(bank(ps, 4 + fc % 2),
                                         AB[0:128, wslab + 1024 * dc:wslab + 1024 * dc + 128],
                                         pr.xTb[0:128, 2048 * dc + 512 * nt:2048 * dc + 512 * nt + 512],
                                         start=(dc == 0), stop=(dc == 7)).then_inc(
                                             s.c_gmm, (1 if dc == 7 else 0))
                            if cnt >= 2:
                                t.wait_ge(s.c_tu, cnt - 1)
                            for dc in range(8):
                                t.matmul(bank(ps, 6 + fc % 2),
                                         AB[0:128, uslab + 1024 * dc:uslab + 1024 * dc + 128],
                                         pr.xTb[0:128, 2048 * dc + 512 * nt:2048 * dc + 512 * nt + 512],
                                         start=(dc == 0), stop=(dc == 7)).then_inc(
                                             s.c_umm, (1 if dc == 7 else 0))
                        if fc == 1 and T >= 1:
                            sweep(T - 1)
                        fd = fc - 2
                        if fd >= 0:
                            t.wait_ge(s.c_h, 8 * T + fd + 1)
                            if fd == 0:
                                t.wait_ge(s.c_ya, 8 * T)
                                if nt == 0:
                                    t.wait_ge(s.c_wdl, 16 * si + 16)
                                if si == 0:
                                    db_group(T, 0)
                            down_group(T, 0, fd, si)
        sweep(63)

        self.ln.et(t, gate=(s.c_rs, 8))

    def ev(self, v):
        pr, s = self.pr, self.pr.sem
        ps, AB, AF = pr.ps, pr.AB, pr.AF
        v.wait_ge(s.ld, pr.ld_val())
        if self.is_first:
            pr.emit_cval_sets(v)

        self.vsc = 0
        m8 = AF[0:128, self.m8:self.m8 + 8]
        for tt in range(16):
            mx = AF[0:128, self.sml + tt:self.sml + tt + 1]
            mxn = AF[0:128, self.sml + 16 + tt:self.sml + 16 + tt + 1]
            Z = AF[0:128, self.sml + 32 + tt:self.sml + 32 + tt + 1]
            thr = AF[0:128, self.sml + 48 + tt:self.sml + 48 + tt + 1]
            den = AF[0:128, self.sml + 64 + tt:self.sml + 64 + tt + 1]
            rden = AF[0:128, self.sml + 80 + tt:self.sml + 80 + tt + 1]
            gtt = AF[0:128, self.gtt + 8 * tt:self.gtt + 8 * tt + 8]
            rt = AF[0:128, self.rtl + 8 * tt:self.rtl + 8 * tt + 8]
            def vs(ins):
                self.vsc += 1
                ins.then_inc(s.c_vsy, 1)
                v.wait_ge(s.c_vsy, self.vsc)
            v.wait_ge(s.c_rmm, tt + 1)
            v.tensor_copy(rt, bank(ps, 7, 0, 128, 0, 8)).then_inc(s.c_rcp, 1)
            v.wait_ge(s.c_rcp, tt + 1)
            vs(v.reduce_max(mx, rt, axis=AX.X))
            v.tensor_scalar_mul(mxn, mx, -1.0).then_inc(s.c_rmx, 1)
            rte = AF[0:128, self.rte + 8 * tt:self.rte + 8 * tt + 8]
            v.wait_ge(s.c_rex, tt + 1)
            vs(v.max(out=m8, in_=rte))
            vs(v.tensor_copy(thr, AF[0:128, self.m8 + 1:self.m8 + 2]))
            vs(v.tensor_single_scalar(gtt, rte, thr, ALU.is_ge))
            vs(v.tensor_mul(gtt, gtt, rte))
            vs(v.reduce_sum(den, gtt, axis=AX.X))
            vs(v.tensor_scalar_mul(Z, Z, 1e-9))
            vs(v.tensor_add(den, den, Z))
            vs(v.reciprocal(rden, den))
            v.tensor_scalar_mul(gtt, gtt, rden).then_inc(s.c_rg, 1)
            j = tt - 1
            if j >= 0:
                v.wait_ge(s.c_gtm, j + 1)
                v.tensor_copy(AB[0:8, self.gtr + 128 * j:self.gtr + 128 * j + 128],
                              bank(ps, 4 + j % 2, 0, 8, 0, 128)).then_inc(s.c_gte, 1)
        v.wait_ge(s.c_gtm, 16)
        v.tensor_copy(AB[0:8, self.gtr + 128 * 15:self.gtr + 128 * 15 + 128],
                      bank(ps, 5, 0, 8, 0, 128)).then_inc(s.c_gte, 1)

        def y_evac(T, p):
            sip, ntp = divmod(T, 4)
            for dc in range(4 * p, 4 * p + 4):
                yv = AB[0:128, self.yac + 2048 * dc + 512 * ntp:
                        self.yac + 2048 * dc + 512 * ntp + 512]
                pv = bank(ps, dc % 4)
                if sip == 0:
                    v.tensor_copy(yv, pv).then_inc(s.c_ya, 1)
                else:
                    v.tensor_add(yv, yv, pv).then_inc(s.c_ya, 1)

        for e in range(8):
            for fch in range(2):
                si = 2 * e + fch
                for nt in range(4):
                    T = 4 * si + nt
                    v.wait_ge(s.c_gbe, 4 * e + nt + 1)
                    for fc in range(8):
                        cnt = 8 * T + fc
                        v.wait_ge(s.c_umm, cnt + 1)
                        tuv = AB[0:128, self.tu + 512 * (fc % 2):self.tu + 512 * (fc % 2) + 512]
                        v.scalar_tensor_tensor(tuv, bank(ps, 6 + fc % 2),
                                               AF[0:128, self.ubp + 16 * e + 8 * fch + fc:
                                                  self.ubp + 16 * e + 8 * fch + fc + 1],
                                               AB[0:128, self.gbs + 512 * nt:self.gbs + 512 * nt + 512],
                                               ALU.add, ALU.mult).then_inc(s.c_tu, 1)
                        v.wait_ge(s.c_tu, cnt + 1)
                        v.wait_ge(s.c_sil, cnt + 1)
                        htv = AB[0:128, self.ht + 512 * fc:self.ht + 512 * fc + 512]
                        v.tensor_mul(htv, htv, tuv).then_inc(s.c_h, 1)
                        if fc == 1 and T >= 1:
                            v.wait_ge(s.c_dmf, 16 * T)
                            y_evac(T - 1, 1)
                    v.wait_ge(s.c_dmf, 16 * T + 8)
                    y_evac(T, 0)
        v.wait_ge(s.c_dmf, 1024)
        y_evac(63, 1)

        for dc in range(8):
            v.wait_ge(s.c_ya, 512)
            v.tensor_add(pr.xTb[0:128, 2048 * dc:2048 * dc + 2048],
                         pr.xTb[0:128, 2048 * dc:2048 * dc + 2048],
                         AB[0:128, self.yac + 2048 * dc:self.yac + 2048 * dc + 2048]).then_inc(s.c_rs, 1)

        self.ln.ev(v, gate=(s.c_rs, 8))

    def es(self, sc):
        pr, s = self.pr, self.pr.sem
        ps, AB, AF = pr.ps, pr.AB, pr.AF
        sc.wait_ge(s.ld, pr.ld_val())
        sc.wait_ge(s.cvals, 1)
        for tt in range(16):
            rt = AF[0:128, self.rtl + 8 * tt:self.rtl + 8 * tt + 8]
            sc.wait_ge(s.c_rmx, tt + 1)
            sc.activation(AF[0:128, self.rte + 8 * tt:self.rte + 8 * tt + 8], rt, AFT.Exp,
                          bias=AF[0:128, self.sml + 16 + tt:self.sml + 16 + tt + 1],
                          accum_out=AF[0:128, self.sml + 32 + tt:self.sml + 32 + tt + 1]).then_inc(s.c_rex, 1)
        for e in range(8):
            for nt in range(4):
                sc.wait_ge(s.c_gbm, 4 * e + nt + 1)
                sc.copy(AB[0:128, self.gbs + 512 * nt:self.gbs + 512 * nt + 512],
                        bank(ps, 6 + nt % 2)).then_inc(s.c_gbe, 1)
            for fch in range(2):
                si = 2 * e + fch
                for nt in range(4):
                    T = 4 * si + nt
                    for fc in range(8):
                        cnt = 8 * T + fc
                        sc.wait_ge(s.c_gmm, cnt + 1)
                        if T >= 1:
                            sc.wait_ge(s.c_dmf, 16 * T + fc - 7)
                        sc.activation(AB[0:128, self.ht + 512 * fc:self.ht + 512 * fc + 512],
                                      bank(ps, 4 + fc % 2), AFT.Silu,
                                      bias=AF[0:128, self.gbp + 16 * e + 8 * fch + fc:
                                              self.gbp + 16 * e + 8 * fch + fc + 1]).then_inc(s.c_sil, 1)
        self.ln.es(sc)

    def eg(self, g):
        pr, s = self.pr, self.pr.sem
        AB = pr.AB
        if self.is_first:
            pr.emit_init_loads(g)
        g.dma_start(out=AB[0:128, self.wrt:self.wrt + 64],
                    in_=pr.h_wrtT.rearrange("(c p) e -> p c e", p=128)).then_inc(s.c_pl, 16)
        g.dma_start(out=pr.AF[0:128, self.gbp:self.gbp + 128], in_=pr.h_gb).then_inc(s.c_pl, 16)
        g.dma_start(out=pr.AF[0:128, self.ubp:self.ubp + 128], in_=pr.h_ub).then_inc(s.c_pl, 16)
        g.dma_start(out=AB[0:8, self.db:self.db + 1024], in_=pr.h_db).then_inc(s.c_pl, 16)
        g.dma_start(out=AB[0:8, self.sel8:self.sel8 + 1024],
                    in_=pr.h_sel8.rearrange("e k p -> k e p")).then_inc(s.c_pl, 16)
        for si in range(16):
            e, fch = divmod(si, 2)
            if si >= 2:
                g.wait_ge(s.c_umm, 32 * (si - 1))
            g.dma_start(out=AB[0:128, self.slab(3 * si):self.slab(3 * si) + 8192],
                        in_=pr.h_wg[1024 * e:1024 * e + 1024, 1024 * fch:1024 * fch + 1024]
                        .rearrange("(dc p) c -> p dc c", p=128)).then_inc(s.c_wl, 16)
            if si >= 2:
                g.wait_ge(s.c_dmf, 64 * (si - 1))
            g.dma_start(out=AB[0:128, self.slab(3 * si + 1):self.slab(3 * si + 1) + 8192],
                        in_=pr.h_wu[1024 * e:1024 * e + 1024, 1024 * fch:1024 * fch + 1024]
                        .rearrange("(dc p) c -> p dc c", p=128)).then_inc(s.c_wl, 16)
            if si >= 1:
                g.wait_ge(s.c_gmm, 32 * si)
            g.dma_start(out=AB[0:128, self.slab(3 * si + 2):self.slab(3 * si + 2) + 8192],
                        in_=pr.h_wd[2048 * e + 1024 * fch:2048 * e + 1024 * fch + 1024, :]
                        .rearrange("(fc p) c -> p fc c", p=128)).then_inc(s.c_wdl, 16)

    def esy(self, sy):
        pr, s = self.pr, self.pr.sem
        if DBG:
            sy.wait_ge(s.c_rs, 8)
            sy.dma_start(out=self.d_gtr, in_=pr.AB[0:8, self.gtr:self.gtr + 2048]).then_inc(s.c_dbg, 16)
            sy.dma_start(out=self.d_yac, in_=pr.AB[0:128, self.yac:self.yac + 16384]).then_inc(s.c_dbg, 16)
            sy.dma_start(out=self.d_gtt, in_=pr.AF[0:128, self.gtt:self.gtt + 128]).then_inc(s.c_dbg, 16)
            sy.dma_start(out=self.d_rtl, in_=pr.AF[0:128, self.rtl:self.rtl + 128]).then_inc(s.c_dbg, 16)
            sy.dma_start(out=self.d_rte, in_=pr.AF[0:128, self.rte:self.rte + 128]).then_inc(s.c_dbg, 16)
            sy.wait_ge(s.c_dbg, 80)
        if pr.last:
            for c in range(8):
                sy.wait_ge(s.c_lap, 4 * (c + 1))
                sy.dma_start(out=pr.h_out[128 * c:128 * c + 128, :],
                             in_=pr.AF[0:128, self.osc + 2048 * (c % 2):self.osc + 2048 * (c % 2) + 2048]).then_inc(s.c_ost, 16)
            sy.wait_ge(s.c_ost, 128)
        else:
            sy.wait_ge(s.c_lap, 32)
            sy.dma_start(out=pr.h_out, in_=pr.xTb[:, :]).then_inc(s.c_out, 16)
            sy.wait_ge(s.c_out, 16)



def kernel(**inputs):
    d = _prep(inputs)
    xs = d["xT"]
    cur = None
    out = None
    for gi, group in enumerate(PHASE_GROUPS()):
        pr = Prog(group)
        nc = pr.build()
        in_maps = []
        for c in range(8):
            m = {}
            for name in pr.inputs:
                if name == "xT":
                    m[name] = xs[c]
                elif name == "xh":
                    m[name] = cur[c]
                else:
                    m[name] = d[name]
            in_maps.append(m)
        res = run_bass_kernel_spmd(nc, in_maps, core_ids=list(range(8)))
        if group[-1] == "l1f":
            out = [res.results[c]["out"] for c in range(8)]
        else:
            cur = [res.results[c]["xho"] for c in range(8)]
    return np.stack([o.T for o in out], axis=0)


def PHASE_GROUPS():
    return GROUPS


# version 14 (final)
# speedup vs baseline: 1.0834x; 1.0035x over previous
import numpy as np
import ml_dtypes
import concourse.bass as bass
import concourse.mybir as mybir
from concourse.bass_utils import run_bass_kernel_spmd

F32 = mybir.dt.float32
BF16 = mybir.dt.bfloat16
AFT = mybir.ActivationFunctionType
ALU = mybir.AluOpType
AX = mybir.AxisListType

S = 2048
E = 1024
NHEAD = 16
HD = 64
NCHUNK = 8
NQT = 4
BF = ml_dtypes.bfloat16

GROUPS = [["l0a", "l0f", "l1a", "l1f"]]
DBG = False



def _prep(inp):
    d = {}
    x = np.asarray(inp["L0_attn_x"])
    d["xT"] = [np.ascontiguousarray(x[c].T) for c in range(8)]

    d["w0qT"] = np.ascontiguousarray(np.asarray(inp["L0_attn_Wq"]).T)
    d["w0kT"] = np.ascontiguousarray(np.asarray(inp["L0_attn_Wk"]).T)
    d["w0vT"] = np.ascontiguousarray(np.asarray(inp["L0_attn_Wv"]).T)
    d["g_rep"] = np.full((128, 1), np.float32(np.asarray(inp["L0_attn_g_scale"])), np.float32)

    d["w0f"] = np.ascontiguousarray(np.asarray(inp["L0_ffn_weight"]))
    d["bias0"] = np.asarray(inp["L0_ffn_bias"]).reshape(8, 128).T.copy()
    for br, tag in (("so", "so"), ("si", "si")):
        cw = np.asarray(inp[f"L0_ffn_{br}_conv_w"])
        bd = np.zeros((8, 128, 128), np.float32)
        for c in range(8):
            for gl in range(32):
                g = 32 * c + gl
                col = 32 * (c % 4) + gl
                bd[c, 4 * gl:4 * gl + 4, col] = cw[g]
        d[f"bd_{tag}"] = bd
        d[f"cb_{tag}"] = np.asarray(inp[f"L0_ffn_{br}_conv_b"]).reshape(2, 128).T.copy()
        d[f"gam_{tag}"] = np.asarray(inp[f"L0_ffn_{br}_gamma"]).reshape(2, 128).T.copy()
        d[f"bet_{tag}"] = np.asarray(inp[f"L0_ffn_{br}_beta"]).reshape(2, 128).T.copy()
        d[f"linT_{tag}"] = np.ascontiguousarray(np.asarray(inp[f"L0_ffn_{br}_lin_w"]).T)
        d[f"linb_{tag}"] = np.asarray(inp[f"L0_ffn_{br}_lin_b"]).reshape(1, 1024).copy()

    d["wqkvT"] = np.ascontiguousarray(np.asarray(inp["L1_attn_w_qkv"]).T)
    d["wprjT"] = np.ascontiguousarray(np.asarray(inp["L1_attn_w_proj"]).T)
    d["bprj"] = np.asarray(inp["L1_attn_b_proj"]).reshape(8, 128).T.copy()

    d["wrtT"] = np.ascontiguousarray(np.asarray(inp["L1_ffn_router_w"]).T)
    d["wg"] = np.ascontiguousarray(np.asarray(inp["L1_ffn_gate_w"]).reshape(8192, 2048))
    d["wu"] = np.ascontiguousarray(np.asarray(inp["L1_ffn_up_w"]).reshape(8192, 2048))
    d["wd"] = np.ascontiguousarray(np.asarray(inp["L1_ffn_down_w"]).reshape(16384, 1024))
    d["gb"] = np.asarray(inp["L1_ffn_gate_b"]).reshape(8, 16, 128).transpose(2, 0, 1).reshape(128, 128).copy()
    d["ub"] = np.asarray(inp["L1_ffn_up_b"]).reshape(8, 16, 128).transpose(2, 0, 1).reshape(128, 128).copy()
    d["db"] = np.ascontiguousarray(np.asarray(inp["L1_ffn_down_b"]))

    d["c_id"] = np.eye(128, dtype=np.float32)
    d["c_ones_row"] = np.ones((1, 128), np.float32)
    d["c_ones_col"] = np.ones((128, 1), np.float32)
    blk16 = np.zeros((8, 128, 16), np.float32)
    sel16 = np.zeros((8, 16, 128), np.float32)
    for m in range(8):
        blk16[m, 0:64, 2 * m] = 1.0
        blk16[m, 64:128, 2 * m + 1] = 1.0
        sel16[m, 2 * m, 0:64] = 1.0
        sel16[m, 2 * m + 1, 64:128] = 1.0
    d["c_blk16"] = blk16
    d["c_sel16"] = sel16
    sel8 = np.zeros((8, 8, 128), np.float32)
    for e in range(8):
        sel8[e, e, :] = 1.0
    d["c_sel8"] = sel8
    return d



class Sems:
    SHARED = ("ld", "cvals", "bar")

    def __init__(self, nc, pool_n=48):
        self.nc = nc
        self.m = {}
        self.pool = [nc.alloc_semaphore(f"pool{i}") for i in range(pool_n)]
        self.phase_maps = {}
        self.phase_map = {}
        self.pi_box = [0]

    def set_phase(self, k):
        if k not in self.phase_maps:
            self.phase_maps[k] = ({}, [0])
        self.phase_map, box = self.phase_maps[k]
        self.pi_box = box

    def next_phase(self):
        self.set_phase(len(self.phase_maps))

    def __getattr__(self, name):
        if name in ("nc", "m", "pool", "phase_map", "phase_maps", "pi", "pi_box"):
            raise AttributeError(name)
        if name in self.SHARED:
            if name not in self.m:
                self.m[name] = self.nc.alloc_semaphore(name)
            return self.m[name]
        if name not in self.phase_map:
            assert self.pi_box[0] < len(self.pool), "sem pool exhausted"
            self.phase_map[name] = self.pool[self.pi_box[0]]
            self.pi_box[0] += 1
        return self.phase_map[name]


class Arena:
    def __init__(self, t):
        self.t = t
        self.off = 0
        self.marks = []

    def take(self, cols):
        o = self.off
        self.off += cols
        assert self.off <= self.t.shape[1], f"arena overflow {self.off} > {self.t.shape[1]}"
        return o

    def push(self):
        self.marks.append(self.off)

    def pop(self):
        self.off = self.marks.pop()


class _MockInst:
    def __init__(self, tot):
        self.tot = tot

    def then_inc(self, sem, n=1, **kw):
        self.tot[sem.num] = self.tot.get(sem.num, 0) + n
        return self

    def annotate(self, *_):
        return self


class _MockEng:
    def __init__(self, tot):
        self.tot = tot

    def wait_ge(self, sem, val):
        pass

    def sem_inc(self, sem, n=1):
        self.tot[sem.num] = self.tot.get(sem.num, 0) + n
        return _MockInst(self.tot)

    def __getattr__(self, op):
        if op.startswith("_"):
            raise AttributeError(op)
        def f(*a, **kw):
            return _MockInst(self.tot)
        return f


class _EngProxy:
    def __init__(self, eng, base):
        object.__setattr__(self, "_e", eng)
        object.__setattr__(self, "_b", base)

    def wait_ge(self, sem, val):
        return self._e.wait_ge(sem, val + self._b.get(sem.num, 0))

    def __getattr__(self, n):
        return getattr(self._e, n)


def bank(ps, i, p0=0, p1=128, c0=0, c1=512):
    return ps[p0:p1, 512 * i + c0:512 * i + c1]




class LNBlock:

    def __init__(self, pr, pfx, sq4, mbf, rbf, tt2, final=False, out_scr=None):
        self.pr, self.pfx = pr, pfx
        self.sq4, self.mbf, self.rbf, self.tt2 = sq4, mbf, rbf, tt2
        self.final = final
        self.out_scr = out_scr

    def sm(self, name):
        return getattr(self.pr.sem, self.pfx + name)

    def et(self, t, gate=None):
        pr = self.pr
        ps, AB = pr.ps, pr.AB
        if gate is not None:
            t.wait_ge(*gate)
        for c in range(8):
            for qt in range(4):
                ins = t.matmul(bank(ps, qt, 0, 1), pr.c_oc_bf[:, :],
                               pr.xTb[0:128, 2048 * c + 512 * qt:2048 * c + 512 * qt + 512],
                               start=(c == 0), stop=(c == 7))
                if c == 7:
                    ins.then_inc(self.sm("stm"), 1)
                idx = c * 4 + qt
                t.wait_ge(self.sm("lsq"), idx + 1)
                t.matmul(bank(ps, 4 + qt, 0, 1), pr.c_oc_bf[:, :],
                         AB[0:128, self.sq4 + 512 * (idx % 4):self.sq4 + 512 * (idx % 4) + 512],
                         start=(c == 0), stop=(c == 7)).then_inc(self.sm("lnmc"), 1)
        if not self.final:
            for qt in range(4):
                t.wait_ge(self.sm("lnv"), 1)
                if qt >= 2:
                    t.wait_ge(self.sm("lap"), 8 * (qt - 1))
                t.matmul(bank(ps, qt % 2), pr.c_or_bf[:, :],
                         AB[0:1, self.mbf + 512 * qt:self.mbf + 512 * qt + 512],
                         start=True, stop=True)
                t.matmul(bank(ps, 2 + qt % 2), pr.c_or_bf[:, :],
                         AB[0:1, self.rbf + 512 * qt:self.rbf + 512 * qt + 512],
                         start=True, stop=True).then_inc(self.sm("lbc"), 1)
        else:
            t.wait_ge(self.sm("lnv"), 1)
            for qt in range(4):
                t.matmul(bank(ps, qt), pr.c_or_bf[:, :],
                         AB[0:1, self.mbf + 512 * qt:self.mbf + 512 * qt + 512],
                         start=True, stop=True)
                t.matmul(bank(ps, 4 + qt), pr.c_or_bf[:, :],
                         AB[0:1, self.rbf + 512 * qt:self.rbf + 512 * qt + 512],
                         start=True, stop=True).then_inc(self.sm("lbc"), 1)

    def ev(self, v, gate=None):
        pr = self.pr
        ps, AB, AF = pr.ps, pr.AB, pr.AF
        for idx in range(32):
            c, qt = divmod(idx, 4)
            if idx == 0 and gate is not None:
                v.wait_ge(*gate)
            if idx >= 4:
                v.wait_ge(self.sm("lnmc"), idx - 3)
            v.tensor_mul(AB[0:128, self.sq4 + 512 * (idx % 4):self.sq4 + 512 * (idx % 4) + 512],
                         pr.xTb[0:128, 2048 * c + 512 * qt:2048 * c + 512 * qt + 512],
                         pr.xTb[0:128, 2048 * c + 512 * qt:2048 * c + 512 * qt + 512]).then_inc(self.sm("lsq"), 1)
        v.wait_ge(self.sm("lnmc"), 32)
        v.wait_ge(self.sm("stm"), 4)
        A_ = AF[0:1, pr.lnva:pr.lnva + 2048]
        B_ = AF[0:1, pr.lnvb:pr.lnvb + 2048]
        v.tensor_scalar_mul(A_, pr.ps[0:1, 0:2048], 1.0 / 1024.0)
        v.tensor_copy(AB[0:1, self.mbf:self.mbf + 2048], A_)
        v.tensor_mul(A_, A_, A_)
        v.tensor_scalar_mul(B_, pr.ps[0:1, 2048:4096], 1.0 / 1024.0)
        v.tensor_sub(B_, B_, A_).then_inc(self.sm("var"), 1)
        v.wait_ge(self.sm("sd"), 1)
        v.reciprocal(B_, A_)
        v.tensor_copy(AB[0:1, self.rbf:self.rbf + 2048], B_).then_inc(self.sm("lnv"), 1)
        if not self.final:
            for qt in range(4):
                v.wait_ge(self.sm("lbc"), qt + 1)
                for c in range(8):
                    tt = self.tt2 + 512 * (c % 2)
                    xv = pr.xTb[0:128, 2048 * c + 512 * qt:2048 * c + 512 * qt + 512]
                    v.tensor_tensor(AB[0:128, tt:tt + 512], xv, bank(ps, qt % 2), ALU.subtract)
                    v.tensor_tensor(xv, AB[0:128, tt:tt + 512], bank(ps, 2 + qt % 2),
                                    ALU.mult).then_inc(self.sm("lap"), 1)
        else:
            v.wait_ge(self.sm("lbc"), 4)
            for c in range(8):
                if c >= 2:
                    v.wait_ge(self.sm("ost"), 16 * (c - 1))
                for qt in range(4):
                    tt = self.tt2 + 512 * (qt % 2)
                    xv = pr.xTb[0:128, 2048 * c + 512 * qt:2048 * c + 512 * qt + 512]
                    v.tensor_tensor(AB[0:128, tt:tt + 512], xv, bank(ps, qt), ALU.subtract)
                    v.tensor_tensor(AF[0:128, self.out_scr + 2048 * (c % 2) + 512 * qt:
                                       self.out_scr + 2048 * (c % 2) + 512 * qt + 512],
                                    AB[0:128, tt:tt + 512], bank(ps, 4 + qt),
                                    ALU.mult).then_inc(self.sm("lap"), 1)

    def es(self, sc):
        pr = self.pr
        AF = pr.AF
        sc.wait_ge(self.sm("var"), 1)
        sc.activation(AF[0:1, pr.lnva:pr.lnva + 2048],
                      AF[0:1, pr.lnvb:pr.lnvb + 2048],
                      AFT.Sqrt, bias=pr.cval(1e-5, 1)).then_inc(self.sm("sd"), 1)



class Prog:
    def __init__(self, phases):
        self.phases = phases
        nc = bass.Bass("TRN2", target_bir_lowering=False, debug=False)
        self.nc = nc
        self.sem = Sems(nc)
        self.first = phases[0] == "l0a"
        self.last = phases[-1] == "l1f"
        self.inputs = []

        def inp(name, shape, dt=F32):
            self.inputs.append(name)
            return nc.dram_tensor(name, list(shape), dt, kind="ExternalInput").ap()

        self.inp = inp
        if self.first:
            self.h_x = inp("xT", (1024, 2048))
        else:
            self.h_x = inp("xh", (128, 16384), BF16)
        if self.last:
            self.h_out = nc.dram_tensor("out", [1024, 2048], F32, kind="ExternalOutput").ap()
        else:
            self.h_out = nc.dram_tensor("xho", [128, 16384], BF16, kind="ExternalOutput").ap()

        self.h_id = inp("c_id", (128, 128))
        self.h_ones_row = inp("c_ones_row", (1, 128))
        self.h_ones_col = inp("c_ones_col", (128, 1))
        self.h_blk16 = inp("c_blk16", (8, 128, 16))
        self.h_sel16 = inp("c_sel16", (8, 16, 128))
        ph = set(phases)
        if "l0a" in ph:
            self.h_w0q = inp("w0qT", (1024, 1024))
            self.h_w0k = inp("w0kT", (1024, 1024))
            self.h_w0v = inp("w0vT", (1024, 1024))
            self.h_grep = inp("g_rep", (128, 1))
        if "l0f" in ph:
            self.h_w0f = inp("w0f", (1024, 1024))
            self.h_bias0 = inp("bias0", (128, 8))
            for tag in ("so", "si"):
                setattr(self, f"h_bd_{tag}", inp(f"bd_{tag}", (8, 128, 128)))
                setattr(self, f"h_cb_{tag}", inp(f"cb_{tag}", (128, 2)))
                setattr(self, f"h_gam_{tag}", inp(f"gam_{tag}", (128, 2)))
                setattr(self, f"h_bet_{tag}", inp(f"bet_{tag}", (128, 2)))
                setattr(self, f"h_linT_{tag}", inp(f"linT_{tag}", (256, 1024)))
                setattr(self, f"h_linb_{tag}", inp(f"linb_{tag}", (1, 1024)))
        if "l1a" in ph:
            self.h_wqkvT = inp("wqkvT", (1024, 3072))
            self.h_wprjT = inp("wprjT", (1024, 1024))
            self.h_bprj = inp("bprj", (128, 8))
        if "l1f" in ph:
            self.h_wrtT = inp("wrtT", (1024, 8))
            self.h_wg = inp("wg", (8192, 2048))
            self.h_wu = inp("wu", (8192, 2048))
            self.h_wd = inp("wd", (16384, 1024))
            self.h_gb = inp("gb", (128, 128))
            self.h_ub = inp("ub", (128, 128))
            self.h_db = inp("db", (8, 1024))
            self.h_sel8 = inp("c_sel8", (8, 8, 128))

        self.xTb = nc.alloc_sbuf_tensor("xTb", [128, 16384], BF16)
        self.c_id = nc.alloc_sbuf_tensor("cid", [128, 128], F32)
        self.c_or_bf = nc.alloc_sbuf_tensor("corb", [1, 128], BF16)
        self.c_oc_bf = nc.alloc_sbuf_tensor("cocb", [128, 1], BF16)
        self.c_blk16 = nc.alloc_sbuf_tensor("cblk", [128, 8 * 16], BF16)
        self.c_sel16 = nc.alloc_sbuf_tensor("csel", [16, 8 * 128], BF16)
        self.AB = nc.alloc_sbuf_tensor("AB", [128, 68672], BF16)
        self.AF = nc.alloc_sbuf_tensor("AF", [128, 8296], F32)
        self.ps = nc.alloc_psum_tensor("PSB", [128, 4096], F32)

        self.lnva = 8296 - 4096
        self.lnvb = 8296 - 2048
        self.CONSTS = [1e-30, 1e-5, 1e-9, 0.125, -0.125]
        self.cval_base = None
        self.cvt = nc.alloc_sbuf_tensor("cvt", [128, len(self.CONSTS)], F32)

        self.n_init_dma = 6

    def build(self):
        nc = self.nc
        emitters = {"l0a": L0A, "l0f": L0F, "l1a": L1A, "l1f": L1F}
        phase_objs = []
        for i, name in enumerate(self.phases):
            P = emitters[name](self)
            P.is_first = (i == 0)
            P.is_last = (i == len(self.phases) - 1)
            phase_objs.append(P)

        bar = self.sem.bar
        nph = len(self.phases)

        totals = []
        for k, P in enumerate(phase_objs):
            self.sem.set_phase(k)
            tot = {}
            me = _MockEng(tot)
            for fn in (P.et, P.ev, P.es, P.eg, P.esy):
                fn(me)
            totals.append(tot)
        pool_nums = {h.num for h in self.sem.pool}
        bases = []
        cum = {}
        for k in range(nph):
            bases.append(dict(cum))
            for num, n in totals[k].items():
                if num in pool_nums:
                    cum[num] = cum.get(num, 0) + n

        for k, (name, P) in enumerate(zip(self.phases, phase_objs)):
            self.sem.set_phase(k)
            with nc.allow_low_precision(reason="bf16 stream/operands by design"), \
                 nc.Block(f"ph_{name}") as blk:
                def mk(fn, k=k):
                    def run(eng):
                        pe = _EngProxy(eng, bases[k])
                        if k > 0:
                            eng.wait_ge(bar, 5 * k)
                        fn(pe)
                        eng.sem_inc(bar, 1)
                    return run
                blk.tensor(mk(P.et))
                blk.vector(mk(P.ev))
                blk.scalar(mk(P.es))
                blk.gpsimd(mk(P.eg))
                blk.sync(mk(P.esy))
        return nc

    def emit_init_loads(self, g):
        s = self.sem
        n0 = 6
        g.dma_start(out=self.c_id[:, :], in_=self.h_id).then_inc(s.ld, 16)
        g.dma_start(out=self.c_or_bf[:, :], in_=self.h_ones_row).then_inc(s.ld, 16)
        g.dma_start(out=self.c_oc_bf[:, :], in_=self.h_ones_col).then_inc(s.ld, 16)
        g.dma_start(out=self.c_blk16[:, :],
                    in_=self.h_blk16.rearrange("m p k -> p m k")).then_inc(s.ld, 16)
        g.dma_start(out=self.c_sel16[:, :],
                    in_=self.h_sel16.rearrange("m q k -> q m k")).then_inc(s.ld, 16)
        if self.first:
            g.dma_start(out=self.xTb[:, :],
                        in_=self.h_x.rearrange("(c p) t -> p c t", p=128)).then_inc(s.ld, 16)
        else:
            g.dma_start(out=self.xTb[:, :], in_=self.h_x).then_inc(s.ld, 16)
        assert self.n_init_dma == n0

    def ld_val(self):
        return 16 * self.n_init_dma

    def emit_cval_sets(self, v):
        for j, val in enumerate(self.CONSTS):
            ins = v.memset(self.cvt[:, j:j + 1], val)
        ins.then_inc(self.sem.cvals, 1)

    def cval(self, val, p=128):
        j = self.CONSTS.index(val)
        return self.cvt[0:p, j:j + 1]



class L0A:
    def __init__(self, pr: Prog):
        self.pr = pr
        nc = pr.nc
        A = Arena(pr.AB)
        self.WT = A.take(8192)
        self.qhT = A.take(16384)
        self.khT = A.take(16384)
        self.vaug = A.take(16640)
        self.PT = self.WT
        self.sq4 = A.take(2048)
        self.rstd2 = A.take(1024)
        self.rec2 = A.take(1024)
        self.rbsb2 = A.take(1024)
        self.t1b2 = A.take(1024)
        self.mbf = self.rstd2
        self.rbf = self.rbsb2
        self.tt2 = self.vaug
        self.ln = LNBlock(pr, "a_", self.sq4, self.mbf, self.rbf, self.tt2)
        assert A.off <= pr.AB.shape[1]

        Fa = Arena(pr.AF)
        self.vtmp = Fa.take(1024)
        self.ns2 = Fa.take(1024)
        self.grep = Fa.take(1)
        self.nabsg = Fa.take(1)
        self.lnva = Fa.take(2048)
        self.lnvb = Fa.take(2048)
        self.Fa = Fa

    def et(self, t):
        pr, s = self.pr, self.pr.sem
        nc = pr.nc
        ps = pr.ps
        AB, AF = pr.AB, pr.AF
        t.wait_ge(s.ld, pr.ld_val())

        def main_mm(a, w, qt, m):
            prior = a - 4
            if prior >= 0:
                if prior < 64:
                    t.wait_ge(s.a_ev, prior + 1)
                else:
                    t.wait_ge(s.a_vtmp, prior - 64 + 1)
            t.wait_ge(s.a_wl, 16 * (w + 1))
            for e in range(8):
                ins = t.matmul(bank(ps, a % 4),
                               AB[0:128, self.WT + 1024 * e + 128 * m:self.WT + 1024 * e + 128 * m + 128],
                               pr.xTb[0:128, 2048 * e + 512 * qt:2048 * e + 512 * qt + 512],
                               start=(e == 0), stop=(e == 7))
            ins.then_inc(s.a_mm, 1)

        def nmm(b, G, m):
            if m == 0 and G >= 2:
                t.wait_ge(s.a_sqrt, G - 1)
            t.wait_ge(s.a_sq, b + 1)
            t.matmul(bank(ps, 4 + G % 2, 0, 16),
                     pr.c_blk16[0:128, 16 * m:16 * m + 16],
                     AB[0:128, self.sq4 + 512 * (b % 4):self.sq4 + 512 * (b % 4) + 512],
                     start=(m == 0), stop=(m == 7)).then_inc(s.a_nmm, 1)

        def bmm(G, m):
            bm = G * 8 + m
            t.wait_ge(s.a_rcp, G + 1)
            if bm >= 2:
                t.wait_ge(s.a_nml, bm - 1)
            t.matmul(bank(ps, 6 + m % 2),
                     pr.c_sel16[0:16, 128 * m:128 * m + 128],
                     AB[0:16, self.rstd2 + 512 * (G % 2):self.rstd2 + 512 * (G % 2) + 512],
                     start=True, stop=True).then_inc(s.a_bmm, 1)

        nqk = 64
        for b in range(nqk + 9):
            if b < nqk:
                w, r = divmod(b, 32)
                qt, m = divmod(r, 8)
                main_mm(b, w, qt, m)
            if 1 <= b + 0 and b - 1 < nqk and b - 1 >= 0:
                G, m = divmod(b - 1, 8)
                nmm(b - 1, G, m)
            if b - 9 >= 0 and b - 9 < nqk:
                G, m = divmod(b - 9, 8)
                bmm(G, m)

        def v_tr(vi, qt, m):
            if vi < 2:
                t.wait_ge(s.a_sqrt, 7 + vi)
            t.wait_ge(s.a_vtmp, vi + 1)
            if vi >= 2:
                t.wait_ge(s.a_vaug, vi - 1)
            for j in range(4):
                ins = t.transpose(bank(ps, 4 + vi % 2, 0, 128, 128 * j, 128 * j + 128),
                                  AF[0:128, self.vtmp + 512 * (vi % 2) + 128 * j:
                                     self.vtmp + 512 * (vi % 2) + 128 * j + 128],
                                  pr.c_id[:, :])
            ins.then_inc(s.a_vtr, 1)

        for vi in range(32 + 2):
            if vi < 32:
                qt, m = divmod(vi, 8)
                main_mm(64 + vi, 2, qt, m)
            if vi - 2 >= 0:
                qt, m = divmod(vi - 2, 8)
                v_tr(vi - 2, qt, m)

        t.wait_ge(s.a_nml, 64)
        t.wait_ge(s.a_vaug, 32)

        def sc(j, ktg):
            h, qt = divmod(j, 4)
            p, hb = h // 2, 64 * (h % 2)
            g = 4 * j + ktg
            if g >= 1:
                t.wait_ge(s.a_exp, g)
            for kseq in range(4):
                kt = 4 * ktg + kseq
                ins = t.matmul(bank(ps, kseq),
                               AB[hb:hb + 64, self.khT + 2048 * p + 128 * kt:
                                  self.khT + 2048 * p + 128 * kt + 128],
                               AB[hb:hb + 64, self.qhT + 2048 * p + 512 * qt:
                                  self.qhT + 2048 * p + 512 * qt + 512],
                               start=True, stop=True, tile_position=(hb, 0))
            ins.then_inc(s.a_smm, 1)

        def av(j, ktg):
            h, qt = divmod(j, 4)
            p, half = h // 2, h % 2
            t.wait_ge(s.a_exp, 4 * j + ktg + 1)
            if ktg == 0 and j >= 2:
                t.wait_ge(s.a_t1, j - 1)
            for kseq in range(4):
                kt = 4 * ktg + kseq
                ins = t.matmul(bank(ps, 4 + 2 * (j % 2), 0, 65),
                               AB[0:128, self.vaug + 2080 * p + 130 * kt + 65 * half:
                                  self.vaug + 2080 * p + 130 * kt + 65 * half + 65],
                               AB[0:128, self.PT + 512 * kt:self.PT + 512 * kt + 512],
                               start=(ktg == 0 and kseq == 0), stop=(ktg == 3 and kseq == 3))
            ins.then_inc(s.a_avm, 1)

        def rb(j):
            t.wait_ge(s.a_rec, j + 1)
            if j >= 2:
                t.wait_ge(s.a_t1, j - 1)
            t.matmul(bank(ps, 5 + 2 * (j % 2), 0, 64),
                     pr.c_or_bf[0:1, 0:64],
                     AB[0:1, self.rec2 + 512 * (j % 2):self.rec2 + 512 * (j % 2) + 512],
                     start=True, stop=True).then_inc(s.a_rbm, 1)

        NT = 64
        for j in range(NT + 1):
            for ktg in range(4):
                if j - 1 >= 0:
                    av(j - 1, ktg)
                if j < NT:
                    sc(j, ktg)
            if j - 1 >= 0:
                rb(j - 1)

        self.ln.et(t, gate=(s.a_add, NT))

    def ev(self, v):
        pr, s = self.pr, self.pr.sem
        ps = pr.ps
        AB, AF = pr.AB, pr.AF
        v.wait_ge(s.ld, pr.ld_val())
        if self.is_first:
            pr.emit_cval_sets(v)
        v.wait_ge(s.a_gabs, 1)
        v.tensor_scalar_mul(AF[0:128, self.nabsg:self.nabsg + 1],
                            AF[0:128, self.nabsg:self.nabsg + 1], -1.0).then_inc(s.a_gneg, 1)

        vv = AB[0:128, self.vaug:self.vaug + 16640].rearrange("P (p k c) -> P p k c", p=8, c=130)
        v.memset(vv[:, :, :, 64:65], 1.0)
        v.memset(vv[:, :, :, 129:130], 1.0).then_inc(s.a_vone, 1)

        def copy_sq(b):
            w, r = divmod(b, 32)
            qt, m = divmod(r, 8)
            dst = (self.qhT if w == 0 else self.khT) + 2048 * m + 512 * qt
            v.wait_ge(s.a_mm, b + 1)
            v.tensor_copy(AB[0:128, dst:dst + 512], bank(ps, b % 4)).then_inc(s.a_ev, 1)
            if b >= 4:
                v.wait_ge(s.a_nmm, b - 3)
            v.wait_ge(s.a_ev, b + 1)
            v.tensor_mul(AB[0:128, self.sq4 + 512 * (b % 4):self.sq4 + 512 * (b % 4) + 512],
                         AB[0:128, dst:dst + 512],
                         AB[0:128, dst:dst + 512]).then_inc(s.a_sq, 1)

        def recip(G):
            v.wait_ge(s.a_sqrt, G + 1)
            if G >= 2:
                v.wait_ge(s.a_bmm, 8 * G - 8)
            v.reciprocal(AB[0:16, self.rstd2 + 512 * (G % 2):self.rstd2 + 512 * (G % 2) + 512],
                         AF[0:16, self.ns2 + 512 * (G % 2):self.ns2 + 512 * (G % 2) + 512]).then_inc(s.a_rcp, 1)

        def nmult(bm):
            G, m = divmod(bm, 8)
            w, r = divmod(bm, 32)
            qt = r // 8
            dst = (self.qhT if w == 0 else self.khT) + 2048 * m + 512 * qt
            v.wait_ge(s.a_bmm, bm + 1)
            v.tensor_tensor(AB[0:128, dst:dst + 512],
                            AB[0:128, dst:dst + 512],
                            bank(ps, 6 + m % 2), ALU.mult).then_inc(s.a_nml, 1)

        for b in range(64 + 9):
            if b < 64:
                copy_sq(b)
            if b >= 8 and (b % 8) == 0 and (b // 8) - 1 < 8:
                recip((b // 8) - 1)
            if b - 9 >= 0:
                nmult(b - 9)

        v.wait_ge(s.a_vone, 1)
        for vi in range(32):
            qt, m = divmod(vi, 8)
            v.wait_ge(s.a_vtr, vi + 1)
            for j in range(4):
                kt = 4 * qt + j
                base = self.vaug + 2080 * m + 130 * kt
                v.tensor_copy(AB[0:128, base:base + 64],
                              bank(ps, 4 + vi % 2, 0, 128, 128 * j, 128 * j + 64))
                ins = v.tensor_copy(AB[0:128, base + 65:base + 65 + 64],
                                    bank(ps, 4 + vi % 2, 0, 128, 128 * j + 64, 128 * j + 128))
            ins.then_inc(s.a_vaug, 1)

        def a2v(j):
            h, qt = divmod(j, 4)
            if j >= 2:
                v.wait_ge(s.a_rbm, j - 1)
            v.wait_ge(s.a_avm, 4 * j + 4)
            v.reciprocal(AB[0:1, self.rec2 + 512 * (j % 2):self.rec2 + 512 * (j % 2) + 512],
                         bank(ps, 4 + 2 * (j % 2), 64, 65)).then_inc(s.a_rec, 1)

        def a2t1(j):
            h, qt = divmod(j, 4)
            c, hb = h // 2, 64 * (h % 2)
            v.wait_ge(s.a_rbc, j + 1)
            t1 = AB[hb:hb + 64, self.t1b2 + 512 * (j % 2):self.t1b2 + 512 * (j % 2) + 512]
            v.tensor_tensor(t1,
                            bank(ps, 4 + 2 * (j % 2), 0, 64),
                            AB[0:64, self.rbsb2 + 512 * (j % 2):self.rbsb2 + 512 * (j % 2) + 512],
                            ALU.mult).then_inc(s.a_t1, 1)
            dst = 2048 * c + 512 * qt
            v.wait_ge(s.a_t1, j + 1)
            v.tensor_add(pr.xTb[hb:hb + 64, dst:dst + 512],
                         pr.xTb[hb:hb + 64, dst:dst + 512],
                         t1).then_inc(s.a_add, 1)

        for j in range(64 + 1):
            if j < 64:
                a2v(j)
            if j - 1 >= 0:
                a2t1(j - 1)

        self.ln.ev(v, gate=(s.a_add, 64))

    def es(self, sc):
        pr, s = self.pr, self.pr.sem
        ps = pr.ps
        AB, AF = pr.AB, pr.AF
        sc.wait_ge(s.ld, pr.ld_val())
        sc.wait_ge(s.a_pl, 16)
        sc.activation(AF[0:128, self.nabsg:self.nabsg + 1],
                      AF[0:128, self.grep:self.grep + 1], AFT.Abs).then_inc(s.a_gabs, 1)

        sc.wait_ge(s.cvals, 1)

        def sqrt_G(G):
            sc.wait_ge(s.a_nmm, 8 * G + 8)
            if G >= 2:
                sc.wait_ge(s.a_rcp, G - 1)
            sc.activation(AF[0:16, self.ns2 + 512 * (G % 2):self.ns2 + 512 * (G % 2) + 512],
                          bank(ps, 4 + G % 2, 0, 16), AFT.Sqrt, bias=pr.cval(1e-30, 16)).then_inc(s.a_sqrt, 1)

        for G in range(8):
            sqrt_G(G)

        def v_copy(vi):
            qt, m = divmod(vi, 8)
            a = 64 + vi
            sc.wait_ge(s.a_mm, a + 1)
            if vi >= 2:
                sc.wait_ge(s.a_vtr, vi - 1)
            sc.copy(AF[0:128, self.vtmp + 512 * (vi % 2):self.vtmp + 512 * (vi % 2) + 512],
                    bank(ps, a % 4)).then_inc(s.a_vtmp, 1)

        for vi in range(32):
            v_copy(vi)

        sc.wait_ge(s.a_gneg, 1)

        def expg(j, ktg):
            g = 4 * j + ktg
            sc.wait_ge(s.a_smm, g + 1)
            if j >= 1:
                sc.wait_ge(s.a_avm, 4 * (j - 1) + ktg + 1)
            sc.activation(AB[0:128, self.PT + 2048 * ktg:self.PT + 2048 * ktg + 2048],
                          pr.ps[0:128, 0:2048], AFT.Exp,
                          bias=AF[0:128, self.nabsg:self.nabsg + 1],
                          scale=AF[0:128, self.grep:self.grep + 1]).then_inc(s.a_exp, 1)

        def rbcopy(j):
            sc.wait_ge(s.a_rbm, j + 1)
            sc.copy(AB[0:64, self.rbsb2 + 512 * (j % 2):self.rbsb2 + 512 * (j % 2) + 512],
                    bank(ps, 5 + 2 * (j % 2), 0, 64)).then_inc(s.a_rbc, 1)

        for j in range(64 + 1):
            if j < 64:
                for ktg in range(4):
                    expg(j, ktg)
            if j - 1 >= 0:
                rbcopy(j - 1)

        self.ln.es(sc)

    def eg(self, g):
        pr, s = self.pr, self.pr.sem
        if self.is_first:
            pr.emit_init_loads(g)
        g.dma_start(out=pr.AF[0:128, self.grep:self.grep + 1], in_=pr.h_grep).then_inc(s.a_pl, 16)
        g.dma_start(out=pr.AB[0:128, self.WT:self.WT + 8192],
                    in_=pr.h_w0q.rearrange("(c p) f -> p c f", p=128)).then_inc(s.a_wl, 16)
        g.wait_ge(s.a_mm, 32)
        g.dma_start(out=pr.AB[0:128, self.WT:self.WT + 8192],
                    in_=pr.h_w0k.rearrange("(c p) f -> p c f", p=128)).then_inc(s.a_wl, 16)
        g.wait_ge(s.a_mm, 64)
        g.dma_start(out=pr.AB[0:128, self.WT:self.WT + 8192],
                    in_=pr.h_w0v.rearrange("(c p) f -> p c f", p=128)).then_inc(s.a_wl, 16)

    def esy(self, sy):
        pr, s = self.pr, self.pr.sem
        if self.is_last:
            sy.wait_ge(s.a_lap, 32)
            sy.dma_start(out=pr.h_out, in_=pr.xTb[:, :]).then_inc(s.a_out, 16)
            sy.wait_ge(s.a_out, 16)




class L0F:

    def __init__(self, pr: Prog):
        self.pr = pr
        nc = pr.nc
        A = Arena(pr.AB)
        self.WT = A.take(8192)
        self.linT = A.take(4096)
        self.bd = A.take(2048)
        self.xc = A.take(8192)
        self.xa = A.take(8192)
        self.full = A.take(16384)
        self.xsp = A.take(16384)
        self.sqb = A.take(2048)
        self.linb = self.sqb
        self.prm = A.take(256)
        self.cmb2 = A.take(1024)
        self.mbf = self.xa + 0
        self.rbf = self.xa + 2048
        self.tt2 = self.xa + 4096
        self.sq4 = self.xa + 5120
        self.ln = LNBlock(pr, "f_", self.sq4, self.mbf, self.rbf, self.tt2)
        assert A.off <= pr.AB.shape[1], A.off

        Fa = Arena(pr.AF)
        self.lg = Fa.take(2048)
        self.zap = Fa.take(2048)
        self.sml = Fa.take(64)
        self.parts = Fa.take(8)
        self.allr = Fa.take(8)
        self.bnv = Fa.take(16)
        assert Fa.off <= pr.lnva, Fa.off

        self.ccin = nc.dram_tensor("ccin", [128, 8], F32, kind="Internal")
        self.ccout = nc.dram_tensor("ccout", [128, 8], F32, kind="Internal")
        if DBG:
            self.d_xsp = nc.dram_tensor("d_xsp", [128, 16384], BF16, kind="ExternalOutput").ap()
            self.d_full = nc.dram_tensor("d_full", [128, 16384], BF16, kind="ExternalOutput").ap()
            self.d_xc = nc.dram_tensor("d_xc", [128, 8192], BF16, kind="ExternalOutput").ap()
            self.d_xa = nc.dram_tensor("d_xa", [128, 8192], BF16, kind="ExternalOutput").ap()
            self.d_lg = nc.dram_tensor("d_lg", [128, 2048], F32, kind="ExternalOutput").ap()
            self.d_zap = nc.dram_tensor("d_zap", [128, 2048], F32, kind="ExternalOutput").ap()
            self.d_bnv = nc.dram_tensor("d_bnv", [128, 16], F32, kind="ExternalOutput").ap()
            self.d_sml = nc.dram_tensor("d_sml", [128, 64], F32, kind="ExternalOutput").ap()

        self.vsc = 0

    def smlv(self, off, p=128, w=1):
        return self.pr.AF[0:p, self.sml + off:self.sml + off + w]

    def sem(self, name, br):
        return getattr(self.pr.sem, f"f_{name}{br}")

    def et(self, t):
        pr, s = self.pr, self.pr.sem
        ps, AB = pr.ps, pr.AB
        t.wait_ge(s.ld, pr.ld_val())

        for idx in range(16):
            br, r = divmod(idx, 8)
            ti, qt = divmod(r, 4)
            t.wait_ge(s.f_wsm, 176)
            if idx >= 2:
                t.wait_ge(s.f_cev, idx - 1)
            for cc in range(4):
                c = 4 * ti + cc
                t.matmul(bank(ps, idx % 2),
                         AB[0:128, self.bd + 1024 * br + 128 * c:self.bd + 1024 * br + 128 * c + 128],
                         pr.xTb[0:128, 2048 * c + 512 * qt:2048 * c + 512 * qt + 512],
                         start=(cc == 0), stop=(cc == 3)).then_inc(s.f_cmm, (1 if cc == 3 else 0))

        for idx in range(32):
            m, qt = divmod(idx, 4)
            t.wait_ge(s.f_wl, 16)
            if idx >= 2:
                t.wait_ge(s.f_fme, idx - 1)
            for e in range(8):
                t.matmul(bank(ps, 2 + idx % 2),
                         AB[0:128, self.WT + 1024 * e + 128 * m:self.WT + 1024 * e + 128 * m + 128],
                         pr.xTb[0:128, 2048 * e + 512 * qt:2048 * e + 512 * qt + 512],
                         start=(e == 0), stop=(e == 7)).then_inc(s.f_fmm, (1 if e == 7 else 0))

        def lmm(j):
            nt, br = divmod(j, 2)
            if nt == 0:
                t.wait_ge(s.f_gel, 2 * br + 2)
                t.wait_ge(s.f_lbl, 32)
                if br == 0:
                    t.wait_ge(s.f_cev, 16)
                else:
                    t.wait_ge(s.f_fme, 32)
            else:
                t.wait_ge(self.sem("exp", br), nt)
            for o in range(2):
                bk = bank(ps, 2 * br + o)
                t.matmul(bk, pr.c_or_bf[0:1, 0:128],
                         AB[0:1, self.linb + 1024 * br + 512 * o:
                            self.linb + 1024 * br + 512 * o + 512],
                         start=True, stop=False)
                for ti in range(2):
                    ins = t.matmul(bk,
                                   AB[0:128, self.xa + 4096 * br + 2048 * ti + 128 * nt:
                                      self.xa + 4096 * br + 2048 * ti + 128 * nt + 128],
                                   AB[0:128, self.linT + 2048 * br + 1024 * ti + 512 * o:
                                      self.linT + 2048 * br + 1024 * ti + 512 * o + 512],
                                   start=False, stop=(ti == 1))
            ins.then_inc(self.sem("lmm", br), 1)

        def mtr(j):
            nt, br = divmod(j, 2)
            t.wait_ge(self.sem("zs", br), nt + 1)
            if nt >= 1:
                t.wait_ge(self.sem("mev", br), nt)
            for q in range(8):
                bk = 4 + 2 * br + q // 4
                ins = t.transpose(bank(ps, bk, 0, 128, 128 * (q % 4), 128 * (q % 4) + 128),
                                  pr.AF[0:128, self.zap + 1024 * br + 128 * q:
                                        self.zap + 1024 * br + 128 * q + 128],
                                  pr.c_id[:, :])
            ins.then_inc(self.sem("mtr", br), 1)

        for k in range(34):
            if k < 32:
                lmm(k)
            if 0 <= k - 2 < 32:
                mtr(k - 2)

        for idx in range(32):
            m, qt = divmod(idx, 4)
            if idx == 0:
                t.wait_ge(s.f_mev0, 16)
                t.wait_ge(s.f_mev1, 16)
            if idx >= 4:
                t.wait_ge(s.f_cmb, idx - 3)
            for e in range(8):
                t.matmul(bank(ps, 4 + idx % 4),
                         AB[0:128, self.WT + 1024 * e + 128 * m:self.WT + 1024 * e + 128 * m + 128],
                         AB[0:128, self.xsp + 2048 * e + 512 * qt:self.xsp + 2048 * e + 512 * qt + 512],
                         start=(e == 0), stop=(e == 7)).then_inc(s.f_imm, (1 if e == 7 else 0))

        self.ln.et(t, gate=(s.f_cmb, 32))

    def ev(self, v):
        pr, s = self.pr, self.pr.sem
        ps, AB, AF = pr.ps, pr.AB, pr.AF
        v.wait_ge(s.ld, pr.ld_val())
        if self.is_first:
            pr.emit_cval_sets(v)
        self.vsc = 0

        for br in range(2):
            for ti in range(2):
                v.wait_ge(s.f_cev, 8 * br + 4 * ti + 4)
                xcv = AB[0:128, self.xc + 4096 * br + 2048 * ti:self.xc + 4096 * br + 2048 * ti + 2048]
                v.reduce_sum(AF[0:128, self.parts + 4 * br + ti:self.parts + 4 * br + ti + 1],
                             xcv, axis=AX.X)
                v.tensor_mul(AB[0:128, self.sqb:self.sqb + 2048], xcv, xcv)
                v.reduce_sum(AF[0:128, self.parts + 4 * br + 2 + ti:self.parts + 4 * br + 2 + ti + 1],
                             AB[0:128, self.sqb:self.sqb + 2048], axis=AX.X).then_inc(s.f_bnp, 1)

        v.wait_ge(s.f_ccd, 32)
        cN = 1.0 / 16384.0
        def vsn(ins):
            self.vsc += 1
            ins.then_inc(s.f_vsy, 1)
            return self.vsc
        def vs(ins):
            n = vsn(ins)
            v.wait_ge(s.f_vsy, n)
        for br in range(2):
            b0 = self.bnv + 8 * br
            mean = AF[0:128, b0:b0 + 2]
            var = AF[0:128, b0 + 2:b0 + 4]
            scl = AF[0:128, b0 + 4:b0 + 6]
            shf = AF[0:128, b0 + 6:b0 + 8]
            ar_s = AF[0:128, self.allr + 4 * br:self.allr + 4 * br + 2]
            ar_q = AF[0:128, self.allr + 4 * br + 2:self.allr + 4 * br + 4]
            vs(v.tensor_scalar_mul(mean, ar_s, cN))
            vs(v.tensor_scalar_mul(var, ar_q, cN))
            vs(v.tensor_mul(scl, mean, mean))
            v.tensor_sub(var, var, scl).then_inc(s.f_bvar, 1)
            v.wait_ge(s.f_bsd, br + 1)
            vs(v.reciprocal(scl, var))
            vs(v.tensor_mul(scl, scl, AB[0:128, self.prm + 4 + 8 * br:self.prm + 4 + 8 * br + 2]))
            vs(v.tensor_mul(shf, mean, scl))
            v.tensor_sub(shf, AB[0:128, self.prm + 8 + 8 * br:self.prm + 8 + 8 * br + 2],
                         shf).then_inc(s.f_bn, 1)

        self.r4_cnt = {}

        def Va(nt, br):
            v.wait_ge(self.sem("lmm", br), nt + 1)
            v.reduce_max(self.smlv(0 + br), ps[0:128, 1024 * br:1024 * br + 1024], axis=AX.X)
            v.tensor_scalar_mul(self.smlv(2 + br), self.smlv(0 + br),
                                -1.0).then_inc(self.sem("mx", br), 1)

        def rzR(nt, br):
            v.wait_ge(self.sem("exp", br), nt + 1)
            if nt >= 1:
                v.wait_ge(self.sem("zs", br), nt)
            v.reciprocal(self.smlv(6 + br), self.smlv(4 + br)).then_inc(self.sem("rz", br), 1)

        def Vc(nt, br):
            v.wait_ge(self.sem("mtr", br), nt + 1)
            for q in range(8):
                bk = 4 + 2 * br + q // 4
                bkv = bank(ps, bk, 0, 128, 128 * (q % 4), 128 * (q % 4) + 128)
                if br == 0:
                    ins = v.tensor_tensor(
                        AB[0:128, self.xsp + 2048 * q + 128 * nt:self.xsp + 2048 * q + 128 * nt + 128],
                        bkv, pr.xTb[0:128, 2048 * q + 128 * nt:2048 * q + 128 * nt + 128], ALU.mult)
                else:
                    fv = AB[0:128, self.full + 2048 * q + 128 * nt:self.full + 2048 * q + 128 * nt + 128]
                    ins = v.tensor_tensor(fv, bkv, fv, ALU.mult)
            ins.then_inc(self.sem("mev", br), 1)

        for nt in range(16):
            for br in range(2):
                Va(nt, br)
            if nt >= 1:
                for br in range(2):
                    Vc(nt - 1, br)
            lgv = [AF[0:128, self.lg + 1024 * b:self.lg + 1024 * b + 1024] for b in range(2)]
            zp = [AF[0:128, self.zap + 1024 * b:self.zap + 1024 * b + 1024] for b in range(2)]
            m8 = [self.smlv(8 + 8 * b, 128, 8) for b in range(2)]
            cnt = {}
            for half in range(4):
                for br in range(2):
                    if half == 0:
                        v.wait_ge(self.sem("exp", br), nt + 1)
                        if nt >= 1:
                            v.wait_ge(self.sem("mtr", br), nt)
                        src = lgv[br]
                    else:
                        v.wait_ge(s.f_vsy, cnt[(br, half - 1, 1)])
                        src = zp[br]
                    cnt[(br, half, 0)] = vsn(v.max(out=m8[br], in_=src))
                for br in range(2):
                    v.wait_ge(s.f_vsy, cnt[(br, half, 0)])
                    src = lgv[br] if half == 0 else zp[br]
                    cnt[(br, half, 1)] = vsn(
                        v.match_replace(out=zp[br], in_to_replace=m8[br],
                                        in_values=src, imm_value=0.0))
                    if half == 3:
                        self.r4_cnt[(nt, br)] = cnt[(br, 3, 1)]
            for br in range(2):
                rzR(nt, br)
            for br in range(2):
                lgv = AF[0:128, self.lg + 1024 * br:self.lg + 1024 * br + 1024]
                zp = AF[0:128, self.zap + 1024 * br:self.zap + 1024 * br + 1024]
                v.wait_ge(s.f_vsy, self.r4_cnt[(nt, br)])
                vs(v.tensor_sub(zp, lgv, zp))
                v.tensor_scalar_mul(zp, zp, self.smlv(6 + br)).then_inc(self.sem("zs", br), 1)
        for br in range(2):
            Vc(15, br)

        for idx in range(32):
            m, qt = divmod(idx, 4)
            v.wait_ge(s.f_imm, idx + 1)
            cm = AB[0:128, self.cmb2 + 512 * (idx % 2):self.cmb2 + 512 * (idx % 2) + 512]
            fv = AB[0:128, self.full + 2048 * m + 512 * qt:self.full + 2048 * m + 512 * qt + 512]
            v.scalar_tensor_tensor(cm, bank(ps, 4 + idx % 4),
                                   AB[0:128, self.prm + 16 + m:self.prm + 16 + m + 1],
                                   fv, ALU.add, ALU.add)
            xv = pr.xTb[0:128, 2048 * m + 512 * qt:2048 * m + 512 * qt + 512]
            v.tensor_add(xv, xv, cm).then_inc(s.f_cmb, 1)

        self.ln.ev(v, gate=(s.f_cmb, 32))

    def es(self, sc):
        pr, s = self.pr, self.pr.sem
        ps, AB, AF = pr.ps, pr.AB, pr.AF
        sc.wait_ge(s.ld, pr.ld_val())
        sc.wait_ge(s.cvals, 1)

        for idx in range(16):
            br, r = divmod(idx, 8)
            ti, qt = divmod(r, 4)
            sc.wait_ge(s.f_cmm, idx + 1)
            sc.add(AB[0:128, self.xc + 4096 * br + 2048 * ti + 512 * qt:
                      self.xc + 4096 * br + 2048 * ti + 512 * qt + 512],
                   bank(ps, idx % 2),
                   AB[0:128, self.prm + 0 + 8 * br + ti:self.prm + 0 + 8 * br + ti + 1]).then_inc(s.f_cev, 1)

        for idx in range(32):
            m, qt = divmod(idx, 4)
            sc.wait_ge(s.f_fmm, idx + 1)
            sc.copy(AB[0:128, self.full + 2048 * m + 512 * qt:self.full + 2048 * m + 512 * qt + 512],
                    bank(ps, 2 + idx % 2)).then_inc(s.f_fme, 1)

        for br in range(2):
            b0 = self.bnv + 8 * br
            sc.wait_ge(s.f_bvar, br + 1)
            sc.activation(AF[0:128, b0 + 2:b0 + 4], AF[0:128, b0 + 2:b0 + 4],
                          AFT.Sqrt, bias=pr.cval(1e-5, 128)).then_inc(s.f_bsd, 1)
        for br in range(2):
            sc.wait_ge(s.f_bn, br + 1)
            b0 = self.bnv + 8 * br
            for ti in range(2):
                sc.activation(AB[0:128, self.xa + 4096 * br + 2048 * ti:
                                 self.xa + 4096 * br + 2048 * ti + 2048],
                              AB[0:128, self.xc + 4096 * br + 2048 * ti:
                                 self.xc + 4096 * br + 2048 * ti + 2048],
                              AFT.Gelu,
                              bias=AF[0:128, b0 + 6 + ti:b0 + 6 + ti + 1],
                              scale=AF[0:128, b0 + 4 + ti:b0 + 4 + ti + 1]).then_inc(s.f_gel, 1)

        for nt in range(16):
            for br in range(2):
                sc.wait_ge(self.sem("mx", br), nt + 1)
                if nt >= 1:
                    sc.wait_ge(self.sem("zs", br), nt)
                    sc.wait_ge(self.sem("rz", br), nt)
                lgv = AF[0:128, self.lg + 1024 * br:self.lg + 1024 * br + 1024]
                sc.activation(lgv, ps[0:128, 1024 * br:1024 * br + 1024], AFT.Exp,
                              bias=self.smlv(2 + br),
                              accum_out=self.smlv(4 + br)).then_inc(self.sem("exp", br), 1)

        self.ln.es(sc)

    def eg(self, g):
        pr, s = self.pr, self.pr.sem
        AB, AF = pr.AB, pr.AF
        if self.is_first:
            pr.emit_init_loads(g)
        g.dma_start(out=AB[0:128, self.WT:self.WT + 8192],
                    in_=pr.h_w0f.rearrange("(c p) f -> p c f", p=128)).then_inc(s.f_wl, 16)
        g.dma_start(out=AB[0:128, self.prm + 0:self.prm + 2], in_=pr.h_cb_si).then_inc(s.f_wsm, 16)
        g.dma_start(out=AB[0:128, self.prm + 8:self.prm + 10], in_=pr.h_cb_so).then_inc(s.f_wsm, 16)
        g.dma_start(out=AB[0:128, self.prm + 4:self.prm + 6], in_=pr.h_gam_si).then_inc(s.f_wsm, 16)
        g.dma_start(out=AB[0:128, self.prm + 12:self.prm + 14], in_=pr.h_gam_so).then_inc(s.f_wsm, 16)
        g.dma_start(out=AB[0:128, self.prm + 6:self.prm + 8], in_=pr.h_bet_si).then_inc(s.f_wsm, 16)
        g.dma_start(out=AB[0:128, self.prm + 14:self.prm + 16], in_=pr.h_bet_so).then_inc(s.f_wsm, 16)
        g.dma_start(out=AB[0:128, self.prm + 16:self.prm + 24], in_=pr.h_bias0).then_inc(s.f_wsm, 16)
        g.dma_start(out=AB[0:128, self.bd:self.bd + 1024],
                    in_=pr.h_bd_si.rearrange("c p k -> p c k")).then_inc(s.f_wsm, 16)
        g.dma_start(out=AB[0:128, self.bd + 1024:self.bd + 2048],
                    in_=pr.h_bd_so.rearrange("c p k -> p c k")).then_inc(s.f_wsm, 16)
        g.dma_start(out=AB[0:128, self.linT:self.linT + 2048],
                    in_=pr.h_linT_si.rearrange("(c p) f -> p c f", p=128)).then_inc(s.f_wsm, 16)
        g.dma_start(out=AB[0:128, self.linT + 2048:self.linT + 4096],
                    in_=pr.h_linT_so.rearrange("(c p) f -> p c f", p=128)).then_inc(s.f_wsm, 16)

        g.wait_ge(s.f_bnp, 4)
        g.dma_start(out=AB[0:1, self.linb:self.linb + 1024],
                    in_=pr.h_linb_si).then_inc(s.f_lbl, 16)
        g.dma_start(out=AB[0:1, self.linb + 1024:self.linb + 2048],
                    in_=pr.h_linb_so).then_inc(s.f_lbl, 16)
        g.dma_start(out=self.ccin.ap(), in_=AF[0:128, self.parts:self.parts + 8]).then_inc(s.f_ccd, 16)
        g.wait_ge(s.f_ccd, 16)
        g.collective_compute("AllReduce", ALU.add,
                             replica_groups=[list(range(8))],
                             ins=[self.ccin.ap().opt()],
                             outs=[self.ccout.ap().opt()]).then_inc(s.f_cc, 1)
        g.wait_ge(s.f_cc, 1)
        g.dma_start(out=AF[0:128, self.allr:self.allr + 8], in_=self.ccout.ap()).then_inc(s.f_ccd, 16)


    def esy(self, sy):
        pr, s = self.pr, self.pr.sem
        if DBG:
            sy.wait_ge(s.f_cmb, 32)
            AB, AF = pr.AB, pr.AF
            sy.dma_start(out=self.d_xsp, in_=AB[0:128, self.xsp:self.xsp + 16384]).then_inc(s.f_dbg, 16)
            sy.dma_start(out=self.d_full, in_=AB[0:128, self.full:self.full + 16384]).then_inc(s.f_dbg, 16)
            sy.dma_start(out=self.d_xc, in_=AB[0:128, self.xc:self.xc + 8192]).then_inc(s.f_dbg, 16)
            sy.dma_start(out=self.d_xa, in_=AB[0:128, self.xa:self.xa + 8192]).then_inc(s.f_dbg, 16)
            sy.dma_start(out=self.d_lg, in_=AF[0:128, self.lg:self.lg + 2048]).then_inc(s.f_dbg, 16)
            sy.dma_start(out=self.d_zap, in_=AF[0:128, self.zap:self.zap + 2048]).then_inc(s.f_dbg, 16)
            sy.dma_start(out=self.d_bnv, in_=AF[0:128, self.bnv:self.bnv + 16]).then_inc(s.f_dbg, 16)
            sy.dma_start(out=self.d_sml, in_=AF[0:128, self.sml:self.sml + 64]).then_inc(s.f_dbg, 16)
            sy.wait_ge(s.f_dbg, 128)
        if self.is_last:
            sy.wait_ge(s.f_lap, 32)
            sy.dma_start(out=pr.h_out, in_=pr.xTb[:, :]).then_inc(s.f_out, 16)
            sy.wait_ge(s.f_out, 16)



class _PH:
    def __init__(self, pr):
        self.pr = pr
    def et(self, t): pass
    def ev(self, v): pass
    def es(self, sc): pass
    def eg(self, g):
        if self.is_first:
            self.pr.emit_init_loads(g)
    def esy(self, sy): pass



class L1A:

    def __init__(self, pr: Prog):
        self.pr = pr
        A = Arena(pr.AB)
        self.kT = A.take(16384)
        self.vtok = A.take(16384)
        self.Ost = A.take(16384)
        self.qTb = A.take(4096)
        self.PT = A.take(8192)
        self.wsl = A.take(4096)
        self.prm = A.take(16)
        self.mbf = self.PT + 0
        self.rbf = self.PT + 2048
        self.tt2 = self.PT + 4096
        self.sq4 = self.PT + 5120
        self.ln = LNBlock(pr, "b_", self.sq4, self.mbf, self.rbf, self.tt2)
        assert A.off <= pr.AB.shape[1], A.off

        Fa = Arena(pr.AF)
        self.P2 = Fa.take(4096)
        self.sml = Fa.take(64)
        assert Fa.off <= pr.lnva, Fa.off

    def slab_buf(self, i):
        if i < 4 and i % 2 == 1:
            return self.qTb
        return self.wsl

    def et(self, t):
        pr, s = self.pr, self.pr.sem
        ps, AB, AF = pr.ps, pr.AB, pr.AF
        t.wait_ge(s.ld, pr.ld_val())

        kv_tile = 0
        for sl in range(4):
            kind, half = divmod(sl, 2)
            buf = self.slab_buf(sl)
            t.wait_ge(s.b_wl, 16 * (sl + 1))
            if kind == 0:
                for mm in range(4):
                    m = 4 * half + mm
                    for qt in range(4):
                        if kv_tile >= 4:
                            t.wait_ge(s.b_kev, kv_tile - 3)
                        for e in range(8):
                            t.matmul(bank(ps, kv_tile % 4),
                                     AB[0:128, buf + 512 * e + 128 * mm:buf + 512 * e + 128 * mm + 128],
                                     pr.xTb[0:128, 2048 * e + 512 * qt:2048 * e + 512 * qt + 512],
                                     start=(e == 0), stop=(e == 7)).then_inc(
                                         s.b_kvm, (1 if e == 7 else 0))
                        kv_tile += 1
            else:
                for tt in range(16):
                    if kv_tile >= 4:
                        t.wait_ge(s.b_kev, kv_tile - 3)
                    for e in range(8):
                        t.matmul(bank(ps, kv_tile % 4),
                                 pr.xTb[0:128, 2048 * e + 128 * tt:2048 * e + 128 * tt + 128],
                                 AB[0:128, buf + 512 * e:buf + 512 * e + 512],
                                 start=(e == 0), stop=(e == 7)).then_inc(
                                     s.b_kvm, (1 if e == 7 else 0))
                    kv_tile += 1
            t.sem_inc(s.b_sdone, 1)

        for qb in range(4):
            for half in range(2):
                sl = 4 + 2 * qb + half
                t.wait_ge(s.b_wl, 16 * (sl + 1))
                for mm in range(4):
                    m = 4 * half + mm
                    if 8 * qb + m >= 2:
                        t.wait_ge(s.b_qev, 8 * qb + m - 1)
                    for e in range(8):
                        t.matmul(bank(ps, 4 + m % 2),
                                 AB[0:128, self.slab_buf(sl) + 512 * e + 128 * mm:
                                    self.slab_buf(sl) + 512 * e + 128 * mm + 128],
                                 pr.xTb[0:128, 2048 * e + 512 * qb:2048 * e + 512 * qb + 512],
                                 start=(e == 0), stop=(e == 7)).then_inc(
                                     s.b_qmm, (1 if e == 7 else 0))
                t.sem_inc(s.b_sdone, 1)
            for il in range(4):
                qg = 4 * qb + il
                t.wait_ge(s.b_qev, 8 * qb + 8)
                if qg == 0:
                    t.wait_ge(s.b_kev, 64)
                if qg >= 1:
                    t.wait_ge(s.b_exp, qg)
                for ks in range(4):
                    for m in range(8):
                        t.matmul(bank(ps, ks),
                                 AB[0:128, self.qTb + 512 * m + 128 * il:
                                    self.qTb + 512 * m + 128 * il + 128],
                                 AB[0:128, self.kT + 2048 * m + 512 * ks:
                                    self.kT + 2048 * m + 512 * ks + 512],
                                 start=(m == 0), stop=(m == 7)).then_inc(
                                     s.b_smm, (1 if (ks == 3 and m == 7) else 0))
                t.wait_ge(s.b_exp, qg + 1)
                if qg >= 1:
                    t.wait_ge(s.b_ptv, qg)
                for jt in range(16):
                    bk = 4 + jt // 4
                    ins = t.transpose(bank(ps, bk, 0, 128, 128 * (jt % 4), 128 * (jt % 4) + 128),
                                      AF[0:128, self.P2 + 2048 * (qg % 2) + 128 * jt:
                                         self.P2 + 2048 * (qg % 2) + 128 * jt + 128],
                                      pr.c_id[:, :])
                ins.then_inc(s.b_ptr, 1)
            t.wait_ge(s.b_ptv, 4 * qb + 4)
            for il in range(4):
                for hs in range(2):
                    gidx = 8 * qb + 2 * il + hs
                    if gidx >= 2:
                        t.wait_ge(s.b_oev, gidx - 1)
                    for jt in range(16):
                        t.matmul(bank(ps, gidx % 2),
                                 AB[0:128, self.PT + 512 * jt + 128 * il:
                                    self.PT + 512 * jt + 128 * il + 128],
                                 AB[0:128, self.vtok + 1024 * jt + 512 * hs:
                                    self.vtok + 1024 * jt + 512 * hs + 512],
                                 start=(jt == 0), stop=(jt == 15)).then_inc(
                                     s.b_avm, (1 if jt == 15 else 0))

        re = AB[0:128, self.Ost:self.Ost + 16384].rearrange("p (b rest) -> p rest b", b=2)
        t.wait_ge(s.b_oev, 32)
        for half in range(2):
            sl = 12 + half
            t.wait_ge(s.b_wl, 16 * (sl + 1))
            for mm in range(4):
                m = 4 * half + mm
                for st in range(4):
                    idx = 16 * half + 4 * mm + st
                    if idx >= 4:
                        t.wait_ge(s.b_pja, idx - 3)
                    for c in range(8):
                        t.matmul(bank(ps, idx % 4),
                                 AB[0:128, self.wsl + 512 * c + 128 * mm:
                                    self.wsl + 512 * c + 128 * mm + 128],
                                 re[:, 1024 * c + 256 * st:1024 * c + 256 * st + 256, :],
                                 start=(c == 0), stop=(c == 7)).then_inc(
                                     s.b_pjm, (1 if c == 7 else 0))
            t.sem_inc(s.b_sdone, 1)

        self.ln.et(t, gate=(s.b_pja, 32))

    def ev(self, v):
        pr, s = self.pr, self.pr.sem
        ps, AB, AF = pr.ps, pr.AB, pr.AF
        v.wait_ge(s.ld, pr.ld_val())
        if self.is_first:
            pr.emit_cval_sets(v)

        kv_tile = 0
        for sl in range(4):
            kind, half = divmod(sl, 2)
            if kind == 0:
                for mm in range(4):
                    m = 4 * half + mm
                    for qt in range(4):
                        v.wait_ge(s.b_kvm, kv_tile + 1)
                        v.tensor_copy(AB[0:128, self.kT + 2048 * m + 512 * qt:
                                         self.kT + 2048 * m + 512 * qt + 512],
                                      bank(ps, kv_tile % 4)).then_inc(s.b_kev, 1)
                        kv_tile += 1
            else:
                for tt in range(16):
                    v.wait_ge(s.b_kvm, kv_tile + 1)
                    v.tensor_copy(AB[0:128, self.vtok + 1024 * tt + 512 * half:
                                     self.vtok + 1024 * tt + 512 * half + 512],
                                  bank(ps, kv_tile % 4)).then_inc(s.b_kev, 1)
                    kv_tile += 1

        for qb in range(4):
            for m in range(8):
                if m == 0 and qb >= 1:
                    v.wait_ge(s.b_smm, 4 * qb)
                v.wait_ge(s.b_qmm, 8 * qb + m + 1)
                v.tensor_copy(AB[0:128, self.qTb + 512 * m:self.qTb + 512 * m + 512],
                              bank(ps, 4 + m % 2)).then_inc(s.b_qev, 1)
            for il in range(4):
                qg = 4 * qb + il
                v.wait_ge(s.b_smm, qg + 1)
                v.reduce_max(AF[0:128, self.sml + qg:self.sml + qg + 1],
                             ps[0:128, 0:2048], axis=AX.X)
                v.tensor_scalar_mul(AF[0:128, self.sml + 16 + qg:self.sml + 16 + qg + 1],
                                    AF[0:128, self.sml + qg:self.sml + qg + 1],
                                    -0.125).then_inc(s.b_mx, 1)
                v.wait_ge(s.b_ptr, qg + 1)
                if qb >= 1 and il == 0:
                    v.wait_ge(s.b_avm, 8 * qb)
                for jt in range(16):
                    bk = 4 + jt // 4
                    ins = v.tensor_copy(AB[0:128, self.PT + 512 * jt + 128 * il:
                                           self.PT + 512 * jt + 128 * il + 128],
                                        bank(ps, bk, 0, 128, 128 * (jt % 4), 128 * (jt % 4) + 128))
                ins.then_inc(s.b_ptv, 1)
                v.wait_ge(s.b_exp, qg + 1)
                v.reciprocal(AF[0:128, self.sml + 48 + qg:self.sml + 48 + qg + 1],
                             AF[0:128, self.sml + 32 + qg:self.sml + 32 + qg + 1]).then_inc(s.b_rz, 1)
            for il in range(4):
                for hs in range(2):
                    gidx = 8 * qb + 2 * il + hs
                    qg = 4 * qb + il
                    v.wait_ge(s.b_rz, qg + 1)
                    v.wait_ge(s.b_avm, gidx + 1)
                    tt = 4 * qb + il
                    v.tensor_scalar_mul(AB[0:128, self.Ost + 1024 * tt + 512 * hs:
                                           self.Ost + 1024 * tt + 512 * hs + 512],
                                        bank(ps, gidx % 2),
                                        AF[0:128, self.sml + 48 + qg:self.sml + 48 + qg + 1]).then_inc(s.b_oev, 1)

        v.wait_ge(s.b_pl, 16)
        for half in range(2):
            for mm in range(4):
                m = 4 * half + mm
                for st in range(4):
                    idx = 16 * half + 4 * mm + st
                    v.wait_ge(s.b_pjm, idx + 1)
                    xv = pr.xTb[0:128, 2048 * m + 512 * st:2048 * m + 512 * st + 512]
                    v.scalar_tensor_tensor(xv, bank(ps, idx % 4),
                                           AB[0:128, self.prm + m:self.prm + m + 1],
                                           xv, ALU.add, ALU.add).then_inc(s.b_pja, 1)

        self.ln.ev(v, gate=(s.b_pja, 32))

    def es(self, sc):
        pr, s = self.pr, self.pr.sem
        ps, AF = pr.ps, pr.AF
        sc.wait_ge(s.ld, pr.ld_val())
        sc.wait_ge(s.cvals, 1)
        for qg in range(16):
            sc.wait_ge(s.b_mx, qg + 1)
            if qg >= 2:
                sc.wait_ge(s.b_ptr, qg - 1)
            sc.activation(AF[0:128, self.P2 + 2048 * (qg % 2):self.P2 + 2048 * (qg % 2) + 2048],
                          ps[0:128, 0:2048], AFT.Exp,
                          bias=AF[0:128, self.sml + 16 + qg:self.sml + 16 + qg + 1],
                          scale=pr.cval(0.125, 128),
                          accum_out=AF[0:128, self.sml + 32 + qg:self.sml + 32 + qg + 1]).then_inc(s.b_exp, 1)
        self.ln.es(sc)

    def eg(self, g):
        pr, s = self.pr, self.pr.sem
        AB = pr.AB
        if self.is_first:
            pr.emit_init_loads(g)
        g.dma_start(out=AB[0:128, self.prm:self.prm + 8], in_=pr.h_bprj).then_inc(s.b_pl, 16)
        def load_slab(i, src_ap):
            buf = self.slab_buf(i)
            g.dma_start(out=AB[0:128, buf:buf + 4096], in_=src_ap).then_inc(s.b_wl, 16)
        order = []
        order.append(pr.h_wqkvT[:, 1024:1536])
        order.append(pr.h_wqkvT[:, 1536:2048])
        order.append(pr.h_wqkvT[:, 2048:2560])
        order.append(pr.h_wqkvT[:, 2560:3072])
        for qb in range(4):
            order.append(pr.h_wqkvT[:, 0:512])
            order.append(pr.h_wqkvT[:, 512:1024])
        order.append(pr.h_wprjT[:, 0:512])
        order.append(pr.h_wprjT[:, 512:1024])
        for i, src in enumerate(order):
            if i in (2, 3):
                g.wait_ge(s.b_sdone, i - 1)
            elif i >= 4:
                g.wait_ge(s.b_sdone, i)
            load_slab(i, src.rearrange("(c p) f -> p c f", p=128))

    def esy(self, sy):
        pr, s = self.pr, self.pr.sem
        if self.is_last:
            sy.wait_ge(s.b_lap, 32)
            sy.dma_start(out=pr.h_out, in_=pr.xTb[:, :]).then_inc(s.b_out, 16)
            sy.wait_ge(s.b_out, 16)



class L1F:
    def __init__(self, pr: Prog):
        self.pr = pr
        A = Arena(pr.AB)
        self.RING = A.take(40960)
        self.yac = A.take(16384)
        self.gtr = A.take(2048)
        self.gbs = A.take(2048)
        self.ht = A.take(4096)
        self.tu = A.take(1024)

        self.db = A.take(1024)
        self.sel8 = A.take(1024)
        self.wrt = A.take(64)
        self.mbf = self.RING + 0
        self.rbf = self.RING + 2048
        self.tt2 = self.RING + 4096
        self.sq4 = self.RING + 5120
        self.ln = LNBlock(pr, "c_", self.sq4, self.mbf, self.rbf, self.tt2,
                          final=pr.last, out_scr=None)
        assert A.off <= pr.AB.shape[1], A.off

        Fa = Arena(pr.AF)
        if DBG:
            self.d_gtr = pr.nc.dram_tensor("d_gtr", [8, 2048], BF16, kind="ExternalOutput").ap()
            self.d_yac = pr.nc.dram_tensor("d_yac", [128, 16384], BF16, kind="ExternalOutput").ap()
            self.d_gtt = pr.nc.dram_tensor("d_gtt", [128, 128], F32, kind="ExternalOutput").ap()
            self.d_rtl = pr.nc.dram_tensor("d_rtl", [128, 128], F32, kind="ExternalOutput").ap()
            self.d_rte = pr.nc.dram_tensor("d_rte", [128, 128], F32, kind="ExternalOutput").ap()
        self.rtl = Fa.take(128)
        self.rte = Fa.take(128)
        self.sml = Fa.take(96)
        self.m8 = Fa.take(8)
        self.gtt = Fa.take(128)
        self.gbp = Fa.take(128)
        self.ubp = Fa.take(128)
        self.osc = pr.lnva
        self.ln.out_scr = self.osc
        assert Fa.off <= pr.lnva, Fa.off

    def slab(self, k):
        return self.RING + 8192 * (k % 5)

    def et(self, t):
        pr, s = self.pr, self.pr.sem
        ps, AB, AF = pr.ps, pr.AB, pr.AF
        t.wait_ge(s.ld, pr.ld_val())

        for tt in range(16):
            t.wait_ge(s.c_pl, 16 * 5)
            if tt >= 1:
                t.wait_ge(s.c_rcp, tt)
            for e8 in range(8):
                t.matmul(bank(ps, 7, 0, 128, 0, 8),
                         pr.xTb[0:128, 2048 * e8 + 128 * tt:2048 * e8 + 128 * tt + 128],
                         AB[0:128, self.wrt + 8 * e8:self.wrt + 8 * e8 + 8],
                         start=(e8 == 0), stop=(e8 == 7)).then_inc(
                             s.c_rmm, (1 if e8 == 7 else 0))
            j = tt - 1
            if j >= 0:
                t.wait_ge(s.c_rg, j + 1)
                if j >= 2:
                    t.wait_ge(s.c_gte, j - 1)
                t.transpose(bank(ps, 4 + j % 2, 0, 8, 0, 128),
                            AF[0:128, self.gtt + 8 * j:self.gtt + 8 * j + 8],
                            pr.c_id[:, :]).then_inc(s.c_gtm, 1)
        t.wait_ge(s.c_rg, 16)
        t.wait_ge(s.c_gte, 15)
        t.transpose(bank(ps, 5, 0, 8, 0, 128),
                    AF[0:128, self.gtt + 8 * 15:self.gtt + 8 * 15 + 8],
                    pr.c_id[:, :]).then_inc(s.c_gtm, 1)

        def down_group(T, p, fd, sip):
            dslab = self.slab(3 * sip + 2) + 1024 * fd
            for dc in range(4 * p, 4 * p + 4):
                t.matmul(bank(ps, dc % 4),
                         AB[0:128, dslab + 128 * dc:dslab + 128 * dc + 128],
                         AB[0:128, self.ht + 512 * fd:self.ht + 512 * fd + 512],
                         start=(fd == 0 and sip > 0),
                         stop=(fd == 7)).then_inc(s.c_dmf, (1 if dc == 4 * p + 3 else 0))

        def db_group(T, p):
            ntp = T % 4
            for dc in range(4 * p, 4 * p + 4):
                t.matmul(bank(ps, dc % 4),
                         AB[0:8, self.db + 128 * dc:self.db + 128 * dc + 128],
                         AB[0:8, self.gtr + 512 * ntp:self.gtr + 512 * ntp + 512],
                         start=True, stop=False)

        def sweep(Tp):
            sip = Tp // 4
            t.wait_ge(s.c_ya, 8 * Tp + 4)
            if sip == 0:
                db_group(Tp, 1)
            for fd in range(8):
                down_group(Tp, 1, fd, sip)

        for e in range(8):
            t.wait_ge(s.c_gte, 16)
            if e >= 1:
                t.wait_ge(s.c_tu, 64 * e)
            for nt in range(4):
                if nt >= 2 or e >= 1:
                    t.wait_ge(s.c_gbe, 4 * e + nt - 1)
                t.matmul(bank(ps, 6 + nt % 2),
                         AB[0:8, self.sel8 + 128 * e:self.sel8 + 128 * e + 128],
                         AB[0:8, self.gtr + 512 * nt:self.gtr + 512 * nt + 512],
                         start=True, stop=True).then_inc(s.c_gbm, 1)
            for fch in range(2):
                si = 2 * e + fch
                t.wait_ge(s.c_wl, 32 * si + 32)
                for nt in range(4):
                    T = 4 * si + nt
                    for fc in range(10):
                        if fc < 8:
                            cnt = 8 * T + fc
                            if cnt >= 2:
                                t.wait_ge(s.c_sil, cnt - 1)
                            if fch == 0 and nt == 0 and fc == 0:
                                t.wait_ge(s.c_gbe, 4 * (e + 1))
                            wslab = self.slab(3 * si) + 128 * fc
                            uslab = self.slab(3 * si + 1) + 128 * fc
                            for dc in range(8):
                                t.matmul(bank(ps, 4 + fc % 2),
                                         AB[0:128, wslab + 1024 * dc:wslab + 1024 * dc + 128],
                                         pr.xTb[0:128, 2048 * dc + 512 * nt:2048 * dc + 512 * nt + 512],
                                         start=(dc == 0), stop=(dc == 7)).then_inc(
                                             s.c_gmm, (1 if dc == 7 else 0))
                            if cnt >= 2:
                                t.wait_ge(s.c_tu, cnt - 1)
                            for dc in range(8):
                                t.matmul(bank(ps, 6 + fc % 2),
                                         AB[0:128, uslab + 1024 * dc:uslab + 1024 * dc + 128],
                                         pr.xTb[0:128, 2048 * dc + 512 * nt:2048 * dc + 512 * nt + 512],
                                         start=(dc == 0), stop=(dc == 7)).then_inc(
                                             s.c_umm, (1 if dc == 7 else 0))
                        if fc == 1 and T >= 1:
                            sweep(T - 1)
                        fd = fc - 2
                        if fd >= 0:
                            t.wait_ge(s.c_h, 8 * T + fd + 1)
                            if fd == 0:
                                t.wait_ge(s.c_ya, 8 * T)
                                if nt == 0:
                                    t.wait_ge(s.c_wdl, 16 * si + 16)
                                if si == 0:
                                    db_group(T, 0)
                            down_group(T, 0, fd, si)
        sweep(63)

        self.ln.et(t, gate=(s.c_rs, 8))

    def ev(self, v):
        pr, s = self.pr, self.pr.sem
        ps, AB, AF = pr.ps, pr.AB, pr.AF
        v.wait_ge(s.ld, pr.ld_val())
        if self.is_first:
            pr.emit_cval_sets(v)

        self.vsc = 0
        m8 = AF[0:128, self.m8:self.m8 + 8]
        for tt in range(16):
            mx = AF[0:128, self.sml + tt:self.sml + tt + 1]
            mxn = AF[0:128, self.sml + 16 + tt:self.sml + 16 + tt + 1]
            Z = AF[0:128, self.sml + 32 + tt:self.sml + 32 + tt + 1]
            thr = AF[0:128, self.sml + 48 + tt:self.sml + 48 + tt + 1]
            den = AF[0:128, self.sml + 64 + tt:self.sml + 64 + tt + 1]
            rden = AF[0:128, self.sml + 80 + tt:self.sml + 80 + tt + 1]
            gtt = AF[0:128, self.gtt + 8 * tt:self.gtt + 8 * tt + 8]
            rt = AF[0:128, self.rtl + 8 * tt:self.rtl + 8 * tt + 8]
            def vs(ins):
                self.vsc += 1
                ins.then_inc(s.c_vsy, 1)
                v.wait_ge(s.c_vsy, self.vsc)
            v.wait_ge(s.c_rmm, tt + 1)
            v.tensor_copy(rt, bank(ps, 7, 0, 128, 0, 8)).then_inc(s.c_rcp, 1)
            v.wait_ge(s.c_rcp, tt + 1)
            vs(v.reduce_max(mx, rt, axis=AX.X))
            v.tensor_scalar_mul(mxn, mx, -1.0).then_inc(s.c_rmx, 1)
            rte = AF[0:128, self.rte + 8 * tt:self.rte + 8 * tt + 8]
            v.wait_ge(s.c_rex, tt + 1)
            vs(v.max(out=m8, in_=rte))
            vs(v.tensor_copy(thr, AF[0:128, self.m8 + 1:self.m8 + 2]))
            vs(v.tensor_single_scalar(gtt, rte, thr, ALU.is_ge))
            vs(v.tensor_mul(gtt, gtt, rte))
            vs(v.reduce_sum(den, gtt, axis=AX.X))
            vs(v.tensor_scalar_mul(Z, Z, 1e-9))
            vs(v.tensor_add(den, den, Z))
            vs(v.reciprocal(rden, den))
            v.tensor_scalar_mul(gtt, gtt, rden).then_inc(s.c_rg, 1)
            j = tt - 1
            if j >= 0:
                v.wait_ge(s.c_gtm, j + 1)
                v.tensor_copy(AB[0:8, self.gtr + 128 * j:self.gtr + 128 * j + 128],
                              bank(ps, 4 + j % 2, 0, 8, 0, 128)).then_inc(s.c_gte, 1)
        v.wait_ge(s.c_gtm, 16)
        v.tensor_copy(AB[0:8, self.gtr + 128 * 15:self.gtr + 128 * 15 + 128],
                      bank(ps, 5, 0, 8, 0, 128)).then_inc(s.c_gte, 1)

        def y_evac(T, p):
            sip, ntp = divmod(T, 4)
            for dc in range(4 * p, 4 * p + 4):
                yv = AB[0:128, self.yac + 2048 * dc + 512 * ntp:
                        self.yac + 2048 * dc + 512 * ntp + 512]
                pv = bank(ps, dc % 4)
                if sip == 0:
                    v.tensor_copy(yv, pv).then_inc(s.c_ya, 1)
                else:
                    v.tensor_add(yv, yv, pv).then_inc(s.c_ya, 1)

        for e in range(8):
            for fch in range(2):
                si = 2 * e + fch
                for nt in range(4):
                    T = 4 * si + nt
                    v.wait_ge(s.c_gbe, 4 * e + nt + 1)
                    for fc in range(8):
                        cnt = 8 * T + fc
                        v.wait_ge(s.c_umm, cnt + 1)
                        tuv = AB[0:128, self.tu + 512 * (fc % 2):self.tu + 512 * (fc % 2) + 512]
                        v.scalar_tensor_tensor(tuv, bank(ps, 6 + fc % 2),
                                               AF[0:128, self.ubp + 16 * e + 8 * fch + fc:
                                                  self.ubp + 16 * e + 8 * fch + fc + 1],
                                               AB[0:128, self.gbs + 512 * nt:self.gbs + 512 * nt + 512],
                                               ALU.add, ALU.mult).then_inc(s.c_tu, 1)
                        v.wait_ge(s.c_tu, cnt + 1)
                        v.wait_ge(s.c_sil, cnt + 1)
                        htv = AB[0:128, self.ht + 512 * fc:self.ht + 512 * fc + 512]
                        v.tensor_mul(htv, htv, tuv).then_inc(s.c_h, 1)
                        if fc == 1 and T >= 1:
                            v.wait_ge(s.c_dmf, 16 * T)
                            y_evac(T - 1, 1)
                    v.wait_ge(s.c_dmf, 16 * T + 8)
                    y_evac(T, 0)
        v.wait_ge(s.c_dmf, 1024)
        y_evac(63, 1)

        for dc in range(8):
            v.wait_ge(s.c_ya, 512)
            v.tensor_add(pr.xTb[0:128, 2048 * dc:2048 * dc + 2048],
                         pr.xTb[0:128, 2048 * dc:2048 * dc + 2048],
                         AB[0:128, self.yac + 2048 * dc:self.yac + 2048 * dc + 2048]).then_inc(s.c_rs, 1)

        self.ln.ev(v, gate=(s.c_rs, 8))

    def es(self, sc):
        pr, s = self.pr, self.pr.sem
        ps, AB, AF = pr.ps, pr.AB, pr.AF
        sc.wait_ge(s.ld, pr.ld_val())
        sc.wait_ge(s.cvals, 1)
        for tt in range(16):
            rt = AF[0:128, self.rtl + 8 * tt:self.rtl + 8 * tt + 8]
            sc.wait_ge(s.c_rmx, tt + 1)
            sc.activation(AF[0:128, self.rte + 8 * tt:self.rte + 8 * tt + 8], rt, AFT.Exp,
                          bias=AF[0:128, self.sml + 16 + tt:self.sml + 16 + tt + 1],
                          accum_out=AF[0:128, self.sml + 32 + tt:self.sml + 32 + tt + 1]).then_inc(s.c_rex, 1)
        for e in range(8):
            for nt in range(4):
                sc.wait_ge(s.c_gbm, 4 * e + nt + 1)
                sc.copy(AB[0:128, self.gbs + 512 * nt:self.gbs + 512 * nt + 512],
                        bank(ps, 6 + nt % 2)).then_inc(s.c_gbe, 1)
            for fch in range(2):
                si = 2 * e + fch
                for nt in range(4):
                    T = 4 * si + nt
                    for fc in range(8):
                        cnt = 8 * T + fc
                        sc.wait_ge(s.c_gmm, cnt + 1)
                        if T >= 1:
                            sc.wait_ge(s.c_dmf, 16 * T + fc - 7)
                        sc.activation(AB[0:128, self.ht + 512 * fc:self.ht + 512 * fc + 512],
                                      bank(ps, 4 + fc % 2), AFT.Silu,
                                      bias=AF[0:128, self.gbp + 16 * e + 8 * fch + fc:
                                              self.gbp + 16 * e + 8 * fch + fc + 1]).then_inc(s.c_sil, 1)
        self.ln.es(sc)

    def eg(self, g):
        pr, s = self.pr, self.pr.sem
        AB = pr.AB
        if self.is_first:
            pr.emit_init_loads(g)
        g.dma_start(out=AB[0:128, self.wrt:self.wrt + 64],
                    in_=pr.h_wrtT.rearrange("(c p) e -> p c e", p=128)).then_inc(s.c_pl, 16)
        g.dma_start(out=pr.AF[0:128, self.gbp:self.gbp + 128], in_=pr.h_gb).then_inc(s.c_pl, 16)
        g.dma_start(out=pr.AF[0:128, self.ubp:self.ubp + 128], in_=pr.h_ub).then_inc(s.c_pl, 16)
        g.dma_start(out=AB[0:8, self.db:self.db + 1024], in_=pr.h_db).then_inc(s.c_pl, 16)
        g.dma_start(out=AB[0:8, self.sel8:self.sel8 + 1024],
                    in_=pr.h_sel8.rearrange("e k p -> k e p")).then_inc(s.c_pl, 16)
        for si in range(16):
            e, fch = divmod(si, 2)
            if si >= 2:
                g.wait_ge(s.c_umm, 32 * (si - 1))
            g.dma_start(out=AB[0:128, self.slab(3 * si):self.slab(3 * si) + 8192],
                        in_=pr.h_wg[1024 * e:1024 * e + 1024, 1024 * fch:1024 * fch + 1024]
                        .rearrange("(dc p) c -> p dc c", p=128)).then_inc(s.c_wl, 16)
            if si >= 2:
                g.wait_ge(s.c_dmf, 64 * (si - 1))
            g.dma_start(out=AB[0:128, self.slab(3 * si + 1):self.slab(3 * si + 1) + 8192],
                        in_=pr.h_wu[1024 * e:1024 * e + 1024, 1024 * fch:1024 * fch + 1024]
                        .rearrange("(dc p) c -> p dc c", p=128)).then_inc(s.c_wl, 16)
            if si >= 1:
                g.wait_ge(s.c_gmm, 32 * si)
            g.dma_start(out=AB[0:128, self.slab(3 * si + 2):self.slab(3 * si + 2) + 8192],
                        in_=pr.h_wd[2048 * e + 1024 * fch:2048 * e + 1024 * fch + 1024, :]
                        .rearrange("(fc p) c -> p fc c", p=128)).then_inc(s.c_wdl, 16)

    def esy(self, sy):
        pr, s = self.pr, self.pr.sem
        if DBG:
            sy.wait_ge(s.c_rs, 8)
            sy.dma_start(out=self.d_gtr, in_=pr.AB[0:8, self.gtr:self.gtr + 2048]).then_inc(s.c_dbg, 16)
            sy.dma_start(out=self.d_yac, in_=pr.AB[0:128, self.yac:self.yac + 16384]).then_inc(s.c_dbg, 16)
            sy.dma_start(out=self.d_gtt, in_=pr.AF[0:128, self.gtt:self.gtt + 128]).then_inc(s.c_dbg, 16)
            sy.dma_start(out=self.d_rtl, in_=pr.AF[0:128, self.rtl:self.rtl + 128]).then_inc(s.c_dbg, 16)
            sy.dma_start(out=self.d_rte, in_=pr.AF[0:128, self.rte:self.rte + 128]).then_inc(s.c_dbg, 16)
            sy.wait_ge(s.c_dbg, 80)
        if pr.last:
            for c in range(8):
                sy.wait_ge(s.c_lap, 4 * (c + 1))
                sy.dma_start(out=pr.h_out[128 * c:128 * c + 128, :],
                             in_=pr.AF[0:128, self.osc + 2048 * (c % 2):self.osc + 2048 * (c % 2) + 2048]).then_inc(s.c_ost, 16)
            sy.wait_ge(s.c_ost, 128)
        else:
            sy.wait_ge(s.c_lap, 32)
            sy.dma_start(out=pr.h_out, in_=pr.xTb[:, :]).then_inc(s.c_out, 16)
            sy.wait_ge(s.c_out, 16)



def kernel(**inputs):
    d = _prep(inputs)
    xs = d["xT"]
    cur = None
    out = None
    for gi, group in enumerate(PHASE_GROUPS()):
        pr = Prog(group)
        nc = pr.build()
        in_maps = []
        for c in range(8):
            m = {}
            for name in pr.inputs:
                if name == "xT":
                    m[name] = xs[c]
                elif name == "xh":
                    m[name] = cur[c]
                else:
                    m[name] = d[name]
            in_maps.append(m)
        res = run_bass_kernel_spmd(nc, in_maps, core_ids=list(range(8)))
        if group[-1] == "l1f":
            out = [res.results[c]["out"] for c in range(8)]
        else:
            cur = [res.results[c]["xho"] for c in range(8)]
    return np.stack([o.T for o in out], axis=0)


def PHASE_GROUPS():
    return GROUPS
